# version 03
# baseline (speedup 1.0000x reference)
import numpy as np
import ml_dtypes
from contextlib import ExitStack

import concourse.bass as bass
import concourse.mybir as mybir
from concourse.bass_utils import run_bass_kernel_spmd

F32 = mybir.dt.float32
BF16 = mybir.dt.bfloat16
U8 = mybir.dt.uint8
AF = mybir.ActivationFunctionType
ALU = mybir.AluOpType
AXX = mybir.AxisListType.X
bfloat16 = ml_dtypes.bfloat16

DBG = {}
PARANOIA = 0
DMA_WINDOW = 0
DMA_CHAIN = 1
T = 2048
D = 2048
NCORES = 8
PW = 512
NP = T // PW
NH = 16
DH = 128

ENGS = ("tensor", "vector", "scalar", "gpsimd", "sync")
DMA_Q = {"sync": "sy_d", "gpsimd": "gp_d", "scalar": "sc_d"}


class Buf:
    __slots__ = ("h", "name", "w", "r")

    def __init__(self, h, name):
        self.h = h
        self.name = name
        self.w = None
        self.r = {}


class Prog:
    def __init__(self, nc, stack):
        self.nc = nc
        self.stack = stack
        self.sems = {}
        self.cnt = {}
        for s in ENGS + tuple(DMA_Q.values()):
            if s == "sync":
                continue
            self.sems[s] = stack.enter_context(nc.semaphore("S_" + s))
            self.cnt[s] = 0
        self.waited = {e: {} for e in ENGS}
        self.dma_log = {q: [] for q in DMA_Q.values()}
        self.last_dma = None
        self.banks = [Buf(stack.enter_context(nc.psum_tensor(f"pbank{i}", [128, 512], F32)), f"pbank{i}")
                      for i in range(7)]
        self.bankT = Buf(stack.enter_context(nc.psum_tensor("pbankT", [128, 1024], BF16)), "pbankT")

    def sb(self, name, shape, dtype, st=None):
        h = (st or self.stack).enter_context(self.nc.sbuf_tensor(name, shape, dtype))
        return Buf(h, name)

    def dram(self, name, shape, dtype, kind="Internal"):
        h = self.nc.dram_tensor(name, list(shape), dtype, kind=kind)
        return Buf(h, name)

    def _wait(self, eng, tok):
        if tok is None:
            return
        s, v = tok
        w = self.waited[eng]
        if w.get(s, 0) >= v:
            return
        getattr(self.nc, eng).wait_ge(self.sems[s], v)
        w[s] = v

    def op(self, eng, fn, r=(), w=(), sem=None, inc=1):
        sem = sem or eng
        is_dma = sem in ("sy_d", "gp_d", "sc_d")
        if PARANOIA == 2 or (PARANOIA == 3 and is_dma) or (PARANOIA == 4 and not is_dma):
            for s, v in self.cnt.items():
                if v:
                    self._wait(eng, (s, v))
        elif PARANOIA in (5, 6, 7) and is_dma:
            for s, v in self.cnt.items():
                if not v:
                    continue
                s_dma = s in ("sy_d", "gp_d", "sc_d")
                if ((PARANOIA == 5 and not s_dma) or (PARANOIA == 6 and s_dma)
                        or (PARANOIA == 7 and s_dma and s != sem)):
                    self._wait(eng, (s, v))
        for b in r:
            self._wait(eng, b.w)
        for b in w:
            self._wait(eng, b.w)
            for s, v in b.r.items():
                self._wait(eng, (s, v))
        inst = fn(getattr(self.nc, eng))
        self.cnt[sem] += inc
        inst.then_inc(self.sems[sem], inc)
        tok = (sem, self.cnt[sem])
        for b in r:
            b.r[tok[0]] = max(b.r.get(tok[0], 0), tok[1])
        for b in w:
            b.w = tok
            b.r = {}
        return tok

    def dbg(self, name, src_ap, src_buf, shape):
        if name not in DBG:
            return
        b = self.dram("dbg_" + name, shape, F32, kind="ExternalOutput")
        self.dma("gpsimd", b.h.ap()[:, :], src_ap, r=[src_buf], w=[b])

    def dma(self, eng, out_ap, in_ap, r=(), w=(), **kw):
        q = DMA_Q[eng]
        log = self.dma_log[q]
        if DMA_CHAIN and self.last_dma is not None:
            self._wait(eng, self.last_dma)
        elif DMA_WINDOW and len(log) >= DMA_WINDOW:
            self._wait(eng, (q, log[-DMA_WINDOW]))
        tok = self.op(eng, lambda E: E.dma_start(out=out_ap, in_=in_ap, **kw),
                      r=r, w=w, sem=q, inc=16)
        log.append(tok[1])
        self.last_dma = tok
        return tok

    def fence(self):
        for e in ENGS:
            for s, v in self.cnt.items():
                if v:
                    self._wait(e, (s, v))

    def tt(self, eng, out, a, b, alu, r, w):
        return self.op(eng, lambda E: E.tensor_tensor(out, a, b, op=alu), r=r, w=w)

    def ts(self, eng, out, a, s1, s2, op0, op1, r, w):
        if s2 is None:
            return self.op(eng, lambda E: E.tensor_scalar(out, a, s1, None, op0=op0), r=r, w=w)
        return self.op(eng, lambda E: E.tensor_scalar(out, a, s1, s2, op0=op0, op1=op1), r=r, w=w)

    def stt(self, eng, out, in0, scalar, in1, op0, op1, r, w):
        return self.op(eng, lambda E: E.scalar_tensor_tensor(out, in0, scalar, in1, op0, op1), r=r, w=w)

    def act(self, out, in_, func, r, w, bias=0.0, scale=1.0, accum_out=None):
        return self.op("scalar",
                       lambda E: E.activation(out=out, in_=in_, func=func, bias=bias, scale=scale,
                                              accum_out=accum_out),
                       r=r, w=w)

    def copy(self, eng, out, in_, r, w):
        if eng == "scalar":
            return self.op(eng, lambda E: E.copy(out, in_), r=r, w=w)
        return self.op(eng, lambda E: E.tensor_copy(out, in_), r=r, w=w)

    def memset(self, eng, ap, val, w):
        return self.op(eng, lambda E: E.memset(ap, val), r=(), w=w)

    def mm(self, out_ap, pairs, r, w):
        def fn(E):
            inst = None
            n = len(pairs)
            for i, (lh, rh) in enumerate(pairs):
                inst = E.matmul(out_ap, lh, rh, start=(i == 0), stop=(i == n - 1))
            return inst
        return self.op("tensor", fn, r=r, w=w)

    def mm_acc(self, out_ap, pairs, r, w, start, stop):
        def fn(E):
            inst = None
            n = len(pairs)
            for i, (lh, rh) in enumerate(pairs):
                inst = E.matmul(out_ap, lh, rh, start=(start and i == 0), stop=(stop and i == n - 1))
            return inst
        return self.op("tensor", fn, r=r, w=w)

    def transpose(self, out_ap, in_ap, ident_ap, r, w):
        return self.op("tensor", lambda E: E.transpose(out_ap, in_ap, ident_ap), r=r, w=w)



class LNStats:
    def __init__(self, p, st, tag):
        self.rows = p.sb(f"{tag}_rows", [1, PW * 4], F32, st)
        self.rows_bf = p.sb(f"{tag}_rowsbf", [1, PW * 4], BF16, st)
        self.eps = p.sb(f"{tag}_eps", [1, 1], F32, st)

    def s(self, i, n=PW):
        return self.rows.h[0:1, i * PW:i * PW + n]

    def sbf(self, i, n=PW):
        return self.rows_bf.h[0:1, i * PW:i * PW + n]

    def mu(self):
        return self.s(0)

    def a(self):
        return self.s(1)

    def b(self):
        return self.s(2)

    def sn(self):
        return self.s(3)

    def a_bf(self):
        return self.sbf(0)

    def b_bf(self):
        return self.sbf(1)

    def sn_bf(self):
        return self.sbf(2)


def emit_ln_stats(p, C, ln, xb, xsq, nchunks, eps, bank_s, bank_q, want_sn=False):
    nd = nchunks * 128
    ones = C["ones_col"]
    for c in range(nchunks):
        xb_c = xb.h[:, c * PW:(c + 1) * PW]
        sq_c = xsq.h[:, (c % 2) * PW:((c % 2) + 1) * PW]
        p.act(sq_c, xb_c, AF.Square, r=[xb], w=[xsq])
        p.mm_acc(bank_s.h[0:1, 0:PW], [(ones.h[:, 0:1], xb_c)], r=[ones, xb], w=[bank_s],
                 start=(c == 0), stop=(c == nchunks - 1))
        p.mm_acc(bank_q.h[0:1, 0:PW], [(ones.h[:, 0:1], sq_c)], r=[ones, xsq], w=[bank_q],
                 start=(c == 0), stop=(c == nchunks - 1))
    rows = ln.rows
    p.ts("vector", ln.s(0), bank_s.h[0:1, 0:PW], 1.0 / nd, None, ALU.mult, None, r=[bank_s], w=[rows])
    p.act(ln.s(3), ln.s(0), AF.Square, r=[rows], w=[rows])
    p.stt("vector", ln.s(1), bank_q.h[0:1, 0:PW], 1.0 / nd, ln.s(3), ALU.mult, ALU.subtract,
          r=[bank_q, rows], w=[rows])
    p.memset("vector", ln.eps.h[:], eps, w=[ln.eps])
    p.act(ln.s(3), ln.s(1), AF.Ln, bias=ln.eps.h[0:1, 0:1], r=[rows, ln.eps], w=[rows])
    p.act(ln.s(1), ln.s(3), AF.Exp, scale=-0.5, r=[rows], w=[rows])
    p.stt("vector", ln.s(2), ln.s(0), -1.0, ln.s(1), ALU.mult, ALU.mult, r=[rows], w=[rows])
    p.copy("gpsimd", ln.a_bf(), ln.a(), r=[rows], w=[ln.rows_bf])
    p.copy("gpsimd", ln.b_bf(), ln.b(), r=[rows], w=[ln.rows_bf])
    if want_sn:
        p.stt("vector", ln.s(3), bank_s.h[0:1, 0:PW], 1.0, ln.s(1), ALU.mult, ALU.mult,
              r=[bank_s, rows], w=[rows])
        p.stt("vector", ln.s(3), ln.s(2), float(nd), ln.s(3), ALU.mult, ALU.add, r=[rows], w=[rows])
        p.copy("gpsimd", ln.sn_bf(), ln.sn(), r=[rows], w=[ln.rows_bf])


def emit_bcast(p, C, row_bf_ap, rowsrc_buf, bank_bc, out_buf, n=PW):
    sl = bank_bc.h[:, 0:n]
    p.mm(sl, [(C["ones_row"].h[0:1, 0:128], row_bf_ap)], r=[C["ones_row"], rowsrc_buf], w=[bank_bc])
    p.copy("scalar", out_buf.h[:, 0:n], sl, r=[bank_bc], w=[out_buf])


def emit_xn(p, C, ln, xb, a_bc, b_bc, bank_bc, nchunks):
    emit_bcast(p, C, ln.a_bf(), ln.rows_bf, bank_bc, a_bc)
    emit_bcast(p, C, ln.b_bf(), ln.rows_bf, bank_bc, b_bc)
    for c in range(nchunks):
        sl = xb.h[:, c * PW:(c + 1) * PW]
        tmp = C["tmp_f32"]
        p.tt("gpsimd", tmp.h[:, 0:PW], sl, a_bc.h[:], ALU.mult, r=[xb, a_bc], w=[tmp])
        p.tt("vector", sl, tmp.h[:, 0:PW], b_bc.h[:], ALU.add, r=[tmp, b_bc], w=[xb])


def emit_post_ln_residual(p, C, ln2, y, ysq, x_out_buf, x_out_ap, panel, a2_bc, b2_bc,
                          bank_s, bank_q, bank_bc, t_rot, nchunks=16):
    emit_ln_stats(p, C, ln2, y, ysq, nchunks, 1e-5, bank_s, bank_q)
    emit_bcast(p, C, ln2.a_bf(), ln2.rows_bf, bank_bc, a2_bc)
    emit_bcast(p, C, ln2.b_bf(), ln2.rows_bf, bank_bc, b2_bc)
    for c in range(nchunks):
        y_c = y.h[:, c * PW:(c + 1) * PW]
        trb, off = t_rot[c % 2]
        p.tt("gpsimd", trb.h[:, off:off + PW], y_c, a2_bc.h[:], ALU.mult, r=[y, a2_bc], w=[trb])
        p.tt("vector", trb.h[:, off + PW:off + 2 * PW], trb.h[:, off:off + PW], b2_bc.h[:],
             ALU.add, r=[trb, b2_bc], w=[trb])
        p.dma("gpsimd", x_out_ap[c * 128:(c + 1) * 128, panel * PW:(panel + 1) * PW],
              trb.h[:, off + PW:off + 2 * PW], r=[trb, x_out_buf], w=[x_out_buf], accum_op=ALU.add)



def emit_row_pair(p, ln, row_f32_ap, hi_bf_ap, lo_bf_ap, scratch_f32_ap):
    p.copy("gpsimd", hi_bf_ap, row_f32_ap, r=[ln.rows], w=[ln.rows_bf])
    p.copy("gpsimd", scratch_f32_ap, hi_bf_ap, r=[ln.rows_bf], w=[ln.rows])
    p.tt("vector", scratch_f32_ap, row_f32_ap, scratch_f32_ap, ALU.subtract, r=[ln.rows], w=[ln.rows])
    p.copy("gpsimd", lo_bf_ap, scratch_f32_ap, r=[ln.rows], w=[ln.rows_bf])



def phase_l0_attn(p, C, W, x_in, x_out):
    nc = p.nc
    st = ExitStack()
    xb = p.sb("A_xb", [128, 16 * PW], BF16, st)
    xsq = p.sb("A_xsq", [128, 2 * PW], BF16, st)
    qsb = p.sb("A_q", [128, 16 * PW], BF16, st)
    ksb = p.sb("A_k", [128, 16 * PW], BF16, st)
    vsb = p.sb("A_v", [128, 16 * PW], BF16, st)
    usb = p.sb("A_u", [128, 16 * PW], BF16, st)
    osb = p.sb("A_o", [128, 16 * PW], BF16, st)
    wbuf = [p.sb(f"A_w{i}", [128, 2048], BF16, st) for i in range(2)]
    masks = p.sb("A_masks", [128, 16 * 512], BF16, st)
    qd_bc = p.sb("A_qdbc", [128, 16 * 256], BF16, st)
    kd_cols = p.sb("A_kdcols", [128, 16 * 2], F32, st)
    bd_cols = p.sb("A_bdcols", [128, 16], F32, st)
    kv = p.sb("A_kv", [128, 16 * 128], F32, st)
    kvb = p.sb("A_kvb", [128, 16 * 128], BF16, st)
    scol = p.sb("A_scol", [128, 16], F32, st)
    srow_bf = p.sb("A_srowbf", [1, 16], BF16, st)
    qq = p.sb("A_qq", [128, 256], BF16, st)
    qkm = p.sb("A_qkm", [128, 2 * 256], BF16, st)
    ktd = p.sb("A_ktd", [128, 2 * 128], BF16, st)
    vtd = p.sb("A_vtd", [128, 2 * 128], BF16, st)
    tmp_f32 = p.sb("A_tmpf", [128, PW], F32, st)
    a_bc = p.sb("A_abc", [128, PW], F32, st)
    b_bc = p.sb("A_bbc", [128, PW], F32, st)
    a2_bc = p.sb("A_a2bc", [128, PW], F32, st)
    b2_bc = p.sb("A_b2bc", [128, PW], F32, st)
    t_rot = [(p.sb(f"A_t{i}", [128, 2 * PW], F32, st), 0) for i in range(2)]
    rst_bc = p.sb("A_rstbc", [128, PW], F32, st)
    ln = LNStats(p, st, "A_ln")
    ln2 = ln
    C["tmp_f32"] = tmp_f32
    load_const_tiles(p, C, W, st, ["Dm", "arr_row", "karr_cols"])

    bankM, bankQK0, bankQK1, bankO, bankKV, bankS, bankQ2 = p.banks
    bankTR = p.bankT
    bank_bc = bankQK0

    p.dma("sync", x_out.h.ap()[:, :], x_in.h.ap()[:, :], r=[x_in], w=[x_out])
    p.dma("gpsimd", srow_bf.h[0:1, :], W["slope_row"].h.ap()[0:1, :], r=[W["slope_row"]], w=[srow_bf])
    p.mm(bankM.h[:, 0:16], [(C["ones_row"].h[0:1, 0:128], srow_bf.h[0:1, 0:16])],
         r=[C["ones_row"], srow_bf], w=[bankM])
    p.ts("vector", scol.h[:], bankM.h[:, 0:16], -1.0, None, ALU.mult, None, r=[bankM], w=[scol])
    for h in range(NH):
        sc = scol.h[:, h:h + 1]
        for j in range(2):
            p.ts("vector", tmp_f32.h[:, 0:256], C["Dm"].h[:, j * 256:(j + 1) * 256], sc, None,
                 ALU.mult, None, r=[C["Dm"], scol], w=[tmp_f32])
            p.act(masks.h[:, h * 512 + j * 256: h * 512 + (j + 1) * 256], tmp_f32.h[:, 0:256],
                  AF.Exp, r=[tmp_f32], w=[masks])
        p.ts("vector", ln.s(3, 256), C["arr_row"].h[0:1, 0:256], scol.h[0:1, h:h + 1], None,
             ALU.mult, None, r=[C["arr_row"], scol], w=[ln.rows])
        p.act(ln.sbf(3, 256), ln.s(3, 256), AF.Exp, r=[ln.rows], w=[ln.rows_bf])
        p.mm(bankM.h[:, 0:256], [(C["ones_row"].h[0:1, 0:128], ln.sbf(3, 256))],
             r=[C["ones_row"], ln.rows_bf], w=[bankM])
        p.copy("scalar", qd_bc.h[:, h * 256:(h + 1) * 256], bankM.h[:, 0:256], r=[bankM], w=[qd_bc])
        p.ts("vector", tmp_f32.h[:, 0:2], C["karr_cols"].h[:, 0:2], sc, None, ALU.mult, None,
             r=[C["karr_cols"], scol], w=[tmp_f32])
        p.act(kd_cols.h[:, h * 2:(h + 1) * 2], tmp_f32.h[:, 0:2], AF.Exp, r=[tmp_f32], w=[kd_cols])
        p.ts("vector", tmp_f32.h[:, 2:3], scol.h[:, h:h + 1], 256.0, None, ALU.mult, None,
             r=[scol], w=[tmp_f32])
        p.act(bd_cols.h[:, h:h + 1], tmp_f32.h[:, 2:3], AF.Exp, r=[tmp_f32], w=[bd_cols])
    p.memset("vector", kv.h[:], 0.0, w=[kv])
    p.memset("vector", kvb.h[:], 0.0, w=[kvb])

    wT = W["wqkvuT"]
    woT = W["woutT"]

    for panel in range(NP):
        t0 = panel * PW
        p.dma("gpsimd", xb.h[:],
              x_in.h.ap()[:, t0:t0 + PW].rearrange("(c p) t -> p c t", p=128),
              r=[x_in], w=[xb])
        emit_ln_stats(p, C, ln, xb, xsq, 16, 1e-5, bankS, bankQ2)
        emit_xn(p, C, ln, xb, a_bc, b_bc, bank_bc, 16)

        for fc in range(64):
            wb = wbuf[fc % 2]
            p.dma("gpsimd", wb.h[:],
                  wT.h.ap()[:, fc * 128:(fc + 1) * 128].rearrange("(c p) f -> p c f", p=128),
                  r=[wT], w=[wb])
            pairs = [(wb.h[:, c * 128:(c + 1) * 128], xb.h[:, c * PW:(c + 1) * PW]) for c in range(16)]
            p.mm(bankM.h[:, 0:PW], pairs, r=[wb, xb], w=[bankM])
            kind, idx = divmod(fc, 16)
            dstbuf = (qsb, ksb, vsb, usb)[kind]
            dst = dstbuf.h[:, idx * PW:(idx + 1) * PW]
            if kind <= 1:
                p.act(dst, bankM.h[:, 0:PW], AF.Silu, r=[bankM], w=[dstbuf])
            else:
                p.copy("scalar", dst, bankM.h[:, 0:PW], r=[bankM], w=[dstbuf])

        for blk in range(2):
            boff = blk * 256
            for h in range(NH):
                qs = qsb.h[:, h * PW + boff:h * PW + boff + 256]
                ks = ksb.h[:, h * PW + boff:h * PW + boff + 256]
                vs = vsb.h[:, h * PW + boff:h * PW + boff + 256]
                kv_h = kv.h[:, h * 128:(h + 1) * 128]
                kvb_h = kvb.h[:, h * 128:(h + 1) * 128]
                p.tt("vector", qq.h[:], qs, qd_bc.h[:, h * 256:(h + 1) * 256], ALU.mult,
                     r=[qsb, qd_bc], w=[qq])
                for j in range(2):
                    p.transpose(bankTR.h[:, 0:128],
                                ks[:, j * 128:(j + 1) * 128], C["ident"].h[:],
                                r=[ksb, C["ident"]], w=[bankTR])
                    p.op("scalar", lambda E, j=j, h=h: E.activation(
                        out=ktd.h[:, j * 128:(j + 1) * 128], in_=bankTR.h[:, 0:128],
                        func=AF.Copy, scale=kd_cols.h[:, h * 2 + j:h * 2 + j + 1], bias=0.0),
                        r=[bankTR, kd_cols], w=[ktd])
                    p.transpose(bankTR.h[:, 128:256],
                                vs[:, j * 128:(j + 1) * 128], C["ident"].h[:],
                                r=[vsb, C["ident"]], w=[bankTR])
                    p.copy("scalar", vtd.h[:, j * 128:(j + 1) * 128],
                           bankTR.h[:, 128:256], r=[bankTR], w=[vtd])
                for j, bankj in ((0, bankQK0), (1, bankQK1)):
                    p.mm(bankj.h[:, 0:256], [(ks[:, j * 128:(j + 1) * 128], qs)],
                         r=[ksb, qsb], w=[bankj])
                    p.tt("vector", qkm.h[:, j * 256:(j + 1) * 256], bankj.h[:, 0:256],
                         masks.h[:, h * 512 + j * 256:h * 512 + (j + 1) * 256], ALU.mult,
                         r=[bankj, masks], w=[qkm])
                p.mm(bankO.h[:, 0:256],
                     [(kvb_h, qq.h[:]),
                      (vtd.h[:, 0:128], qkm.h[:, 0:256]),
                      (vtd.h[:, 128:256], qkm.h[:, 256:512])],
                     r=[kvb, qq, vtd, qkm], w=[bankO])
                p.copy("scalar", osb.h[:, h * PW + boff:h * PW + boff + 256], bankO.h[:, 0:256],
                       r=[bankO], w=[osb])
                p.mm(bankKV.h[:, 0:128],
                     [(ktd.h[:, 0:128], vtd.h[:, 0:128]),
                      (ktd.h[:, 128:256], vtd.h[:, 128:256])],
                     r=[ktd, vtd], w=[bankKV])
                p.stt("vector", kv_h, kv_h, bd_cols.h[:, h:h + 1], bankKV.h[:, 0:128],
                      ALU.mult, ALU.add, r=[kv, bd_cols, bankKV], w=[kv])
                p.copy("gpsimd", kvb_h, kv_h, r=[kv], w=[kvb])

        for c in range(16):
            o_c = osb.h[:, c * PW:(c + 1) * PW]
            sq_c = xsq.h[:, (c % 2) * PW:((c % 2) + 1) * PW]
            p.act(sq_c, o_c, AF.Square, r=[osb], w=[xsq])
            p.mm_acc(bankS.h[0:1, 0:PW], [(C["ones_col"].h[:, 0:1], sq_c)],
                     r=[C["ones_col"], xsq], w=[bankS], start=(c == 0), stop=(c == 15))
        p.ts("vector", ln.s(0), bankS.h[0:1, 0:PW], 1.0 / 2048.0, None, ALU.mult, None,
             r=[bankS], w=[ln.rows])
        p.memset("vector", ln.eps.h[:], 1e-6, w=[ln.eps])
        p.act(ln.s(3), ln.s(0), AF.Ln, bias=ln.eps.h[0:1, 0:1], r=[ln.rows, ln.eps], w=[ln.rows])
        p.act(ln.s(0), ln.s(3), AF.Exp, scale=-0.5, r=[ln.rows], w=[ln.rows])
        p.copy("gpsimd", ln.sbf(3), ln.s(0), r=[ln.rows], w=[ln.rows_bf])
        emit_bcast(p, C, ln.sbf(3), ln.rows_bf, bank_bc, rst_bc)
        for c in range(16):
            o_c = osb.h[:, c * PW:(c + 1) * PW]
            u_c = usb.h[:, c * PW:(c + 1) * PW]
            p.tt("gpsimd", tmp_f32.h[:, 0:PW], o_c, rst_bc.h[:], ALU.mult, r=[osb, rst_bc], w=[tmp_f32])
            p.tt("vector", o_c, tmp_f32.h[:, 0:PW], u_c, ALU.mult, r=[tmp_f32, usb], w=[osb])

        for fc in range(16):
            wb = wbuf[fc % 2]
            p.dma("gpsimd", wb.h[:],
                  woT.h.ap()[:, fc * 128:(fc + 1) * 128].rearrange("(c p) f -> p c f", p=128),
                  r=[woT], w=[wb])
            pairs = [(wb.h[:, c * 128:(c + 1) * 128], osb.h[:, c * PW:(c + 1) * PW]) for c in range(16)]
            p.mm(bankM.h[:, 0:PW], pairs, r=[wb, osb], w=[bankM])
            p.copy("scalar", usb.h[:, fc * PW:(fc + 1) * PW], bankM.h[:, 0:PW], r=[bankM], w=[usb])

        emit_post_ln_residual(p, C, ln2, usb, xsq, x_out, x_out.h.ap(), panel,
                              a2_bc, b2_bc, bankS, bankQ2, bank_bc, t_rot)

    p.fence()
    st.close()





def phase_l0_ffn(p, C, W, x_in, x_out):
    st = ExitStack()
    xb = p.sb("B_xb", [128, 16 * PW], BF16, st)
    xsq = p.sb("B_xsq", [128, 2 * PW], BF16, st)
    hsb = p.sb("B_h", [128, 64 * PW], BF16, st)
    ysb = p.sb("B_y", [128, 16 * PW], BF16, st)
    gs = p.sb("B_gs", [128, 2 * PW], BF16, st)
    us = p.sb("B_us", [128, 2 * PW], BF16, st)
    wgbuf = [p.sb(f"B_wg{i}", [128, 2048], BF16, st) for i in range(2)]
    wubuf = [p.sb(f"B_wu{i}", [128, 2048], BF16, st) for i in range(2)]
    wdbuf = [p.sb(f"B_wd{i}", [128, 8192], BF16, st) for i in range(2)]
    tmp_f32 = p.sb("B_tmpf", [128, PW], F32, st)
    a_bc = p.sb("B_abc", [128, PW], F32, st)
    b_bc = p.sb("B_bbc", [128, PW], F32, st)
    t_rot = [(p.sb(f"B_t{i}", [128, 2 * PW], F32, st), 0) for i in range(2)]
    ln = LNStats(p, st, "B_ln")
    C["tmp_f32"] = tmp_f32
    load_const_tiles(p, C, W, st, ["gsc", "usc", "dsc"])

    bankG, bankU, bankD, bankBC, bankS, bankQ2, _sp = p.banks

    gsc, usc, dsc = C["gsc"], C["usc"], C["dsc"]
    n128 = C["neg128_row"]

    p.dma("sync", x_out.h.ap()[:, :], x_in.h.ap()[:, :], r=[x_in], w=[x_out])

    for panel in range(NP):
        t0 = panel * PW
        p.dma("gpsimd", xb.h[:],
              x_in.h.ap()[:, t0:t0 + PW].rearrange("(c p) t -> p c t", p=128),
              r=[x_in], w=[xb])
        emit_ln_stats(p, C, ln, xb, xsq, 16, 1e-5, bankS, bankQ2)
        emit_xn(p, C, ln, xb, a_bc, b_bc, bankBC, 16)
        for c in range(16):
            p.mm_acc(bankS.h[0:1, 0:PW], [(C["ones_col"].h[:, 0:1], xb.h[:, c * PW:(c + 1) * PW])],
                     r=[C["ones_col"], xb], w=[bankS], start=(c == 0), stop=(c == 15))
        p.copy("vector", ln.s(3), bankS.h[0:1, 0:PW], r=[bankS], w=[ln.rows])
        emit_row_pair(p, ln, ln.s(3), ln.sbf(2), ln.sbf(3), ln.s(0))

        for fc in range(64):
            wg = wgbuf[fc % 2]
            wu = wubuf[fc % 2]
            p.dma("gpsimd", wg.h[:],
                  W["gateT8"].h.ap()[:, fc * 128:(fc + 1) * 128].rearrange("(c p) f -> p c f", p=128),
                  r=[W["gateT8"]], w=[wg])
            p.dma("gpsimd", wu.h[:],
                  W["upT8"].h.ap()[:, fc * 128:(fc + 1) * 128].rearrange("(c p) f -> p c f", p=128),
                  r=[W["upT8"]], w=[wu])
            pairs_g = [(wg.h[:, c * 128:(c + 1) * 128], xb.h[:, c * PW:(c + 1) * PW]) for c in range(16)]
            pairs_g.append((n128.h[0:1, 0:128], ln.sbf(2)))
            pairs_g.append((n128.h[0:1, 0:128], ln.sbf(3)))
            p.mm(bankG.h[:, 0:PW], pairs_g, r=[wg, xb, n128, ln.rows_bf], w=[bankG])
            pairs_u = [(wu.h[:, c * 128:(c + 1) * 128], xb.h[:, c * PW:(c + 1) * PW]) for c in range(16)]
            pairs_u.append((n128.h[0:1, 0:128], ln.sbf(2)))
            pairs_u.append((n128.h[0:1, 0:128], ln.sbf(3)))
            p.mm(bankU.h[:, 0:PW], pairs_u, r=[wu, xb, n128, ln.rows_bf], w=[bankU])
            g_r = gs.h[:, (fc % 2) * PW:((fc % 2) + 1) * PW]
            u_r = us.h[:, (fc % 2) * PW:((fc % 2) + 1) * PW]
            p.act(g_r, bankG.h[:, 0:PW], AF.Silu, scale=gsc.h[:, fc:fc + 1], r=[bankG, gsc], w=[gs])
            p.act(u_r, bankU.h[:, 0:PW], AF.Copy, scale=usc.h[:, fc:fc + 1], bias=0.0,
                  r=[bankU, usc], w=[us])
            p.tt("gpsimd", hsb.h[:, fc * PW:(fc + 1) * PW], g_r, u_r, ALU.mult, r=[gs, us], w=[hsb])

        for c in range(64):
            p.mm_acc(bankS.h[0:1, 0:PW], [(C["ones_col"].h[:, 0:1], hsb.h[:, c * PW:(c + 1) * PW])],
                     r=[C["ones_col"], hsb], w=[bankS], start=(c == 0), stop=(c == 63))
        p.copy("vector", ln.s(3), bankS.h[0:1, 0:PW], r=[bankS], w=[ln.rows])
        emit_row_pair(p, ln, ln.s(3), ln.sbf(2), ln.sbf(3), ln.s(0))

        for oc in range(16):
            wd = wdbuf[oc % 2]
            p.dma("gpsimd", wd.h[:],
                  W["downT8"].h.ap()[:, oc * 128:(oc + 1) * 128].rearrange("(c p) f -> p c f", p=128),
                  r=[W["downT8"]], w=[wd])
            pairs = [(wd.h[:, c * 128:(c + 1) * 128], hsb.h[:, c * PW:(c + 1) * PW]) for c in range(64)]
            pairs.append((n128.h[0:1, 0:128], ln.sbf(2)))
            pairs.append((n128.h[0:1, 0:128], ln.sbf(3)))
            p.mm(bankD.h[:, 0:PW], pairs, r=[wd, hsb, n128, ln.rows_bf], w=[bankD])
            p.act(ysb.h[:, oc * PW:(oc + 1) * PW], bankD.h[:, 0:PW], AF.Copy,
                  scale=dsc.h[:, oc:oc + 1], bias=0.0, r=[bankD, dsc], w=[ysb])

        emit_post_ln_residual(p, C, ln, ysb, xsq, x_out, x_out.h.ap(), panel,
                              a_bc, b_bc, bankS, bankQ2, bankBC, t_rot)

    p.fence()
    st.close()




def phase_l1_attn_proj(p, C, W, x_in, S):
    st = ExitStack()
    xb = p.sb("C_xb", [128, 16 * PW], BF16, st)
    xsq = p.sb("C_xsq", [128, 2 * PW], BF16, st)
    es32 = p.sb("C_es32", [128, 8 * 2048], F32, st)
    cz = p.sb("C_cz", [128, 2048], F32, st)
    wbuf = [p.sb(f"C_w{i}", [128, 2048], BF16, st) for i in range(2)]
    wpan = [p.sb(f"C_wp{i}", [128, 16 * 512], BF16, st) for i in range(2)]
    ropea = p.sb("C_ropea", [64, 4 * PW], F32, st)
    qf2 = p.sb("C_qf2", [64, 4 * PW], BF16, st)
    otile = p.sb("C_otile", [128, 2 * PW], BF16, st)
    ws1sb = p.sb("C_ws1", [128, 16 * 16], BF16, st)
    ws2sb = p.sb("C_ws2", [16, 1024], BF16, st)
    s1bf = p.sb("C_s1", [16, PW], BF16, st)
    esbf = p.sb("C_esbf", [128, 2048], BF16, st)
    estT = p.sb("C_estT", [128, 2048], BF16, st)
    tmp_f32 = p.sb("C_tmpf", [128, PW], F32, st)
    a_bc = p.sb("C_abc", [128, PW], F32, st)
    b_bc = p.sb("C_bbc", [128, PW], F32, st)
    ln = LNStats(p, st, "C_ln")
    C["tmp_f32"] = tmp_f32
    load_const_tiles(p, C, W, st, ["cosq", "sinq", "cosk", "sink", "bs2c"])

    bankM, bankV, bankS2, bankBC, bankS, bankQ2, _sp = p.banks
    bankT = p.bankT

    p.dma("gpsimd", ws1sb.h[:], W["ws1T"].h.ap()[:, :].rearrange("(c p) f -> p c f", p=128),
          r=[W["ws1T"]], w=[ws1sb])
    p.dma("gpsimd", ws2sb.h[:], W["ws2T"].h.ap()[:, :], r=[W["ws2T"]], w=[ws2sb])

    for panel in range(NP):
        t0 = panel * PW
        p.dma("gpsimd", xb.h[:],
              x_in.h.ap()[:, t0:t0 + PW].rearrange("(c p) t -> p c t", p=128),
              r=[x_in], w=[xb])
        emit_ln_stats(p, C, ln, xb, xsq, 16, 1e-5, bankS, bankQ2)
        emit_xn(p, C, ln, xb, a_bc, b_bc, bankBC, 16)

        for which, wT, dram, cosn, sinn in (
                (0, W["wqT"], S["qfD"], "cosq", "sinq"),
                (1, W["wkT"], S["kfD"], "cosk", "sink")):
            for hc in range(NH):
                wb = wbuf[hc % 2]
                p.dma("gpsimd", wb.h[:],
                      wT.h.ap()[:, hc * 128:(hc + 1) * 128].rearrange("(c p) f -> p c f", p=128),
                      r=[wT], w=[wb])
                pairs = [(wb.h[:, c * 128:(c + 1) * 128], xb.h[:, c * PW:(c + 1) * PW]) for c in range(16)]
                p.mm(bankM.h[:, 0:PW], pairs, r=[wb, xb], w=[bankM])
                cosT = C[cosn]
                sinT = C[sinn]
                cs = cosT.h[0:64, t0:t0 + PW]
                sn = sinT.h[0:64, t0:t0 + PW]
                ps0 = bankM.h[0:64, 0:PW]
                ps1 = bankM.h[64:128, 0:PW]
                p.tt("vector", ropea.h[0:64, 0:PW], ps0, cs, ALU.mult, r=[bankM, cosT], w=[ropea])
                p.tt("vector", ropea.h[0:64, PW:2 * PW], ps1, sn, ALU.mult, r=[bankM, sinT], w=[ropea])
                p.tt("vector", ropea.h[0:64, 2 * PW:3 * PW], ps1, cs, ALU.mult, r=[bankM, cosT], w=[ropea])
                p.tt("vector", ropea.h[0:64, 3 * PW:4 * PW], ps0, sn, ALU.mult, r=[bankM, sinT], w=[ropea])
                qo = qf2.h[0:64, (hc % 2) * 2 * PW:((hc % 2) + 1) * 2 * PW]
                p.tt("gpsimd", qo[0:64, 0:PW], ropea.h[0:64, 0:PW], ropea.h[0:64, PW:2 * PW],
                     ALU.subtract, r=[ropea], w=[qf2])
                p.tt("gpsimd", qo[0:64, PW:2 * PW], ropea.h[0:64, 2 * PW:3 * PW],
                     ropea.h[0:64, 3 * PW:4 * PW], ALU.add, r=[ropea], w=[qf2])
                p.dma("sync",
                      dram.h.ap()[hc * 128:(hc + 1) * 128, t0:t0 + PW].rearrange(
                          "(half p) t -> p half t", p=64),
                      qo.rearrange("p (half t) -> p half t", t=PW),
                      r=[qf2, dram], w=[dram])

        for wT, dram in ((W["wvT"], S["vtD"]), (W["wgT"], S["gtD"])):
            for eg in range(4):
                wp = wpan[eg % 2]
                p.dma("gpsimd", wp.h[:],
                      wT.h.ap()[:, eg * 512:(eg + 1) * 512].rearrange("(c p) e -> p c e", p=128),
                      r=[wT], w=[wp])
                for tt_ in range(4):
                    pairs = [(xb.h[:, c * PW + tt_ * 128:c * PW + (tt_ + 1) * 128],
                              wp.h[:, c * 512:(c + 1) * 512]) for c in range(16)]
                    p.mm(bankV.h[:, 0:512], pairs, r=[xb, wp], w=[bankV])
                    ot = otile.h[:, (tt_ % 2) * PW:((tt_ % 2) + 1) * PW]
                    p.copy("scalar", ot, bankV.h[:, 0:512], r=[bankV], w=[otile])
                    p.dma("sync", dram.h.ap()[t0 + tt_ * 128:t0 + (tt_ + 1) * 128, eg * 512:(eg + 1) * 512],
                          ot, r=[otile, dram], w=[dram])

        pairs = [(ws1sb.h[:, c * 16:(c + 1) * 16], xb.h[:, c * PW:(c + 1) * PW]) for c in range(16)]
        p.mm(bankS2.h[0:16, 0:PW], pairs, r=[ws1sb, xb], w=[bankS2])
        p.copy("vector", s1bf.h[:], bankS2.h[0:16, 0:PW], r=[bankS2], w=[s1bf])
        for ch in range(8):
            p.mm(bankS2.h[:, 0:PW], [(ws2sb.h[:, ch * 128:(ch + 1) * 128], s1bf.h[:])],
                 r=[ws2sb, s1bf], w=[bankS2])
            p.ts("vector", tmp_f32.h[:, 0:PW], bankS2.h[:, 0:PW], C["bs2c"].h[:, ch:ch + 1], 32.0,
                 ALU.add, ALU.min, r=[bankS2, C["bs2c"]], w=[tmp_f32])
            p.ts("vector", tmp_f32.h[:, 0:PW], tmp_f32.h[:, 0:PW], -32.0, None, ALU.max, None,
                 r=[tmp_f32], w=[tmp_f32])
            p.act(es32.h[:, ch * 2048 + t0:ch * 2048 + t0 + PW], tmp_f32.h[:, 0:PW], AF.Exp,
                  r=[tmp_f32], w=[es32])

    for ch in range(8):
        sl = es32.h[:, ch * 2048:(ch + 1) * 2048]
        p.op("vector", lambda E, sl=sl: E.tensor_tensor_scan(cz.h[:], sl, sl, 0.0,
                                                             op0=ALU.add, op1=ALU.max),
             r=[es32], w=[cz])
        p.copy("gpsimd", esbf.h[:], sl, r=[es32], w=[esbf])
        p.dma("sync", S["esfD"].h.ap()[ch * 128:(ch + 1) * 128, :], esbf.h[:],
              r=[esbf, S["esfD"]], w=[S["esfD"]])
        p.act(sl, cz.h[:], AF.Ln, r=[cz], w=[es32])
        p.act(cz.h[:], sl, AF.Exp, scale=-1.0, r=[es32], w=[cz])
        for tt_ in range(16):
            p.transpose(bankT.h[:, 0:128], esbf.h[:, tt_ * 128:(tt_ + 1) * 128], C["ident"].h[:],
                        r=[esbf, C["ident"]], w=[bankT])
            p.copy("scalar", estT.h[:, tt_ * 128:(tt_ + 1) * 128], bankT.h[:, 0:128],
                   r=[bankT], w=[estT])
        p.dma("sync", S["estD"].h.ap()[:, ch * 128:(ch + 1) * 128].rearrange("(tt p) m -> p tt m", p=128),
              estT.h[:], r=[estT, S["estD"]], w=[S["estD"]])
        p.copy("gpsimd", esbf.h[:], cz.h[:], r=[cz], w=[esbf])
        p.dma("sync", S["enfD"].h.ap()[ch * 128:(ch + 1) * 128, :], esbf.h[:],
              r=[esbf, S["enfD"]], w=[S["enfD"]])
        for tt_ in range(16):
            p.transpose(bankT.h[:, 0:128], esbf.h[:, tt_ * 128:(tt_ + 1) * 128], C["ident"].h[:],
                        r=[esbf, C["ident"]], w=[bankT])
            p.copy("scalar", estT.h[:, tt_ * 128:(tt_ + 1) * 128], bankT.h[:, 0:128],
                   r=[bankT], w=[estT])
        p.dma("sync", S["entD"].h.ap()[:, ch * 128:(ch + 1) * 128].rearrange("(tt p) m -> p tt m", p=128),
              estT.h[:], r=[estT, S["entD"]], w=[S["entD"]])

    p.fence()
    st.close()



def phase_l1_attn_core(p, C, W, x_in, x_out, S):
    st = ExitStack()
    qp = p.sb("C2_qp", [128, 16 * 512], BF16, st)
    kblk = p.sb("C2_kblk", [128, 16 * 512], BF16, st)
    estm = p.sb("C2_estm", [128, 4 * 1024], BF16, st)
    esfm = p.sb("C2_esfm", [128, 8 * 512], BF16, st)
    vt = p.sb("C2_vt", [128, 4 * 2048], BF16, st)
    enz = p.sb("C2_enz", [128, 4 * 1024], BF16, st)
    ok_sb = p.sb("C2_ok", [128, 4 * 1024], F32, st)
    tmpa = p.sb("C2_tmpa", [128, 3 * 1024], F32, st)
    qvzT = p.sb("C2_qvzT", [128, 8 * 512], BF16, st)
    o_sb = p.sb("C2_o", [128, 4 * 2048], F32, st)
    gsl = A_sb = None
    cols = p.sb("C2_cols", [128, 4 * 16], F32, st)
    gnw_bc = p.sb("C2_gnw", [128, 2048], BF16, st)
    A_sb = p.sb("C2_A", [128, 4 * 512], BF16, st)
    wbuf = [p.sb(f"C2_w{i}", [128, 2048], BF16, st) for i in range(2)]
    a_bc = p.sb("C2_abc", [128, PW], F32, st)
    b_bc = p.sb("C2_bbc", [128, PW], F32, st)
    xsq = p.sb("C2_xsq", [128, 2 * PW], BF16, st)
    ln = LNStats(p, st, "C2_ln")
    ysb = kblk
    t_rot = [(o_sb, 0), (o_sb, 2 * PW)]
    load_const_tiles(p, C, W, st, ["cmask", "gnw16"])

    bankA, bankOK, bankW2, bankO, bankS, bankQ2, bankBC = p.banks
    bankT = p.bankT

    p.dma("sync", x_out.h.ap()[:, :], x_in.h.ap()[:, :], r=[x_in], w=[x_out])
    p.copy("gpsimd", ln.sbf(0, 2048).rearrange("o (a b) -> o a b", b=PW), C["gnw16"].h[0:1, :].rearrange("o (a b) -> o a b", b=PW),
           r=[C["gnw16"]], w=[ln.rows_bf])
    for q4 in range(4):
        p.mm(bankBC.h[:, 0:PW], [(C["ones_row"].h[0:1, 0:128], ln.rows_bf.h[0:1, q4 * PW:(q4 + 1) * PW])],
             r=[C["ones_row"], ln.rows_bf], w=[bankBC])
        p.copy("scalar", gnw_bc.h[:, q4 * PW:(q4 + 1) * PW], bankBC.h[:, 0:PW], r=[bankBC], w=[gnw_bc])

    for tqp in range(NP):
        t0 = tqp * PW
        p.dma("gpsimd", qp.h[:],
              S["qfD"].h.ap()[:, t0:t0 + PW].rearrange("(h p) t -> p h t", p=128),
              r=[S["qfD"]], w=[qp])
        p.dma("gpsimd", enz.h[:],
              S["entD"].h.ap()[t0:t0 + PW, :].rearrange("(s p) m -> p s m", p=128),
              r=[S["entD"]], w=[enz])
        nblk = tqp + 1
        for kb in range(nblk):
            k0 = kb * PW
            p.dma("gpsimd", kblk.h[:],
                  S["kfD"].h.ap()[:, k0:k0 + PW].rearrange("(h p) t -> p h t", p=128),
                  r=[S["kfD"]], w=[kblk])
            p.dma("gpsimd", estm.h[:],
                  S["estD"].h.ap()[k0:k0 + PW, :].rearrange("(j p) m -> p j m", p=128),
                  r=[S["estD"]], w=[estm])
            for h in range(NH):
                kh = kblk.h[:, h * 512:(h + 1) * 512]
                qh = qp.h[:, h * 512:(h + 1) * 512]
                for tkt in range(4):
                    ps = (bankA, bankW2)[tkt % 2]
                    p.mm(ps.h[:, 0:512], [(kh[:, tkt * 128:(tkt + 1) * 128], qh)],
                         r=[kblk, qp], w=[ps])
                    asl = A_sb.h[:, tkt * 512:(tkt + 1) * 512]
                    if kb == tqp:
                        p.tt("vector", asl, ps.h[:, 0:512],
                             C["cmask"].h[:, tkt * 512:(tkt + 1) * 512], ALU.mult,
                             r=[ps, C["cmask"]], w=[A_sb])
                    else:
                        p.copy("scalar", asl, ps.h[:, 0:512], r=[ps], w=[A_sb])
                for sub in range(4):
                    prs = [(A_sb.h[:, tkt * 512 + sub * 128:tkt * 512 + (sub + 1) * 128],
                            estm.h[:, tkt * 1024 + h * 64:tkt * 1024 + (h + 1) * 64])
                           for tkt in range(4)]
                    p.mm_acc(bankOK.h[:, sub * 64:(sub + 1) * 64], prs, r=[A_sb, estm], w=[bankOK],
                             start=(True), stop=(True))
                okv = ok_sb.h[:, :].rearrange("p (s q) -> p s q", q=1024)[:, :, h * 64:(h + 1) * 64]
                psv = bankOK.h[:, 0:256].rearrange("p (s m) -> p s m", m=64)
                if kb == 0:
                    p.copy("vector", okv, psv, r=[bankOK], w=[ok_sb])
                else:
                    p.tt("vector", okv, okv, psv, ALU.add, r=[bankOK, ok_sb], w=[ok_sb])
        for sub in range(4):
            oks = ok_sb.h[:, sub * 1024:(sub + 1) * 1024]
            u0 = tmpa.h[:, 0:1024]
            p.tt("vector", u0, oks, enz.h[:, sub * 1024:(sub + 1) * 1024], ALU.mult,
                 r=[ok_sb, enz], w=[tmpa])
            u03 = u0.rearrange("p (h m) -> p h m", m=64)
            nmax = cols.h[:, 0:16]
            p.op("vector", lambda E, a=nmax, b=u03: E.reduce_max(a, b, axis=AXX, negate=True),
                 r=[tmpa], w=[cols])
            t1 = tmpa.h[:, 1024:2048]
            nb3 = cols.h[:, 0:16].rearrange("p (h o) -> p h o", o=1).broadcast_to([128, 16, 64])
            p.tt("vector", t1.rearrange("p (h m) -> p h m", m=64), u03, nb3, ALU.add,
                 r=[tmpa, cols], w=[tmpa])
            t2 = tmpa.h[:, 2048:3072]
            p.act(t2, t1, AF.Exp, r=[tmpa], w=[tmpa])
            ssum = cols.h[:, 16:32]
            p.op("vector", lambda E, a=ssum, b=t2.rearrange("p (h m) -> p h m", m=64):
                 E.reduce_sum(a, b, axis=AXX), r=[tmpa], w=[cols])
            rcp = cols.h[:, 32:48]
            p.op("vector", lambda E, a=rcp, b=ssum: E.reciprocal(a, b), r=[cols], w=[cols])
            rb3 = rcp.rearrange("p (h o) -> p h o", o=1).broadcast_to([128, 16, 64])
            t3 = tmpa.h[:, 0:1024]
            p.tt("vector", t3.rearrange("p (h m) -> p h m", m=64),
                 t2.rearrange("p (h m) -> p h m", m=64), rb3, ALU.mult, r=[tmpa, cols], w=[tmpa])
            qvz = A_sb.h[:, 1024:2048]
            p.tt("vector", qvz, t3, enz.h[:, sub * 1024:(sub + 1) * 1024], ALU.mult,
                 r=[tmpa, enz], w=[A_sb])
            for pr in range(8):
                p.transpose(bankT.h[:, 0:128], qvz[:, pr * 128:(pr + 1) * 128], C["ident"].h[:],
                            r=[A_sb, C["ident"]], w=[bankT])
                p.copy("scalar", qvzT.h[:, pr * 512 + sub * 128:pr * 512 + (sub + 1) * 128],
                       bankT.h[:, 0:128], r=[bankT], w=[qvzT])
        for kb in range(nblk):
            k0 = kb * PW
            p.dma("gpsimd", esfm.h[:],
                  S["esfD"].h.ap()[:, k0:k0 + PW].rearrange("(ch p) t -> p ch t", p=128),
                  r=[S["esfD"]], w=[esfm])
            p.dma("gpsimd", vt.h[:],
                  S["vtD"].h.ap()[k0:k0 + PW, :].rearrange("(j p) e -> p j e", p=128),
                  r=[S["vtD"]], w=[vt])
            for h in range(NH):
                pb = 64 * (h % 2)
                for tkt in range(4):
                    ps = (bankA, bankW2)[tkt % 2]
                    lh = esfm.h[pb:pb + 64, (h // 2) * 512 + tkt * 128:(h // 2) * 512 + (tkt + 1) * 128]
                    rh = qvzT.h[pb:pb + 64, (h // 2) * 512:(h // 2 + 1) * 512]
                    p.mm(ps.h[:, 0:512], [(lh, rh)], r=[esfm, qvzT], w=[ps])
                    asl = A_sb.h[:, tkt * 512:(tkt + 1) * 512]
                    if kb == tqp:
                        p.tt("vector", asl, ps.h[:, 0:512],
                             C["cmask"].h[:, tkt * 512:(tkt + 1) * 512], ALU.mult,
                             r=[ps, C["cmask"]], w=[A_sb])
                    else:
                        p.copy("scalar", asl, ps.h[:, 0:512], r=[ps], w=[A_sb])
                for sub in range(4):
                    prs = [(A_sb.h[:, tkt * 512 + sub * 128:tkt * 512 + (sub + 1) * 128],
                            vt.h[:, tkt * 2048 + h * 128:tkt * 2048 + (h + 1) * 128])
                           for tkt in range(4)]
                    p.mm_acc(bankO.h[:, sub * 128:(sub + 1) * 128], prs, r=[A_sb, vt], w=[bankO],
                             start=True, stop=True)
                ov = o_sb.h[:, :].rearrange("p (s q) -> p s q", q=2048)[:, :, h * 128:(h + 1) * 128]
                psv = bankO.h[:, 0:512].rearrange("p (s e) -> p s e", e=128)
                if kb == 0:
                    p.copy("vector", ov, psv, r=[bankO], w=[o_sb])
                else:
                    p.tt("vector", ov, ov, psv, ALU.add, r=[bankO, o_sb], w=[o_sb])
        for sub in range(4):
            osl = o_sb.h[:, sub * 2048:(sub + 1) * 2048]
            sq = tmpa.h[:, 0:2048]
            p.act(sq, osl, AF.Square, r=[o_sb], w=[tmpa])
            ss = cols.h[:, 0:16]
            p.op("vector", lambda E, a=ss, b=sq.rearrange("p (h e) -> p h e", e=128):
                 E.reduce_sum(a, b, axis=AXX), r=[tmpa], w=[cols])
            p.ts("vector", ss, ss, 1.0 / 128.0, None, ALU.mult, None, r=[cols], w=[cols])
            p.memset("vector", cols.h[:, 48:49], 1e-5, w=[cols])
            ss2 = cols.h[:, 16:32]
            p.act(ss2, ss, AF.Ln, bias=cols.h[:, 48:49], r=[cols], w=[cols])
            p.act(ss, ss2, AF.Exp, scale=-0.5, r=[cols], w=[cols])
            rb3 = ss.rearrange("p (h o) -> p h o", o=1).broadcast_to([128, 16, 128])
            t1 = tmpa.h[:, 1024:3072]
            p.tt("vector", t1.rearrange("p (h e) -> p h e", e=128),
                 osl.rearrange("p (h e) -> p h e", e=128), rb3, ALU.mult, r=[o_sb, cols], w=[tmpa])
            p.tt("gpsimd", t1, t1, gnw_bc.h[:], ALU.mult, r=[tmpa, gnw_bc], w=[tmpa])
            p.dma("gpsimd", enz.h[:, 0:2048],
                  S["gtD"].h.ap()[t0 + sub * 128:t0 + (sub + 1) * 128, :], r=[S["gtD"]], w=[enz])
            gsl = A_sb.h[:, 0:2048]
            p.act(gsl, enz.h[:, 0:2048], AF.Silu, r=[enz], w=[A_sb])
            og = vt.h[:, 0:2048]
            p.tt("vector", og, t1, gsl, ALU.mult, r=[tmpa, A_sb], w=[vt])
            for h in range(NH):
                p.transpose(bankT.h[:, 0:128], og[:, h * 128:(h + 1) * 128], C["ident"].h[:],
                            r=[vt, C["ident"]], w=[bankT])
                p.copy("scalar", qp.h[:, h * 512 + sub * 128:h * 512 + (sub + 1) * 128],
                       bankT.h[:, 0:128], r=[bankT], w=[qp])
        for oc in range(16):
            wb = wbuf[oc % 2]
            p.dma("gpsimd", wb.h[:],
                  W["woT"].h.ap()[:, oc * 128:(oc + 1) * 128].rearrange("(c p) f -> p c f", p=128),
                  r=[W["woT"]], w=[wb])
            pairs = [(wb.h[:, c * 128:(c + 1) * 128], qp.h[:, c * 512:(c + 1) * 512]) for c in range(16)]
            p.mm(bankA.h[:, 0:512], pairs, r=[wb, qp], w=[bankA])
            p.copy("scalar", ysb.h[:, oc * PW:(oc + 1) * PW], bankA.h[:, 0:512], r=[bankA], w=[ysb])
        emit_post_ln_residual(p, C, ln, ysb, xsq, x_out, x_out.h.ap(), tqp,
                              a_bc, b_bc, bankS, bankQ2, bankBC, t_rot)

    p.fence()
    st.close()




HID = [1024, 2048, 3072, 4096, 1024, 2048, 3072, 4096]


def phase_l1_ffn(p, C, W, x_in, x_out):
    st = ExitStack()
    xb = p.sb("D_xb", [128, 16 * PW], BF16, st)
    xsq = p.sb("D_xsq", [128, 2 * PW], BF16, st)
    gxn = p.sb("D_gxn", [128, 16 * PW], BF16, st)
    gh = p.sb("D_gh", [128, 8 * PW], BF16, st)
    gmid = p.sb("D_gmid", [128, 32 * PW], BF16, st)
    acc = p.sb("D_acc", [128, 8 * PW], F32, st)
    accbf = gmid
    ysb = xb
    lg = p.sb("D_lg", [8, PW], F32, st)
    lgbf = p.sb("D_lgbf", [8, PW], BF16, st)
    lt = p.sb("D_lt", [128, 8], F32, st)
    cols = p.sb("D_cols", [128, 16], F32, st)
    pr8 = p.sb("D_pr8", [128, 3 * 8], F32, st)
    wexpT = p.sb("D_wexpT", [8, PW], BF16, st)
    w_bc = p.sb("D_wbc", [128, PW], BF16, st)
    crows = p.sb("D_crows", [1, 8 * 1024], BF16, st)
    b2row = p.sb("D_b2row", [1, 1024], BF16, st)
    ctot = p.sb("D_ctot", [1, 1024], F32, st)
    ctbf = p.sb("D_ctbf", [1, 1024], BF16, st)
    ctcols = p.sb("D_ctcols", [128, 8], F32, st)
    gb1 = p.sb("D_gb1", [128, 32], BF16, st)
    onespw = p.sb("D_onespw", [1, PW], BF16, st)
    w1buf = [p.sb(f"D_w1{i}", [128, 8 * 128], BF16, st) for i in range(2)]
    w2buf = [p.sb(f"D_w2{i}", [128, 32 * 128], BF16, st) for i in range(2)]
    wdbuf = [p.sb(f"D_wd{i}", [128, 16 * 128], BF16, st) for i in range(2)]
    rwsb = p.sb("D_rw", [128, 16 * 8], BF16, st)
    tmpf = p.sb("D_tmpf", [128, PW], F32, st)
    a_bc = p.sb("D_abc", [128, PW], F32, st)
    b_bc = p.sb("D_bbc", [128, PW], F32, st)
    t_rot = [(p.sb(f"D_t{i}", [128, 2 * PW], F32, st), 0) for i in range(2)]
    ln = LNStats(p, st, "D_ln")
    C["tmp_f32"] = tmpf
    load_const_tiles(p, C, W, st, ["dn_bc", "up_bc", "rt_bc", "lcost", "sh_b1c", "sh_b2c",
                                   "co_bc", "sel8"] + [f"e{e}_b1c" for e in range(8)])

    bankH, bankMID, bankF, bankR, bankS, bankQ2, bankBC = p.banks
    bankT = p.bankT

    p.dma("sync", x_out.h.ap()[:, :], x_in.h.ap()[:, :], r=[x_in], w=[x_out])
    p.memset("vector", onespw.h[:], 1.0, w=[onespw])
    p.memset("vector", ctot.h[:], 0.0, w=[ctot])
    p.dma("gpsimd", rwsb.h[:], W["rt_w"].h.ap()[:, :].rearrange("(c p) e -> p c e", p=128),
          r=[W["rt_w"]], w=[rwsb])

    for panel in range(NP):
        t0 = panel * PW
        p.dma("gpsimd", xb.h[:],
              x_in.h.ap()[:, t0:t0 + PW].rearrange("(c p) t -> p c t", p=128),
              r=[x_in], w=[xb])
        emit_ln_stats(p, C, ln, xb, xsq, 16, 1e-5, bankS, bankQ2)
        emit_xn(p, C, ln, xb, a_bc, b_bc, bankBC, 16)
        for c in range(16):
            p.act(gxn.h[:, c * PW:(c + 1) * PW], xb.h[:, c * PW:(c + 1) * PW], AF.Gelu,
                  r=[xb], w=[gxn])

        for lc in range(8):
            wd = wdbuf[lc % 2]
            p.dma("gpsimd", wd.h[:, 0:2048],
                  W["dn_w"].h.ap()[:, lc * 128:(lc + 1) * 128].rearrange("(c p) f -> p c f", p=128),
                  r=[W["dn_w"]], w=[wd])
            pairs = [(wd.h[:, c * 128:(c + 1) * 128], xb.h[:, c * PW:(c + 1) * PW]) for c in range(16)]
            p.mm(bankH.h[:, 0:PW], pairs, r=[wd, xb], w=[bankH])
            p.act(gh.h[:, lc * PW:(lc + 1) * PW], bankH.h[:, 0:PW], AF.Gelu,
                  bias=C["dn_bc"].h[:, lc:lc + 1], r=[bankH, C["dn_bc"]], w=[gh])

        pairs = [(rwsb.h[:, c * 8:(c + 1) * 8], xb.h[:, c * PW:(c + 1) * PW]) for c in range(16)]
        p.mm(bankH.h[0:8, 0:PW], pairs, r=[rwsb, xb], w=[bankH])
        p.ts("vector", lg.h[:], bankH.h[0:8, 0:PW], C["rt_bc"].h[0:8, 0:1], C["lcost"].h[0:8, 0:1],
             ALU.add, ALU.subtract, r=[bankH, C["rt_bc"], C["lcost"]], w=[lg])
        p.copy("gpsimd", lgbf.h[:], lg.h[:], r=[lg], w=[lgbf])
        for tt_ in range(4):
            p.transpose(bankT.h[:, 0:8], lgbf.h[:, tt_ * 128:(tt_ + 1) * 128], C["ident"].h[0:8, 0:8],
                        r=[lgbf, C["ident"]], w=[bankT])
            p.copy("scalar", lt.h[:], bankT.h[:, 0:8], r=[bankT], w=[lt])
            nm = cols.h[:, 0:1]
            p.op("vector", lambda E, a=nm: E.reduce_max(a, lt.h[:], axis=AXX, negate=True),
                 r=[lt], w=[cols])
            ex = pr8.h[:, 0:8]
            p.act(ex, lt.h[:], AF.Exp, bias=nm, r=[lt, cols], w=[pr8])
            sm = cols.h[:, 1:2]
            p.op("vector", lambda E, a=sm, b=ex: E.reduce_sum(a, b, axis=AXX), r=[pr8], w=[cols])
            rc = cols.h[:, 2:3]
            p.op("vector", lambda E, a=rc, b=sm: E.reciprocal(a, b), r=[cols], w=[cols])
            probs = pr8.h[:, 0:8]
            p.ts("vector", probs, probs, rc, None, ALU.mult, None, r=[cols, pr8], w=[pr8])
            m1 = cols.h[:, 3:4]
            p.op("vector", lambda E, a=m1, b=probs: E.reduce_max(a, b, axis=AXX), r=[pr8], w=[cols])
            mask1 = pr8.h[:, 8:16]
            p.ts("vector", mask1, probs, m1, None, ALU.is_equal, None, r=[pr8, cols], w=[pr8])
            p2v = pr8.h[:, 16:24]
            p.stt("vector", p2v, mask1, -1e9, probs, ALU.mult, ALU.add, r=[pr8], w=[pr8])
            m2 = cols.h[:, 4:5]
            p.op("vector", lambda E, a=m2, b=p2v: E.reduce_max(a, b, axis=AXX), r=[pr8], w=[cols])
            mask2 = pr8.h[:, 16:24]
            p.ts("vector", mask2, p2v, m2, None, ALU.is_equal, None, r=[pr8, cols], w=[pr8])
            e1c = cols.h[:, 5:6]
            e2c = cols.h[:, 6:7]
            p.act(e1c, m1, AF.Exp, r=[cols], w=[cols])
            p.act(e2c, m2, AF.Exp, r=[cols], w=[cols])
            dn = cols.h[:, 7:8]
            p.tt("vector", dn, e1c, e2c, ALU.add, r=[cols], w=[cols])
            rd = cols.h[:, 8:9]
            p.op("vector", lambda E, a=rd, b=dn: E.reciprocal(a, b), r=[cols], w=[cols])
            g1 = cols.h[:, 9:10]
            g2 = cols.h[:, 10:11]
            p.tt("vector", g1, e1c, rd, ALU.mult, r=[cols], w=[cols])
            p.tt("vector", g2, e2c, rd, ALU.mult, r=[cols], w=[cols])
            wex = pr8.h[:, 8:16]
            p.ts("vector", wex, mask1, g1, None, ALU.mult, None, r=[pr8, cols], w=[pr8])
            p.stt("vector", wex, mask2, g2, wex, ALU.mult, ALU.add, r=[pr8, cols], w=[pr8])
            p.copy("gpsimd", gb1.h[:, 0:8], wex, r=[pr8], w=[gb1])
            p.transpose(bankT.h[0:8, 8:136], gb1.h[:, 0:8], C["ident"].h[:], r=[gb1, C["ident"]], w=[bankT])
            p.copy("scalar", wexpT.h[0:8, tt_ * 128:(tt_ + 1) * 128], bankT.h[0:8, 8:136],
                   r=[bankT], w=[wexpT])

        if panel == 0:
            p.dbg("gh", gh.h[:], gh, (128, 8 * PW))
            p.dbg("lg", lg.h[:], lg, (8, PW))
            p.dbg("wexpT", wexpT.h[0:8, :], wexpT, (8, PW))
        for e in range(8):
            ne = HID[e] // 128
            p.mm(bankR.h[:, 0:PW], [(C["sel8"].h[0:8, e * 128:(e + 1) * 128], wexpT.h[0:8, 0:PW])],
                 r=[C["sel8"], wexpT], w=[bankR])
            p.copy("scalar", w_bc.h[:], bankR.h[:, 0:PW], r=[bankR], w=[w_bc])
            if panel == 0 and e == 0:
                p.dbg("wbc0", w_bc.h[:], w_bc, (128, PW))
            if panel == 0:
                p.act(gb1.h[:, 0:ne], C[f"e{e}_b1c"].h[:, 0:ne], AF.Gelu,
                      r=[C[f"e{e}_b1c"]], w=[gb1])
                p.dma("gpsimd", b2row.h[0:1, :], W[f"e{e}_b2r"].h.ap()[0:1, :],
                      r=[W[f"e{e}_b2r"]], w=[b2row])
            for j in range(ne):
                wb1 = w1buf[j % 2]
                p.dma("gpsimd", wb1.h[:, 0:1024],
                      W[f"e{e}_w1"].h.ap()[:, j * 128:(j + 1) * 128].rearrange("(c p) f -> p c f", p=128),
                      r=[W[f"e{e}_w1"]], w=[wb1])
                pairs = [(wb1.h[:, lc * 128:(lc + 1) * 128], gh.h[:, lc * PW:(lc + 1) * PW])
                         for lc in range(8)]
                p.mm(bankMID.h[:, 0:PW], pairs, r=[wb1, gh], w=[bankMID])
                p.act(gmid.h[:, j * PW:(j + 1) * PW], bankMID.h[:, 0:PW], AF.Gelu,
                      bias=C[f"e{e}_b1c"].h[:, j:j + 1], r=[bankMID, C[f"e{e}_b1c"]], w=[gmid])
            if panel == 0 and e == 0:
                p.dbg("gmid0", gmid.h[:, 0:8 * PW], gmid, (128, 8 * PW))
            for lc in range(8):
                wb2 = w2buf[lc % 2]
                p.dma("gpsimd", wb2.h[:, 0:ne * 128],
                      W[f"e{e}_w2"].h.ap()[:, lc * 128:(lc + 1) * 128].rearrange("(c p) f -> p c f", p=128),
                      r=[W[f"e{e}_w2"]], w=[wb2])
                if panel == 0:
                    prs = [(gb1.h[:, i:i + 1], wb2.h[:, i * 128:(i + 1) * 128]) for i in range(ne)]
                    p.mm(bankR.h[0:1, 0:128], prs, r=[gb1, wb2], w=[bankR])
                    p.ts("vector", crows.h[0:1, e * 1024 + lc * 128:e * 1024 + (lc + 1) * 128],
                         bankR.h[0:1, 0:128], -1.0, None, ALU.mult, None,
                         r=[bankR], w=[crows])
                    p.tt("vector", ctot.h[0:1, lc * 128:(lc + 1) * 128],
                         ctot.h[0:1, lc * 128:(lc + 1) * 128], bankR.h[0:1, 0:128], ALU.add,
                         r=[bankR, ctot], w=[ctot])
                    p.tt("vector", ctot.h[0:1, lc * 128:(lc + 1) * 128],
                         ctot.h[0:1, lc * 128:(lc + 1) * 128], b2row.h[0:1, lc * 128:(lc + 1) * 128],
                         ALU.add, r=[b2row, ctot], w=[ctot])
                pairs = [(wb2.h[:, i * 128:(i + 1) * 128], gmid.h[:, i * PW:(i + 1) * PW])
                         for i in range(ne)]
                pairs.append((crows.h[0:1, e * 1024 + lc * 128:e * 1024 + (lc + 1) * 128], onespw.h[0:1, :]))
                p.mm(bankF.h[:, 0:PW], pairs, r=[wb2, gmid, crows, onespw], w=[bankF])
                p.tt("vector", tmpf.h[:], bankF.h[:, 0:PW], w_bc.h[:], ALU.mult,
                     r=[bankF, w_bc], w=[tmpf])
                if panel == 0 and e == 0:
                    p.dbg(f"f0_{lc}", tmpf.h[:], tmpf, (128, PW))
                asl = acc.h[:, lc * PW:(lc + 1) * PW]
                if e == 0:
                    p.copy("vector", asl, tmpf.h[:], r=[tmpf], w=[acc])
                else:
                    p.tt("vector", asl, asl, tmpf.h[:], ALU.add, r=[tmpf, acc], w=[acc])
        if panel == 0:
            p.dbg("acc0", acc.h[:], acc, (128, 8 * PW))
            p.dbg("crows", crows.h[0:1, :], crows, (1, 8 * 1024))
            p.copy("gpsimd", ctbf.h[:], ctot.h[:], r=[ctot], w=[ctbf])
            for lc in range(8):
                p.mm(bankR.h[:, lc:lc + 1],
                     [(ctbf.h[0:1, lc * 128:(lc + 1) * 128], onespw.h[0:1, 0:1])],
                     r=[ctbf, onespw], w=[bankR])
            p.copy("vector", ctcols.h[:], bankR.h[:, 0:8], r=[bankR], w=[ctcols])

        for sc in range(8):
            wb1 = w1buf[sc % 2]
            p.dma("gpsimd", wb1.h[:, 0:1024],
                  W["sh_w1"].h.ap()[:, sc * 128:(sc + 1) * 128].rearrange("(c p) f -> p c f", p=128),
                  r=[W["sh_w1"]], w=[wb1])
            pairs = [(wb1.h[:, lc * 128:(lc + 1) * 128], gh.h[:, lc * PW:(lc + 1) * PW])
                     for lc in range(8)]
            p.mm(bankMID.h[:, 0:PW], pairs, r=[wb1, gh], w=[bankMID])
            p.act(gmid.h[:, sc * PW:(sc + 1) * PW], bankMID.h[:, 0:PW], AF.Gelu,
                  bias=C["sh_b1c"].h[:, sc:sc + 1], r=[bankMID, C["sh_b1c"]], w=[gmid])
        for lc in range(8):
            wb1 = w1buf[lc % 2]
            p.dma("gpsimd", wb1.h[:, 0:1024],
                  W["sh_w2"].h.ap()[:, lc * 128:(lc + 1) * 128].rearrange("(c p) f -> p c f", p=128),
                  r=[W["sh_w2"]], w=[wb1])
            pairs = [(wb1.h[:, sc * 128:(sc + 1) * 128], gmid.h[:, sc * PW:(sc + 1) * PW])
                     for sc in range(8)]
            p.mm(bankF.h[:, 0:PW], pairs, r=[wb1, gmid], w=[bankF])
            p.ts("vector", tmpf.h[:], bankF.h[:, 0:PW], C["sh_b2c"].h[:, lc:lc + 1], 0.1,
                 ALU.add, ALU.mult, r=[bankF, C["sh_b2c"]], w=[tmpf])
            asl = acc.h[:, lc * PW:(lc + 1) * PW]
            p.tt("vector", asl, asl, tmpf.h[:], ALU.add, r=[tmpf, acc], w=[acc])

        for lc in range(8):
            p.ts("vector", accbf.h[:, (8 + lc) * PW:(9 + lc) * PW], acc.h[:, lc * PW:(lc + 1) * PW],
                 ctcols.h[:, lc:lc + 1], None, ALU.add, None, r=[acc, ctcols], w=[accbf])

        for oc in range(16):
            wup = w1buf[oc % 2]
            wco = wdbuf[oc % 2]
            p.dma("gpsimd", wup.h[:, 0:1024],
                  W["up_w"].h.ap()[:, oc * 128:(oc + 1) * 128].rearrange("(c p) f -> p c f", p=128),
                  r=[W["up_w"]], w=[wup])
            p.dma("gpsimd", wco.h[:, 0:2048],
                  W["co_w"].h.ap()[:, oc * 128:(oc + 1) * 128].rearrange("(c p) f -> p c f", p=128),
                  r=[W["co_w"]], w=[wco])
            pairs = [(wup.h[:, lc * 128:(lc + 1) * 128], accbf.h[:, (8 + lc) * PW:(9 + lc) * PW])
                     for lc in range(8)]
            pairs += [(wco.h[:, c * 128:(c + 1) * 128], gxn.h[:, c * PW:(c + 1) * PW])
                      for c in range(16)]
            p.mm(bankH.h[:, 0:PW], pairs, r=[wup, wco, accbf, gxn], w=[bankH])
            p.ts("vector", tmpf.h[:], bankH.h[:, 0:PW], C["up_bc"].h[:, oc:oc + 1],
                 C["co_bc"].h[:, oc:oc + 1], ALU.add, ALU.add, r=[bankH, C["up_bc"], C["co_bc"]],
                 w=[tmpf])
            p.copy("gpsimd", ysb.h[:, oc * PW:(oc + 1) * PW], tmpf.h[:], r=[tmpf], w=[ysb])

        emit_post_ln_residual(p, C, ln, ysb, xsq, x_out, x_out.h.ap(), panel,
                              a_bc, b_bc, bankS, bankQ2, bankBC, t_rot)

    p.fence()
    st.close()


def asc(a):
    return np.ascontiguousarray(a)


def u8_view(int32_mat):
    return asc(asc(int32_mat).view(np.uint8)[:, 0::4])


def host_prep(inp):
    w = {}
    w["wqkvuT"] = asc(inp["L0_attn_W_qkvu"].T.astype(bfloat16))
    w["woutT"] = asc(inp["L0_attn_W_out"].T.astype(bfloat16))
    w["slope_row"] = asc(inp["L0_attn_slope_rate"].reshape(1, 16))
    w["gateT8"] = u8_view(asc(inp["L0_ffn_gate_w"].T))
    w["upT8"] = u8_view(asc(inp["L0_ffn_up_w"].T))
    w["downT8"] = u8_view(asc(inp["L0_ffn_down_w"].T))
    w["gsc"] = asc(inp["L0_ffn_gate_scale"].reshape(64, 128).T)
    w["usc"] = asc(inp["L0_ffn_up_scale"].reshape(64, 128).T)
    w["dsc"] = asc(inp["L0_ffn_down_scale"].reshape(16, 128).T)
    for nm in ("Wq", "Wk", "Wv", "Wg", "Wo"):
        w[f"w{nm[1:].lower()}T"] = asc(inp[f"L1_attn_{nm}"].T)
    w["ws1T"] = asc(inp["L1_attn_Ws1"].T)
    w["ws2T"] = asc(inp["L1_attn_Ws2"].T)
    w["bs2c"] = asc(inp["L1_attn_bs2"].reshape(8, 128).T)
    w["gnw16"] = asc(np.broadcast_to(inp["L1_attn_g_norm_weight"], (16, 128)).reshape(1, 2048))
    w["dn_w"] = asc(inp["L1_ffn_down_w"])
    w["dn_bc"] = asc(inp["L1_ffn_down_b"].reshape(8, 128).T)
    w["up_w"] = asc(inp["L1_ffn_up_w"])
    w["up_bc"] = asc(inp["L1_ffn_up_b"].reshape(16, 128).T)
    w["rt_w"] = asc(inp["L1_ffn_router_w"])
    w["rt_bc"] = asc(inp["L1_ffn_router_b"].reshape(8, 1))
    w["sh_w1"] = asc(inp["L1_ffn_shared_w1"])
    w["sh_w2"] = asc(inp["L1_ffn_shared_w2"])
    w["sh_b1c"] = asc(inp["L1_ffn_shared_b1"].reshape(8, 128).T)
    w["sh_b2c"] = asc(inp["L1_ffn_shared_b2"].reshape(8, 128).T)
    w["co_w"] = asc(inp["L1_ffn_core_w"])
    w["co_bc"] = asc(inp["L1_ffn_core_b"].reshape(16, 128).T)
    HID = [1024, 2048, 3072, 4096, 1024, 2048, 3072, 4096]
    for e, hd in enumerate(HID):
        w[f"e{e}_w1"] = asc(inp[f"L1_ffn_e{e}_w1"])
        w[f"e{e}_w2"] = asc(inp[f"L1_ffn_e{e}_w2"])
        w[f"e{e}_b1c"] = asc(inp[f"L1_ffn_e{e}_b1"].reshape(hd // 128, 128).T)
        w[f"e{e}_b2r"] = asc(inp[f"L1_ffn_e{e}_b2"].reshape(1, 1024))
    w["ident"] = np.eye(128, dtype=bfloat16)
    w["arr_row"] = np.arange(1, 257, dtype=np.float32).reshape(1, 256)
    kc = np.zeros((128, 2), np.float32)
    for pp in range(128):
        for j in range(2):
            kc[pp, j] = 256.0 - (j * 128 + pp + 1)
    w["karr_cols"] = kc
    dm = np.zeros((128, 512), np.float32)
    for a in range(128):
        for b in range(256):
            for t in range(2):
                v = b - a - 128 * t
                dm[a, t * 256 + b] = v if v >= 0 else 1e9
    w["Dm"] = dm
    pos = np.arange(T, dtype=np.float64)
    inv_freq = 1.0 / 10000.0 ** (np.arange(0, DH, 2) / DH)
    ang = pos[None, :] * inv_freq[:, None]
    scale = DH ** -0.5
    w["cosq"] = (np.cos(ang) * scale).astype(bfloat16)
    w["sinq"] = (np.sin(ang) * scale).astype(bfloat16)
    w["cosk"] = np.cos(ang).astype(bfloat16)
    w["sink"] = np.sin(ang).astype(bfloat16)
    cm = np.zeros((128, 4 * 512), np.float32)
    for off in range(4):
        for r in range(128):
            for c in range(512):
                cm[r, off * 512 + c] = 1.0 if c >= r + off * 128 else 0.0
    w["cmask"] = cm.astype(bfloat16)
    sel = np.zeros((8, 1024), np.float32)
    for e in range(8):
        sel[e, e * 128:(e + 1) * 128] = 1.0
    w["sel8"] = sel.astype(bfloat16)
    cost = np.array([2 * 1024 * hd for hd in HID], np.float32) * 1e-07
    w["lcost"] = cost.reshape(8, 1)
    return w


W_DTYPES = {
    "gateT8": U8, "upT8": U8, "downT8": U8,
    "wqkvuT": BF16, "woutT": BF16,
    "ident": BF16, "cosq": BF16, "sinq": BF16, "cosk": BF16, "sink": BF16,
    "cmask": BF16, "sel8": BF16,
}


def declare_weights(p, names, host_w):
    W = {}
    for n in names:
        arr = host_w[n]
        dt = W_DTYPES.get(n, F32)
        W[n] = p.dram(n, arr.shape, dt, kind="ExternalInput")
    return W


def make_consts(p, C_stack):
    C = {}
    C["ones_col"] = p.sb("c_onescol", [128, 1], BF16, C_stack)
    C["ones_row"] = p.sb("c_onesrow", [1, 128], BF16, C_stack)
    C["ones_pw"] = p.sb("c_onespw", [1, PW], BF16, C_stack)
    C["neg128_row"] = p.sb("c_n128row", [1, 128], BF16, C_stack)
    p.memset("vector", C["ones_col"].h[:], 1.0, w=[C["ones_col"]])
    p.memset("vector", C["ones_row"].h[:], 1.0, w=[C["ones_row"]])
    p.memset("vector", C["ones_pw"].h[:], 1.0, w=[C["ones_pw"]])
    p.memset("vector", C["neg128_row"].h[:], -128.0, w=[C["neg128_row"]])
    return C


def load_const_tiles(p, C, W, C_stack, need):
    shapes = {
        "ident": (128, 128), "Dm": (128, 512), "arr_row": (1, 256), "karr_cols": (128, 2),
        "cosq": (64, 2048), "sinq": (64, 2048), "cosk": (64, 2048), "sink": (64, 2048),
        "cmask": (128, 2048), "sel8": (8, 1024), "gnw16": (1, 2048), "lcost": (8, 1),
        "gsc": (128, 64), "usc": (128, 64), "dsc": (128, 16), "bs2c": (128, 8),
        "dn_bc": (128, 8), "up_bc": (128, 16), "rt_bc": (8, 1),
        "sh_b1c": (128, 8), "sh_b2c": (128, 8), "co_bc": (128, 16),
        "e0_b1c": (128, 8), "e1_b1c": (128, 16), "e2_b1c": (128, 24), "e3_b1c": (128, 32),
        "e4_b1c": (128, 8), "e5_b1c": (128, 16), "e6_b1c": (128, 24), "e7_b1c": (128, 32),
    }
    for n in need:
        if n not in W:
            continue
        sh = shapes[n]
        dt = W_DTYPES.get(n, F32)
        C[n] = p.sb("ct_" + n, list(sh), dt, C_stack)
        p.dma("sync", C[n].h[:], W[n].h.ap()[:, :], r=[W[n]], w=[C[n]])



PHASE_WEIGHTS = {
    "A": ["wqkvuT", "woutT", "slope_row", "ident", "Dm", "arr_row", "karr_cols"],
    "B": ["gateT8", "upT8", "downT8", "gsc", "usc", "dsc"],
    "C1": ["wqT", "wkT", "wvT", "wgT", "ws1T", "ws2T", "bs2c", "ident",
           "cosq", "sinq", "cosk", "sink"],
    "C2": ["woT", "gnw16", "ident", "cmask"],
    "D": ["dn_w", "dn_bc", "up_w", "up_bc", "rt_w", "rt_bc", "lcost", "sh_w1", "sh_w2",
          "sh_b1c", "sh_b2c", "co_w", "co_bc", "sel8", "ident"]
    + [f"e{e}_{nm}" for e in range(8) for nm in ("w1", "w2", "b1c", "b2r")],
}

PHASE_CONSTS = {
    "A": ["ident", "Dm", "arr_row", "karr_cols"],
    "B": ["gsc", "usc", "dsc"],
    "C1": ["ident", "cosq", "sinq", "cosk", "sink", "bs2c"],
    "C2": ["ident", "cmask", "gnw16"],
    "D": ["ident", "dn_bc", "up_bc", "rt_bc", "lcost", "sh_b1c", "sh_b2c", "co_bc", "sel8"]
    + [f"e{e}_b1c" for e in range(8)],
}


def build_program(phases, host_w, dbg_scratch=False, dump_mids=False):
    nc = bass.Bass("TRN2", target_bir_lowering=False)
    stack = ExitStack()
    p = Prog(nc, stack)

    x0 = p.dram("x0T", (D, T), F32, kind="ExternalInput")
    out = p.dram("outT", (D, T), F32, kind="ExternalOutput")
    names = []
    for ph in phases:
        names += PHASE_WEIGHTS[ph]
    W = declare_weights(p, sorted(set(names)), host_w)

    C = make_consts(p, stack)
    load_const_tiles(p, C, W, stack, ["ident"])

    S = {}
    if "C1" in phases or "C2" in phases:
        sk = "ExternalOutput" if dbg_scratch else "Internal"
        S["qfD"] = p.dram("qfD", (2048, 2048), BF16, kind=sk)
        S["kfD"] = p.dram("kfD", (2048, 2048), BF16, kind=sk)
        S["vtD"] = p.dram("vtD", (2048, 2048), BF16, kind=sk)
        S["gtD"] = p.dram("gtD", (2048, 2048), BF16, kind=sk)
        S["esfD"] = p.dram("esfD", (1024, 2048), BF16, kind=sk)
        S["estD"] = p.dram("estD", (2048, 1024), BF16, kind=sk)
        S["enfD"] = p.dram("enfD", (1024, 2048), BF16, kind=sk)
        S["entD"] = p.dram("entD", (2048, 1024), BF16, kind=sk)

    cur = x0
    residual_phases = [ph for ph in phases if ph != "C1"]
    for i, ph in enumerate(phases):
        if ph == "C1":
            phase_l1_attn_proj(p, C, W, cur, S)
            continue
        last = (residual_phases[-1] == ph)
        mk = "ExternalOutput" if dump_mids else "Internal"
        dst = out if last else p.dram(f"xmid{i}", (D, T), F32, kind=mk)
        if ph == "A":
            phase_l0_attn(p, C, W, cur, dst)
        elif ph == "B":
            phase_l0_ffn(p, C, W, cur, dst)
        elif ph == "C2":
            phase_l1_attn_core(p, C, W, cur, dst, S)
        elif ph == "D":
            phase_l1_ffn(p, C, W, cur, dst)
        else:
            raise NotImplementedError(ph)
        cur = dst

    p.fence()
    return nc, stack


PHASES = ["A", "B", "C1", "C2", "D"]


def kernel(**inputs):
    host_w = host_prep(inputs)
    nc, stack = build_program(PHASES, host_w)
    names = sorted({n for ph in PHASES for n in PHASE_WEIGHTS[ph]})
    in_maps = []
    for core in range(NCORES):
        m = {"x0T": asc(inputs["L0_attn_x"][core].T)}
        for n in names:
            m[n] = host_w[n]
        in_maps.append(m)
    res = run_bass_kernel_spmd(nc, in_maps, core_ids=list(range(NCORES)))
    stack.close()
    outs = [r["outT"] for r in res.results]
    return np.stack([o.T for o in outs], axis=0).astype(np.float32)


# version 04
# speedup vs baseline: 1.1663x; 1.1663x over previous
import numpy as np
import ml_dtypes
from contextlib import ExitStack

import concourse.bass as bass
import concourse.mybir as mybir
from concourse.bass_utils import run_bass_kernel_spmd

F32 = mybir.dt.float32
BF16 = mybir.dt.bfloat16
U8 = mybir.dt.uint8
AF = mybir.ActivationFunctionType
ALU = mybir.AluOpType
AXX = mybir.AxisListType.X
bfloat16 = ml_dtypes.bfloat16

DBG = {}
PARANOIA = 0
DMA_WINDOW = 0
DMA_CHAIN = 1
T = 2048
D = 2048
NCORES = 8
PW = 512
NP = T // PW
NH = 16
DH = 128

ENGS = ("tensor", "vector", "scalar", "gpsimd", "sync")
DMA_Q = {"sync": "sy_d", "gpsimd": "gp_d", "scalar": "sc_d"}


class Buf:
    __slots__ = ("h", "name", "w", "r")

    def __init__(self, h, name):
        self.h = h
        self.name = name
        self.w = None
        self.r = {}


class Prog:
    def __init__(self, nc, stack):
        self.nc = nc
        self.stack = stack
        self.sems = {}
        self.cnt = {}
        for s in ENGS + tuple(DMA_Q.values()):
            if s == "sync":
                continue
            self.sems[s] = stack.enter_context(nc.semaphore("S_" + s))
            self.cnt[s] = 0
        self.waited = {e: {} for e in ENGS}
        self.dma_log = {q: [] for q in DMA_Q.values()}
        self.last_dma = None
        self.banks = [Buf(stack.enter_context(nc.psum_tensor(f"pbank{i}", [128, 512], F32)), f"pbank{i}")
                      for i in range(7)]
        self.bankT = Buf(stack.enter_context(nc.psum_tensor("pbankT", [128, 1024], BF16)), "pbankT")

    def sb(self, name, shape, dtype, st=None):
        h = (st or self.stack).enter_context(self.nc.sbuf_tensor(name, shape, dtype))
        return Buf(h, name)

    def dram(self, name, shape, dtype, kind="Internal"):
        h = self.nc.dram_tensor(name, list(shape), dtype, kind=kind)
        return Buf(h, name)

    def _wait(self, eng, tok):
        if tok is None:
            return
        s, v = tok
        w = self.waited[eng]
        if w.get(s, 0) >= v:
            return
        getattr(self.nc, eng).wait_ge(self.sems[s], v)
        w[s] = v

    def op(self, eng, fn, r=(), w=(), sem=None, inc=1):
        sem = sem or eng
        is_dma = sem in ("sy_d", "gp_d", "sc_d")
        if PARANOIA == 2 or (PARANOIA == 3 and is_dma) or (PARANOIA == 4 and not is_dma):
            for s, v in self.cnt.items():
                if v:
                    self._wait(eng, (s, v))
        elif PARANOIA in (5, 6, 7) and is_dma:
            for s, v in self.cnt.items():
                if not v:
                    continue
                s_dma = s in ("sy_d", "gp_d", "sc_d")
                if ((PARANOIA == 5 and not s_dma) or (PARANOIA == 6 and s_dma)
                        or (PARANOIA == 7 and s_dma and s != sem)):
                    self._wait(eng, (s, v))
        for b in r:
            self._wait(eng, b.w)
        for b in w:
            self._wait(eng, b.w)
            for s, v in b.r.items():
                self._wait(eng, (s, v))
        inst = fn(getattr(self.nc, eng))
        self.cnt[sem] += inc
        inst.then_inc(self.sems[sem], inc)
        tok = (sem, self.cnt[sem])
        for b in r:
            b.r[tok[0]] = max(b.r.get(tok[0], 0), tok[1])
        for b in w:
            b.w = tok
            b.r = {}
        return tok

    def dbg(self, name, src_ap, src_buf, shape):
        if name not in DBG:
            return
        b = self.dram("dbg_" + name, shape, F32, kind="ExternalOutput")
        self.dma("gpsimd", b.h.ap()[:, :], src_ap, r=[src_buf], w=[b])

    def dma(self, eng, out_ap, in_ap, r=(), w=(), **kw):
        q = DMA_Q[eng]
        log = self.dma_log[q]
        if DMA_CHAIN and self.last_dma is not None:
            self._wait(eng, self.last_dma)
        elif DMA_WINDOW and len(log) >= DMA_WINDOW:
            self._wait(eng, (q, log[-DMA_WINDOW]))
        tok = self.op(eng, lambda E: E.dma_start(out=out_ap, in_=in_ap, **kw),
                      r=r, w=w, sem=q, inc=16)
        log.append(tok[1])
        self.last_dma = tok
        return tok

    def fence(self):
        for e in ENGS:
            for s, v in self.cnt.items():
                if v:
                    self._wait(e, (s, v))

    def tt(self, eng, out, a, b, alu, r, w):
        return self.op(eng, lambda E: E.tensor_tensor(out, a, b, op=alu), r=r, w=w)

    def ts(self, eng, out, a, s1, s2, op0, op1, r, w):
        if s2 is None:
            return self.op(eng, lambda E: E.tensor_scalar(out, a, s1, None, op0=op0), r=r, w=w)
        return self.op(eng, lambda E: E.tensor_scalar(out, a, s1, s2, op0=op0, op1=op1), r=r, w=w)

    def stt(self, eng, out, in0, scalar, in1, op0, op1, r, w):
        return self.op(eng, lambda E: E.scalar_tensor_tensor(out, in0, scalar, in1, op0, op1), r=r, w=w)

    def act(self, out, in_, func, r, w, bias=0.0, scale=1.0, accum_out=None):
        return self.op("scalar",
                       lambda E: E.activation(out=out, in_=in_, func=func, bias=bias, scale=scale,
                                              accum_out=accum_out),
                       r=r, w=w)

    def copy(self, eng, out, in_, r, w):
        if eng == "scalar":
            return self.op(eng, lambda E: E.copy(out, in_), r=r, w=w)
        return self.op(eng, lambda E: E.tensor_copy(out, in_), r=r, w=w)

    def memset(self, eng, ap, val, w):
        return self.op(eng, lambda E: E.memset(ap, val), r=(), w=w)

    def mm(self, out_ap, pairs, r, w):
        def fn(E):
            inst = None
            n = len(pairs)
            for i, (lh, rh) in enumerate(pairs):
                inst = E.matmul(out_ap, lh, rh, start=(i == 0), stop=(i == n - 1))
            return inst
        return self.op("tensor", fn, r=r, w=w)

    def mm_acc(self, out_ap, pairs, r, w, start, stop):
        def fn(E):
            inst = None
            n = len(pairs)
            for i, (lh, rh) in enumerate(pairs):
                inst = E.matmul(out_ap, lh, rh, start=(start and i == 0), stop=(stop and i == n - 1))
            return inst
        return self.op("tensor", fn, r=r, w=w)

    def transpose(self, out_ap, in_ap, ident_ap, r, w):
        return self.op("tensor", lambda E: E.transpose(out_ap, in_ap, ident_ap), r=r, w=w)



class LNStats:
    def __init__(self, p, st, tag):
        self.rows = p.sb(f"{tag}_rows", [1, PW * 4], F32, st)
        self.rows_bf = p.sb(f"{tag}_rowsbf", [1, PW * 4], BF16, st)
        self.eps = p.sb(f"{tag}_eps", [1, 1], F32, st)

    def s(self, i, n=PW):
        return self.rows.h[0:1, i * PW:i * PW + n]

    def sbf(self, i, n=PW):
        return self.rows_bf.h[0:1, i * PW:i * PW + n]

    def mu(self):
        return self.s(0)

    def a(self):
        return self.s(1)

    def b(self):
        return self.s(2)

    def sn(self):
        return self.s(3)

    def a_bf(self):
        return self.sbf(0)

    def b_bf(self):
        return self.sbf(1)

    def sn_bf(self):
        return self.sbf(2)


def emit_ln_stats(p, C, ln, xb, xsq, nchunks, eps, bank_s, bank_q, want_sn=False):
    nd = nchunks * 128
    ones = C["ones_col"]
    for c in range(nchunks):
        xb_c = xb.h[:, c * PW:(c + 1) * PW]
        sq_c = xsq.h[:, (c % 2) * PW:((c % 2) + 1) * PW]
        p.act(sq_c, xb_c, AF.Square, r=[xb], w=[xsq])
        p.mm_acc(bank_s.h[0:1, 0:PW], [(ones.h[:, 0:1], xb_c)], r=[ones, xb], w=[bank_s],
                 start=(c == 0), stop=(c == nchunks - 1))
        p.mm_acc(bank_q.h[0:1, 0:PW], [(ones.h[:, 0:1], sq_c)], r=[ones, xsq], w=[bank_q],
                 start=(c == 0), stop=(c == nchunks - 1))
    rows = ln.rows
    p.ts("vector", ln.s(0), bank_s.h[0:1, 0:PW], 1.0 / nd, None, ALU.mult, None, r=[bank_s], w=[rows])
    p.act(ln.s(3), ln.s(0), AF.Square, r=[rows], w=[rows])
    p.stt("vector", ln.s(1), bank_q.h[0:1, 0:PW], 1.0 / nd, ln.s(3), ALU.mult, ALU.subtract,
          r=[bank_q, rows], w=[rows])
    p.memset("vector", ln.eps.h[:], eps, w=[ln.eps])
    p.act(ln.s(3), ln.s(1), AF.Ln, bias=ln.eps.h[0:1, 0:1], r=[rows, ln.eps], w=[rows])
    p.act(ln.s(1), ln.s(3), AF.Exp, scale=-0.5, r=[rows], w=[rows])
    p.stt("vector", ln.s(2), ln.s(0), -1.0, ln.s(1), ALU.mult, ALU.mult, r=[rows], w=[rows])
    p.copy("gpsimd", ln.a_bf(), ln.a(), r=[rows], w=[ln.rows_bf])
    p.copy("gpsimd", ln.b_bf(), ln.b(), r=[rows], w=[ln.rows_bf])
    if want_sn:
        p.stt("vector", ln.s(3), bank_s.h[0:1, 0:PW], 1.0, ln.s(1), ALU.mult, ALU.mult,
              r=[bank_s, rows], w=[rows])
        p.stt("vector", ln.s(3), ln.s(2), float(nd), ln.s(3), ALU.mult, ALU.add, r=[rows], w=[rows])
        p.copy("gpsimd", ln.sn_bf(), ln.sn(), r=[rows], w=[ln.rows_bf])


def emit_bcast(p, C, row_bf_ap, rowsrc_buf, bank_bc, out_buf, n=PW):
    sl = bank_bc.h[:, 0:n]
    p.mm(sl, [(C["ones_row"].h[0:1, 0:128], row_bf_ap)], r=[C["ones_row"], rowsrc_buf], w=[bank_bc])
    p.copy("scalar", out_buf.h[:, 0:n], sl, r=[bank_bc], w=[out_buf])


def emit_xn(p, C, ln, xb, a_bc, b_bc, bank_bc, nchunks):
    emit_bcast(p, C, ln.a_bf(), ln.rows_bf, bank_bc, a_bc)
    emit_bcast(p, C, ln.b_bf(), ln.rows_bf, bank_bc, b_bc)
    for c in range(nchunks):
        sl = xb.h[:, c * PW:(c + 1) * PW]
        tmp = C["tmp_f32"]
        p.tt("gpsimd", tmp.h[:, 0:PW], sl, a_bc.h[:], ALU.mult, r=[xb, a_bc], w=[tmp])
        p.tt("vector", sl, tmp.h[:, 0:PW], b_bc.h[:], ALU.add, r=[tmp, b_bc], w=[xb])


def emit_post_ln_residual(p, C, ln2, y, ysq, x_out_buf, x_out_ap, panel, a2_bc, b2_bc,
                          bank_s, bank_q, bank_bc, t_rot, nchunks=16):
    emit_ln_stats(p, C, ln2, y, ysq, nchunks, 1e-5, bank_s, bank_q)
    emit_bcast(p, C, ln2.a_bf(), ln2.rows_bf, bank_bc, a2_bc)
    emit_bcast(p, C, ln2.b_bf(), ln2.rows_bf, bank_bc, b2_bc)
    for c in range(nchunks):
        y_c = y.h[:, c * PW:(c + 1) * PW]
        trb, off = t_rot[c % 2]
        p.tt("gpsimd", trb.h[:, off:off + PW], y_c, a2_bc.h[:], ALU.mult, r=[y, a2_bc], w=[trb])
        p.tt("vector", trb.h[:, off + PW:off + 2 * PW], trb.h[:, off:off + PW], b2_bc.h[:],
             ALU.add, r=[trb, b2_bc], w=[trb])
        p.dma("gpsimd", x_out_ap[c * 128:(c + 1) * 128, panel * PW:(panel + 1) * PW],
              trb.h[:, off + PW:off + 2 * PW], r=[trb, x_out_buf], w=[x_out_buf], accum_op=ALU.add)



def emit_row_pair(p, ln, row_f32_ap, hi_bf_ap, lo_bf_ap, scratch_f32_ap):
    p.copy("gpsimd", hi_bf_ap, row_f32_ap, r=[ln.rows], w=[ln.rows_bf])
    p.copy("gpsimd", scratch_f32_ap, hi_bf_ap, r=[ln.rows_bf], w=[ln.rows])
    p.tt("vector", scratch_f32_ap, row_f32_ap, scratch_f32_ap, ALU.subtract, r=[ln.rows], w=[ln.rows])
    p.copy("gpsimd", lo_bf_ap, scratch_f32_ap, r=[ln.rows], w=[ln.rows_bf])



def phase_l0_attn(p, C, W, x_in, x_out):
    nc = p.nc
    st = ExitStack()
    xb = p.sb("A_xb", [128, 16 * PW], BF16, st)
    xsq = p.sb("A_xsq", [128, 2 * PW], BF16, st)
    qsb = p.sb("A_q", [128, 16 * PW], BF16, st)
    ksb = p.sb("A_k", [128, 16 * PW], BF16, st)
    vsb = p.sb("A_v", [128, 16 * PW], BF16, st)
    usb = p.sb("A_u", [128, 16 * PW], BF16, st)
    osb = p.sb("A_o", [128, 16 * PW], BF16, st)
    wbuf = [p.sb(f"A_w{i}", [128, 2048], BF16, st) for i in range(2)]
    masks = p.sb("A_masks", [128, 16 * 512], BF16, st)
    qd_bc = p.sb("A_qdbc", [128, 16 * 256], BF16, st)
    kd_cols = p.sb("A_kdcols", [128, 16 * 2], F32, st)
    bd_cols = p.sb("A_bdcols", [128, 16], F32, st)
    kv = p.sb("A_kv", [128, 16 * 128], F32, st)
    kvb = p.sb("A_kvb", [128, 16 * 128], BF16, st)
    scol = p.sb("A_scol", [128, 16], F32, st)
    srow_bf = p.sb("A_srowbf", [1, 16], BF16, st)
    qq = p.sb("A_qq", [128, 256], BF16, st)
    qkm = p.sb("A_qkm", [128, 2 * 256], BF16, st)
    ktd = p.sb("A_ktd", [128, 2 * 128], BF16, st)
    vtd = p.sb("A_vtd", [128, 2 * 128], BF16, st)
    tmp_f32 = p.sb("A_tmpf", [128, PW], F32, st)
    a_bc = p.sb("A_abc", [128, PW], F32, st)
    b_bc = p.sb("A_bbc", [128, PW], F32, st)
    a2_bc = p.sb("A_a2bc", [128, PW], F32, st)
    b2_bc = p.sb("A_b2bc", [128, PW], F32, st)
    t_rot = [(p.sb(f"A_t{i}", [128, 2 * PW], F32, st), 0) for i in range(2)]
    rst_bc = p.sb("A_rstbc", [128, PW], F32, st)
    ln = LNStats(p, st, "A_ln")
    ln2 = ln
    C["tmp_f32"] = tmp_f32
    load_const_tiles(p, C, W, st, ["Dm", "arr_row", "karr_cols"])

    bankM, bankQK0, bankQK1, bankO, bankKV, bankS, bankQ2 = p.banks
    bankTR = p.bankT
    bank_bc = bankQK0

    p.dma("sync", x_out.h.ap()[:, :], x_in.h.ap()[:, :], r=[x_in], w=[x_out])
    p.dma("gpsimd", srow_bf.h[0:1, :], W["slope_row"].h.ap()[0:1, :], r=[W["slope_row"]], w=[srow_bf])
    p.mm(bankM.h[:, 0:16], [(C["ones_row"].h[0:1, 0:128], srow_bf.h[0:1, 0:16])],
         r=[C["ones_row"], srow_bf], w=[bankM])
    p.ts("vector", scol.h[:], bankM.h[:, 0:16], -1.0, None, ALU.mult, None, r=[bankM], w=[scol])
    for h in range(NH):
        sc = scol.h[:, h:h + 1]
        for j in range(2):
            p.ts("vector", tmp_f32.h[:, 0:256], C["Dm"].h[:, j * 256:(j + 1) * 256], sc, None,
                 ALU.mult, None, r=[C["Dm"], scol], w=[tmp_f32])
            p.act(masks.h[:, h * 512 + j * 256: h * 512 + (j + 1) * 256], tmp_f32.h[:, 0:256],
                  AF.Exp, r=[tmp_f32], w=[masks])
        p.ts("vector", ln.s(3, 256), C["arr_row"].h[0:1, 0:256], scol.h[0:1, h:h + 1], None,
             ALU.mult, None, r=[C["arr_row"], scol], w=[ln.rows])
        p.act(ln.sbf(3, 256), ln.s(3, 256), AF.Exp, r=[ln.rows], w=[ln.rows_bf])
        p.mm(bankM.h[:, 0:256], [(C["ones_row"].h[0:1, 0:128], ln.sbf(3, 256))],
             r=[C["ones_row"], ln.rows_bf], w=[bankM])
        p.copy("scalar", qd_bc.h[:, h * 256:(h + 1) * 256], bankM.h[:, 0:256], r=[bankM], w=[qd_bc])
        p.ts("vector", tmp_f32.h[:, 0:2], C["karr_cols"].h[:, 0:2], sc, None, ALU.mult, None,
             r=[C["karr_cols"], scol], w=[tmp_f32])
        p.act(kd_cols.h[:, h * 2:(h + 1) * 2], tmp_f32.h[:, 0:2], AF.Exp, r=[tmp_f32], w=[kd_cols])
        p.ts("vector", tmp_f32.h[:, 2:3], scol.h[:, h:h + 1], 256.0, None, ALU.mult, None,
             r=[scol], w=[tmp_f32])
        p.act(bd_cols.h[:, h:h + 1], tmp_f32.h[:, 2:3], AF.Exp, r=[tmp_f32], w=[bd_cols])
    p.memset("vector", kv.h[:], 0.0, w=[kv])
    p.memset("vector", kvb.h[:], 0.0, w=[kvb])

    wT = W["wqkvuT"]
    woT = W["woutT"]

    for panel in range(NP):
        t0 = panel * PW
        p.dma("gpsimd", xb.h[:],
              x_in.h.ap()[:, t0:t0 + PW].rearrange("(c p) t -> p c t", p=128),
              r=[x_in], w=[xb])
        emit_ln_stats(p, C, ln, xb, xsq, 16, 1e-5, bankS, bankQ2)
        emit_xn(p, C, ln, xb, a_bc, b_bc, bank_bc, 16)

        for fc in range(64):
            wb = wbuf[fc % 2]
            p.dma("gpsimd", wb.h[:], wT.h.ap()[fc * 128:(fc + 1) * 128, :],
                  r=[wT], w=[wb])
            pairs = [(wb.h[:, c * 128:(c + 1) * 128], xb.h[:, c * PW:(c + 1) * PW]) for c in range(16)]
            p.mm(bankM.h[:, 0:PW], pairs, r=[wb, xb], w=[bankM])
            kind, idx = divmod(fc, 16)
            dstbuf = (qsb, ksb, vsb, usb)[kind]
            dst = dstbuf.h[:, idx * PW:(idx + 1) * PW]
            if kind <= 1:
                p.act(dst, bankM.h[:, 0:PW], AF.Silu, r=[bankM], w=[dstbuf])
            else:
                p.copy("scalar", dst, bankM.h[:, 0:PW], r=[bankM], w=[dstbuf])

        for blk in range(2):
            boff = blk * 256
            for h in range(NH):
                qs = qsb.h[:, h * PW + boff:h * PW + boff + 256]
                ks = ksb.h[:, h * PW + boff:h * PW + boff + 256]
                vs = vsb.h[:, h * PW + boff:h * PW + boff + 256]
                kv_h = kv.h[:, h * 128:(h + 1) * 128]
                kvb_h = kvb.h[:, h * 128:(h + 1) * 128]
                p.tt("vector", qq.h[:], qs, qd_bc.h[:, h * 256:(h + 1) * 256], ALU.mult,
                     r=[qsb, qd_bc], w=[qq])
                for j in range(2):
                    p.transpose(bankTR.h[:, 0:128],
                                ks[:, j * 128:(j + 1) * 128], C["ident"].h[:],
                                r=[ksb, C["ident"]], w=[bankTR])
                    p.op("scalar", lambda E, j=j, h=h: E.activation(
                        out=ktd.h[:, j * 128:(j + 1) * 128], in_=bankTR.h[:, 0:128],
                        func=AF.Copy, scale=kd_cols.h[:, h * 2 + j:h * 2 + j + 1], bias=0.0),
                        r=[bankTR, kd_cols], w=[ktd])
                    p.transpose(bankTR.h[:, 128:256],
                                vs[:, j * 128:(j + 1) * 128], C["ident"].h[:],
                                r=[vsb, C["ident"]], w=[bankTR])
                    p.copy("scalar", vtd.h[:, j * 128:(j + 1) * 128],
                           bankTR.h[:, 128:256], r=[bankTR], w=[vtd])
                for j, bankj in ((0, bankQK0), (1, bankQK1)):
                    p.mm(bankj.h[:, 0:256], [(ks[:, j * 128:(j + 1) * 128], qs)],
                         r=[ksb, qsb], w=[bankj])
                    p.tt("vector", qkm.h[:, j * 256:(j + 1) * 256], bankj.h[:, 0:256],
                         masks.h[:, h * 512 + j * 256:h * 512 + (j + 1) * 256], ALU.mult,
                         r=[bankj, masks], w=[qkm])
                p.mm(bankO.h[:, 0:256],
                     [(kvb_h, qq.h[:]),
                      (vtd.h[:, 0:128], qkm.h[:, 0:256]),
                      (vtd.h[:, 128:256], qkm.h[:, 256:512])],
                     r=[kvb, qq, vtd, qkm], w=[bankO])
                p.copy("scalar", osb.h[:, h * PW + boff:h * PW + boff + 256], bankO.h[:, 0:256],
                       r=[bankO], w=[osb])
                p.mm(bankKV.h[:, 0:128],
                     [(ktd.h[:, 0:128], vtd.h[:, 0:128]),
                      (ktd.h[:, 128:256], vtd.h[:, 128:256])],
                     r=[ktd, vtd], w=[bankKV])
                p.stt("vector", kv_h, kv_h, bd_cols.h[:, h:h + 1], bankKV.h[:, 0:128],
                      ALU.mult, ALU.add, r=[kv, bd_cols, bankKV], w=[kv])
                p.copy("gpsimd", kvb_h, kv_h, r=[kv], w=[kvb])

        for c in range(16):
            o_c = osb.h[:, c * PW:(c + 1) * PW]
            sq_c = xsq.h[:, (c % 2) * PW:((c % 2) + 1) * PW]
            p.act(sq_c, o_c, AF.Square, r=[osb], w=[xsq])
            p.mm_acc(bankS.h[0:1, 0:PW], [(C["ones_col"].h[:, 0:1], sq_c)],
                     r=[C["ones_col"], xsq], w=[bankS], start=(c == 0), stop=(c == 15))
        p.ts("vector", ln.s(0), bankS.h[0:1, 0:PW], 1.0 / 2048.0, None, ALU.mult, None,
             r=[bankS], w=[ln.rows])
        p.memset("vector", ln.eps.h[:], 1e-6, w=[ln.eps])
        p.act(ln.s(3), ln.s(0), AF.Ln, bias=ln.eps.h[0:1, 0:1], r=[ln.rows, ln.eps], w=[ln.rows])
        p.act(ln.s(0), ln.s(3), AF.Exp, scale=-0.5, r=[ln.rows], w=[ln.rows])
        p.copy("gpsimd", ln.sbf(3), ln.s(0), r=[ln.rows], w=[ln.rows_bf])
        emit_bcast(p, C, ln.sbf(3), ln.rows_bf, bank_bc, rst_bc)
        for c in range(16):
            o_c = osb.h[:, c * PW:(c + 1) * PW]
            u_c = usb.h[:, c * PW:(c + 1) * PW]
            p.tt("gpsimd", tmp_f32.h[:, 0:PW], o_c, rst_bc.h[:], ALU.mult, r=[osb, rst_bc], w=[tmp_f32])
            p.tt("vector", o_c, tmp_f32.h[:, 0:PW], u_c, ALU.mult, r=[tmp_f32, usb], w=[osb])

        for fc in range(16):
            wb = wbuf[fc % 2]
            p.dma("gpsimd", wb.h[:], woT.h.ap()[fc * 128:(fc + 1) * 128, :],
                  r=[woT], w=[wb])
            pairs = [(wb.h[:, c * 128:(c + 1) * 128], osb.h[:, c * PW:(c + 1) * PW]) for c in range(16)]
            p.mm(bankM.h[:, 0:PW], pairs, r=[wb, osb], w=[bankM])
            p.copy("scalar", usb.h[:, fc * PW:(fc + 1) * PW], bankM.h[:, 0:PW], r=[bankM], w=[usb])

        emit_post_ln_residual(p, C, ln2, usb, xsq, x_out, x_out.h.ap(), panel,
                              a2_bc, b2_bc, bankS, bankQ2, bank_bc, t_rot)

    p.fence()
    st.close()





def phase_l0_ffn(p, C, W, x_in, x_out):
    st = ExitStack()
    xb = p.sb("B_xb", [128, 16 * PW], BF16, st)
    xsq = p.sb("B_xsq", [128, 2 * PW], BF16, st)
    hsb = p.sb("B_h", [128, 64 * PW], BF16, st)
    ysb = p.sb("B_y", [128, 16 * PW], BF16, st)
    gs = p.sb("B_gs", [128, 2 * PW], BF16, st)
    us = p.sb("B_us", [128, 2 * PW], BF16, st)
    wgbuf = [p.sb(f"B_wg{i}", [128, 2048], BF16, st) for i in range(2)]
    wubuf = [p.sb(f"B_wu{i}", [128, 2048], BF16, st) for i in range(2)]
    wdbuf = [p.sb(f"B_wd{i}", [128, 8192], BF16, st) for i in range(2)]
    tmp_f32 = p.sb("B_tmpf", [128, PW], F32, st)
    a_bc = p.sb("B_abc", [128, PW], F32, st)
    b_bc = p.sb("B_bbc", [128, PW], F32, st)
    t_rot = [(p.sb(f"B_t{i}", [128, 2 * PW], F32, st), 0) for i in range(2)]
    ln = LNStats(p, st, "B_ln")
    C["tmp_f32"] = tmp_f32
    load_const_tiles(p, C, W, st, ["gsc", "usc", "dsc"])

    bankG, bankU, bankD, bankBC, bankS, bankQ2, _sp = p.banks

    gsc, usc, dsc = C["gsc"], C["usc"], C["dsc"]
    n128 = C["neg128_row"]

    p.dma("sync", x_out.h.ap()[:, :], x_in.h.ap()[:, :], r=[x_in], w=[x_out])

    for panel in range(NP):
        t0 = panel * PW
        p.dma("gpsimd", xb.h[:],
              x_in.h.ap()[:, t0:t0 + PW].rearrange("(c p) t -> p c t", p=128),
              r=[x_in], w=[xb])
        emit_ln_stats(p, C, ln, xb, xsq, 16, 1e-5, bankS, bankQ2)
        emit_xn(p, C, ln, xb, a_bc, b_bc, bankBC, 16)
        for c in range(16):
            p.mm_acc(bankS.h[0:1, 0:PW], [(C["ones_col"].h[:, 0:1], xb.h[:, c * PW:(c + 1) * PW])],
                     r=[C["ones_col"], xb], w=[bankS], start=(c == 0), stop=(c == 15))
        p.copy("vector", ln.s(3), bankS.h[0:1, 0:PW], r=[bankS], w=[ln.rows])
        emit_row_pair(p, ln, ln.s(3), ln.sbf(2), ln.sbf(3), ln.s(0))

        for fc in range(64):
            wg = wgbuf[fc % 2]
            wu = wubuf[fc % 2]
            p.dma("gpsimd", wg.h[:], W["gateT8"].h.ap()[fc * 128:(fc + 1) * 128, :],
                  r=[W["gateT8"]], w=[wg])
            p.dma("gpsimd", wu.h[:], W["upT8"].h.ap()[fc * 128:(fc + 1) * 128, :],
                  r=[W["upT8"]], w=[wu])
            pairs_g = [(wg.h[:, c * 128:(c + 1) * 128], xb.h[:, c * PW:(c + 1) * PW]) for c in range(16)]
            pairs_g.append((n128.h[0:1, 0:128], ln.sbf(2)))
            pairs_g.append((n128.h[0:1, 0:128], ln.sbf(3)))
            p.mm(bankG.h[:, 0:PW], pairs_g, r=[wg, xb, n128, ln.rows_bf], w=[bankG])
            pairs_u = [(wu.h[:, c * 128:(c + 1) * 128], xb.h[:, c * PW:(c + 1) * PW]) for c in range(16)]
            pairs_u.append((n128.h[0:1, 0:128], ln.sbf(2)))
            pairs_u.append((n128.h[0:1, 0:128], ln.sbf(3)))
            p.mm(bankU.h[:, 0:PW], pairs_u, r=[wu, xb, n128, ln.rows_bf], w=[bankU])
            g_r = gs.h[:, (fc % 2) * PW:((fc % 2) + 1) * PW]
            u_r = us.h[:, (fc % 2) * PW:((fc % 2) + 1) * PW]
            p.act(g_r, bankG.h[:, 0:PW], AF.Silu, scale=gsc.h[:, fc:fc + 1], r=[bankG, gsc], w=[gs])
            p.act(u_r, bankU.h[:, 0:PW], AF.Copy, scale=usc.h[:, fc:fc + 1], bias=0.0,
                  r=[bankU, usc], w=[us])
            p.tt("gpsimd", hsb.h[:, fc * PW:(fc + 1) * PW], g_r, u_r, ALU.mult, r=[gs, us], w=[hsb])

        for c in range(64):
            p.mm_acc(bankS.h[0:1, 0:PW], [(C["ones_col"].h[:, 0:1], hsb.h[:, c * PW:(c + 1) * PW])],
                     r=[C["ones_col"], hsb], w=[bankS], start=(c == 0), stop=(c == 63))
        p.copy("vector", ln.s(3), bankS.h[0:1, 0:PW], r=[bankS], w=[ln.rows])
        emit_row_pair(p, ln, ln.s(3), ln.sbf(2), ln.sbf(3), ln.s(0))

        for oc in range(16):
            wd = wdbuf[oc % 2]
            p.dma("gpsimd", wd.h[:], W["downT8"].h.ap()[oc * 128:(oc + 1) * 128, :],
                  r=[W["downT8"]], w=[wd])
            pairs = [(wd.h[:, c * 128:(c + 1) * 128], hsb.h[:, c * PW:(c + 1) * PW]) for c in range(64)]
            pairs.append((n128.h[0:1, 0:128], ln.sbf(2)))
            pairs.append((n128.h[0:1, 0:128], ln.sbf(3)))
            p.mm(bankD.h[:, 0:PW], pairs, r=[wd, hsb, n128, ln.rows_bf], w=[bankD])
            p.act(ysb.h[:, oc * PW:(oc + 1) * PW], bankD.h[:, 0:PW], AF.Copy,
                  scale=dsc.h[:, oc:oc + 1], bias=0.0, r=[bankD, dsc], w=[ysb])

        emit_post_ln_residual(p, C, ln, ysb, xsq, x_out, x_out.h.ap(), panel,
                              a_bc, b_bc, bankS, bankQ2, bankBC, t_rot)

    p.fence()
    st.close()




def phase_l1_attn_proj(p, C, W, x_in, S):
    st = ExitStack()
    xb = p.sb("C_xb", [128, 16 * PW], BF16, st)
    xsq = p.sb("C_xsq", [128, 2 * PW], BF16, st)
    es32 = p.sb("C_es32", [128, 8 * 2048], F32, st)
    cz = p.sb("C_cz", [128, 2048], F32, st)
    wbuf = [p.sb(f"C_w{i}", [128, 2048], BF16, st) for i in range(2)]
    wpan = [p.sb(f"C_wp{i}", [128, 16 * 512], BF16, st) for i in range(2)]
    ropea = p.sb("C_ropea", [64, 4 * PW], F32, st)
    qf2 = p.sb("C_qf2", [64, 4 * PW], BF16, st)
    otile = p.sb("C_otile", [128, 2 * PW], BF16, st)
    ws1sb = p.sb("C_ws1", [128, 16 * 16], BF16, st)
    ws2sb = p.sb("C_ws2", [16, 1024], BF16, st)
    s1bf = p.sb("C_s1", [16, PW], BF16, st)
    esbf = p.sb("C_esbf", [128, 2048], BF16, st)
    estT = p.sb("C_estT", [128, 2048], BF16, st)
    tmp_f32 = p.sb("C_tmpf", [128, PW], F32, st)
    a_bc = p.sb("C_abc", [128, PW], F32, st)
    b_bc = p.sb("C_bbc", [128, PW], F32, st)
    ln = LNStats(p, st, "C_ln")
    C["tmp_f32"] = tmp_f32
    load_const_tiles(p, C, W, st, ["cosq", "sinq", "cosk", "sink", "bs2c"])

    bankM, bankV, bankS2, bankBC, bankS, bankQ2, _sp = p.banks
    bankT = p.bankT

    p.dma("gpsimd", ws1sb.h[:], W["ws1T"].h.ap()[:, :].rearrange("(c p) f -> p c f", p=128),
          r=[W["ws1T"]], w=[ws1sb])
    p.dma("gpsimd", ws2sb.h[:], W["ws2T"].h.ap()[:, :], r=[W["ws2T"]], w=[ws2sb])

    for panel in range(NP):
        t0 = panel * PW
        p.dma("gpsimd", xb.h[:],
              x_in.h.ap()[:, t0:t0 + PW].rearrange("(c p) t -> p c t", p=128),
              r=[x_in], w=[xb])
        emit_ln_stats(p, C, ln, xb, xsq, 16, 1e-5, bankS, bankQ2)
        emit_xn(p, C, ln, xb, a_bc, b_bc, bankBC, 16)

        for which, wT, dram, cosn, sinn in (
                (0, W["wqT"], S["qfD"], "cosq", "sinq"),
                (1, W["wkT"], S["kfD"], "cosk", "sink")):
            for hc in range(NH):
                wb = wbuf[hc % 2]
                p.dma("gpsimd", wb.h[:], wT.h.ap()[hc * 128:(hc + 1) * 128, :],
                      r=[wT], w=[wb])
                pairs = [(wb.h[:, c * 128:(c + 1) * 128], xb.h[:, c * PW:(c + 1) * PW]) for c in range(16)]
                p.mm(bankM.h[:, 0:PW], pairs, r=[wb, xb], w=[bankM])
                cosT = C[cosn]
                sinT = C[sinn]
                cs = cosT.h[0:64, t0:t0 + PW]
                sn = sinT.h[0:64, t0:t0 + PW]
                ps0 = bankM.h[0:64, 0:PW]
                ps1 = bankM.h[64:128, 0:PW]
                p.tt("vector", ropea.h[0:64, 0:PW], ps0, cs, ALU.mult, r=[bankM, cosT], w=[ropea])
                p.tt("vector", ropea.h[0:64, PW:2 * PW], ps1, sn, ALU.mult, r=[bankM, sinT], w=[ropea])
                p.tt("vector", ropea.h[0:64, 2 * PW:3 * PW], ps1, cs, ALU.mult, r=[bankM, cosT], w=[ropea])
                p.tt("vector", ropea.h[0:64, 3 * PW:4 * PW], ps0, sn, ALU.mult, r=[bankM, sinT], w=[ropea])
                qo = qf2.h[0:64, (hc % 2) * 2 * PW:((hc % 2) + 1) * 2 * PW]
                p.tt("gpsimd", qo[0:64, 0:PW], ropea.h[0:64, 0:PW], ropea.h[0:64, PW:2 * PW],
                     ALU.subtract, r=[ropea], w=[qf2])
                p.tt("gpsimd", qo[0:64, PW:2 * PW], ropea.h[0:64, 2 * PW:3 * PW],
                     ropea.h[0:64, 3 * PW:4 * PW], ALU.add, r=[ropea], w=[qf2])
                p.dma("sync",
                      dram.h.ap()[hc * 128:(hc + 1) * 128, t0:t0 + PW].rearrange(
                          "(half p) t -> p half t", p=64),
                      qo.rearrange("p (half t) -> p half t", t=PW),
                      r=[qf2, dram], w=[dram])

        for wT, dram in ((W["wvT"], S["vtD"]), (W["wgT"], S["gtD"])):
            for eg in range(4):
                wp = wpan[eg % 2]
                p.dma("gpsimd", wp.h[:], wT.h.ap()[eg * 128:(eg + 1) * 128, :],
                      r=[wT], w=[wp])
                for tt_ in range(4):
                    pairs = [(xb.h[:, c * PW + tt_ * 128:c * PW + (tt_ + 1) * 128],
                              wp.h[:, c * 512:(c + 1) * 512]) for c in range(16)]
                    p.mm(bankV.h[:, 0:512], pairs, r=[xb, wp], w=[bankV])
                    ot = otile.h[:, (tt_ % 2) * PW:((tt_ % 2) + 1) * PW]
                    p.copy("scalar", ot, bankV.h[:, 0:512], r=[bankV], w=[otile])
                    p.dma("sync", dram.h.ap()[t0 + tt_ * 128:t0 + (tt_ + 1) * 128, eg * 512:(eg + 1) * 512],
                          ot, r=[otile, dram], w=[dram])

        pairs = [(ws1sb.h[:, c * 16:(c + 1) * 16], xb.h[:, c * PW:(c + 1) * PW]) for c in range(16)]
        p.mm(bankS2.h[0:16, 0:PW], pairs, r=[ws1sb, xb], w=[bankS2])
        p.copy("vector", s1bf.h[:], bankS2.h[0:16, 0:PW], r=[bankS2], w=[s1bf])
        for ch in range(8):
            p.mm(bankS2.h[:, 0:PW], [(ws2sb.h[:, ch * 128:(ch + 1) * 128], s1bf.h[:])],
                 r=[ws2sb, s1bf], w=[bankS2])
            p.ts("vector", tmp_f32.h[:, 0:PW], bankS2.h[:, 0:PW], C["bs2c"].h[:, ch:ch + 1], 32.0,
                 ALU.add, ALU.min, r=[bankS2, C["bs2c"]], w=[tmp_f32])
            p.ts("vector", tmp_f32.h[:, 0:PW], tmp_f32.h[:, 0:PW], -32.0, None, ALU.max, None,
                 r=[tmp_f32], w=[tmp_f32])
            p.act(es32.h[:, ch * 2048 + t0:ch * 2048 + t0 + PW], tmp_f32.h[:, 0:PW], AF.Exp,
                  r=[tmp_f32], w=[es32])

    for ch in range(8):
        sl = es32.h[:, ch * 2048:(ch + 1) * 2048]
        p.op("vector", lambda E, sl=sl: E.tensor_tensor_scan(cz.h[:], sl, sl, 0.0,
                                                             op0=ALU.add, op1=ALU.max),
             r=[es32], w=[cz])
        p.copy("gpsimd", esbf.h[:], sl, r=[es32], w=[esbf])
        p.dma("sync", S["esfD"].h.ap()[ch * 128:(ch + 1) * 128, :], esbf.h[:],
              r=[esbf, S["esfD"]], w=[S["esfD"]])
        p.act(sl, cz.h[:], AF.Ln, r=[cz], w=[es32])
        p.act(cz.h[:], sl, AF.Exp, scale=-1.0, r=[es32], w=[cz])
        for tt_ in range(16):
            p.transpose(bankT.h[:, 0:128], esbf.h[:, tt_ * 128:(tt_ + 1) * 128], C["ident"].h[:],
                        r=[esbf, C["ident"]], w=[bankT])
            p.copy("scalar", estT.h[:, tt_ * 128:(tt_ + 1) * 128], bankT.h[:, 0:128],
                   r=[bankT], w=[estT])
        p.dma("sync", S["estD"].h.ap()[:, ch * 128:(ch + 1) * 128].rearrange("(tt p) m -> p tt m", p=128),
              estT.h[:], r=[estT, S["estD"]], w=[S["estD"]])
        p.copy("gpsimd", esbf.h[:], cz.h[:], r=[cz], w=[esbf])
        p.dma("sync", S["enfD"].h.ap()[ch * 128:(ch + 1) * 128, :], esbf.h[:],
              r=[esbf, S["enfD"]], w=[S["enfD"]])
        for tt_ in range(16):
            p.transpose(bankT.h[:, 0:128], esbf.h[:, tt_ * 128:(tt_ + 1) * 128], C["ident"].h[:],
                        r=[esbf, C["ident"]], w=[bankT])
            p.copy("scalar", estT.h[:, tt_ * 128:(tt_ + 1) * 128], bankT.h[:, 0:128],
                   r=[bankT], w=[estT])
        p.dma("sync", S["entD"].h.ap()[:, ch * 128:(ch + 1) * 128].rearrange("(tt p) m -> p tt m", p=128),
              estT.h[:], r=[estT, S["entD"]], w=[S["entD"]])

    p.fence()
    st.close()



def phase_l1_attn_core(p, C, W, x_in, x_out, S):
    st = ExitStack()
    qp = p.sb("C2_qp", [128, 16 * 512], BF16, st)
    kblk = p.sb("C2_kblk", [128, 16 * 512], BF16, st)
    estm = p.sb("C2_estm", [128, 4 * 1024], BF16, st)
    esfm = p.sb("C2_esfm", [128, 8 * 512], BF16, st)
    vt = p.sb("C2_vt", [128, 4 * 2048], BF16, st)
    enz = p.sb("C2_enz", [128, 4 * 1024], BF16, st)
    ok_sb = p.sb("C2_ok", [128, 4 * 1024], F32, st)
    tmpa = p.sb("C2_tmpa", [128, 3 * 1024], F32, st)
    qvzT = p.sb("C2_qvzT", [128, 8 * 512], BF16, st)
    o_sb = p.sb("C2_o", [128, 4 * 2048], F32, st)
    gsl = A_sb = None
    cols = p.sb("C2_cols", [128, 4 * 16], F32, st)
    gnw_bc = p.sb("C2_gnw", [128, 2048], BF16, st)
    A_sb = p.sb("C2_A", [128, 4 * 512], BF16, st)
    wbuf = [p.sb(f"C2_w{i}", [128, 2048], BF16, st) for i in range(2)]
    a_bc = p.sb("C2_abc", [128, PW], F32, st)
    b_bc = p.sb("C2_bbc", [128, PW], F32, st)
    xsq = p.sb("C2_xsq", [128, 2 * PW], BF16, st)
    ln = LNStats(p, st, "C2_ln")
    ysb = kblk
    t_rot = [(o_sb, 0), (o_sb, 2 * PW)]
    load_const_tiles(p, C, W, st, ["cmask", "gnw16"])

    bankA, bankOK, bankW2, bankO, bankS, bankQ2, bankBC = p.banks
    bankT = p.bankT

    p.dma("sync", x_out.h.ap()[:, :], x_in.h.ap()[:, :], r=[x_in], w=[x_out])
    p.copy("gpsimd", ln.sbf(0, 2048).rearrange("o (a b) -> o a b", b=PW), C["gnw16"].h[0:1, :].rearrange("o (a b) -> o a b", b=PW),
           r=[C["gnw16"]], w=[ln.rows_bf])
    for q4 in range(4):
        p.mm(bankBC.h[:, 0:PW], [(C["ones_row"].h[0:1, 0:128], ln.rows_bf.h[0:1, q4 * PW:(q4 + 1) * PW])],
             r=[C["ones_row"], ln.rows_bf], w=[bankBC])
        p.copy("scalar", gnw_bc.h[:, q4 * PW:(q4 + 1) * PW], bankBC.h[:, 0:PW], r=[bankBC], w=[gnw_bc])

    for tqp in range(NP):
        t0 = tqp * PW
        p.dma("gpsimd", qp.h[:],
              S["qfD"].h.ap()[:, t0:t0 + PW].rearrange("(h p) t -> p h t", p=128),
              r=[S["qfD"]], w=[qp])
        p.dma("gpsimd", enz.h[:],
              S["entD"].h.ap()[t0:t0 + PW, :].rearrange("(s p) m -> p s m", p=128),
              r=[S["entD"]], w=[enz])
        nblk = tqp + 1
        for kb in range(nblk):
            k0 = kb * PW
            p.dma("gpsimd", kblk.h[:],
                  S["kfD"].h.ap()[:, k0:k0 + PW].rearrange("(h p) t -> p h t", p=128),
                  r=[S["kfD"]], w=[kblk])
            p.dma("gpsimd", estm.h[:],
                  S["estD"].h.ap()[k0:k0 + PW, :].rearrange("(j p) m -> p j m", p=128),
                  r=[S["estD"]], w=[estm])
            for h in range(NH):
                kh = kblk.h[:, h * 512:(h + 1) * 512]
                qh = qp.h[:, h * 512:(h + 1) * 512]
                for tkt in range(4):
                    ps = (bankA, bankW2)[tkt % 2]
                    p.mm(ps.h[:, 0:512], [(kh[:, tkt * 128:(tkt + 1) * 128], qh)],
                         r=[kblk, qp], w=[ps])
                    asl = A_sb.h[:, tkt * 512:(tkt + 1) * 512]
                    if kb == tqp:
                        p.tt("vector", asl, ps.h[:, 0:512],
                             C["cmask"].h[:, tkt * 512:(tkt + 1) * 512], ALU.mult,
                             r=[ps, C["cmask"]], w=[A_sb])
                    else:
                        p.copy("scalar", asl, ps.h[:, 0:512], r=[ps], w=[A_sb])
                for sub in range(4):
                    prs = [(A_sb.h[:, tkt * 512 + sub * 128:tkt * 512 + (sub + 1) * 128],
                            estm.h[:, tkt * 1024 + h * 64:tkt * 1024 + (h + 1) * 64])
                           for tkt in range(4)]
                    p.mm_acc(bankOK.h[:, sub * 64:(sub + 1) * 64], prs, r=[A_sb, estm], w=[bankOK],
                             start=(True), stop=(True))
                okv = ok_sb.h[:, :].rearrange("p (s q) -> p s q", q=1024)[:, :, h * 64:(h + 1) * 64]
                psv = bankOK.h[:, 0:256].rearrange("p (s m) -> p s m", m=64)
                if kb == 0:
                    p.copy("vector", okv, psv, r=[bankOK], w=[ok_sb])
                else:
                    p.tt("vector", okv, okv, psv, ALU.add, r=[bankOK, ok_sb], w=[ok_sb])
        for sub in range(4):
            oks = ok_sb.h[:, sub * 1024:(sub + 1) * 1024]
            u0 = tmpa.h[:, 0:1024]
            p.tt("vector", u0, oks, enz.h[:, sub * 1024:(sub + 1) * 1024], ALU.mult,
                 r=[ok_sb, enz], w=[tmpa])
            u03 = u0.rearrange("p (h m) -> p h m", m=64)
            nmax = cols.h[:, 0:16]
            p.op("vector", lambda E, a=nmax, b=u03: E.reduce_max(a, b, axis=AXX, negate=True),
                 r=[tmpa], w=[cols])
            t1 = tmpa.h[:, 1024:2048]
            nb3 = cols.h[:, 0:16].rearrange("p (h o) -> p h o", o=1).broadcast_to([128, 16, 64])
            p.tt("vector", t1.rearrange("p (h m) -> p h m", m=64), u03, nb3, ALU.add,
                 r=[tmpa, cols], w=[tmpa])
            t2 = tmpa.h[:, 2048:3072]
            p.act(t2, t1, AF.Exp, r=[tmpa], w=[tmpa])
            ssum = cols.h[:, 16:32]
            p.op("vector", lambda E, a=ssum, b=t2.rearrange("p (h m) -> p h m", m=64):
                 E.reduce_sum(a, b, axis=AXX), r=[tmpa], w=[cols])
            rcp = cols.h[:, 32:48]
            p.op("vector", lambda E, a=rcp, b=ssum: E.reciprocal(a, b), r=[cols], w=[cols])
            rb3 = rcp.rearrange("p (h o) -> p h o", o=1).broadcast_to([128, 16, 64])
            t3 = tmpa.h[:, 0:1024]
            p.tt("vector", t3.rearrange("p (h m) -> p h m", m=64),
                 t2.rearrange("p (h m) -> p h m", m=64), rb3, ALU.mult, r=[tmpa, cols], w=[tmpa])
            qvz = A_sb.h[:, 1024:2048]
            p.tt("vector", qvz, t3, enz.h[:, sub * 1024:(sub + 1) * 1024], ALU.mult,
                 r=[tmpa, enz], w=[A_sb])
            for pr in range(8):
                p.transpose(bankT.h[:, 0:128], qvz[:, pr * 128:(pr + 1) * 128], C["ident"].h[:],
                            r=[A_sb, C["ident"]], w=[bankT])
                p.copy("scalar", qvzT.h[:, pr * 512 + sub * 128:pr * 512 + (sub + 1) * 128],
                       bankT.h[:, 0:128], r=[bankT], w=[qvzT])
        for kb in range(nblk):
            k0 = kb * PW
            p.dma("gpsimd", esfm.h[:],
                  S["esfD"].h.ap()[:, k0:k0 + PW].rearrange("(ch p) t -> p ch t", p=128),
                  r=[S["esfD"]], w=[esfm])
            p.dma("gpsimd", vt.h[:],
                  S["vtD"].h.ap()[k0:k0 + PW, :].rearrange("(j p) e -> p j e", p=128),
                  r=[S["vtD"]], w=[vt])
            for h in range(NH):
                pb = 64 * (h % 2)
                for tkt in range(4):
                    ps = (bankA, bankW2)[tkt % 2]
                    lh = esfm.h[pb:pb + 64, (h // 2) * 512 + tkt * 128:(h // 2) * 512 + (tkt + 1) * 128]
                    rh = qvzT.h[pb:pb + 64, (h // 2) * 512:(h // 2 + 1) * 512]
                    p.mm(ps.h[:, 0:512], [(lh, rh)], r=[esfm, qvzT], w=[ps])
                    asl = A_sb.h[:, tkt * 512:(tkt + 1) * 512]
                    if kb == tqp:
                        p.tt("vector", asl, ps.h[:, 0:512],
                             C["cmask"].h[:, tkt * 512:(tkt + 1) * 512], ALU.mult,
                             r=[ps, C["cmask"]], w=[A_sb])
                    else:
                        p.copy("scalar", asl, ps.h[:, 0:512], r=[ps], w=[A_sb])
                for sub in range(4):
                    prs = [(A_sb.h[:, tkt * 512 + sub * 128:tkt * 512 + (sub + 1) * 128],
                            vt.h[:, tkt * 2048 + h * 128:tkt * 2048 + (h + 1) * 128])
                           for tkt in range(4)]
                    p.mm_acc(bankO.h[:, sub * 128:(sub + 1) * 128], prs, r=[A_sb, vt], w=[bankO],
                             start=True, stop=True)
                ov = o_sb.h[:, :].rearrange("p (s q) -> p s q", q=2048)[:, :, h * 128:(h + 1) * 128]
                psv = bankO.h[:, 0:512].rearrange("p (s e) -> p s e", e=128)
                if kb == 0:
                    p.copy("vector", ov, psv, r=[bankO], w=[o_sb])
                else:
                    p.tt("vector", ov, ov, psv, ALU.add, r=[bankO, o_sb], w=[o_sb])
        for sub in range(4):
            osl = o_sb.h[:, sub * 2048:(sub + 1) * 2048]
            sq = tmpa.h[:, 0:2048]
            p.act(sq, osl, AF.Square, r=[o_sb], w=[tmpa])
            ss = cols.h[:, 0:16]
            p.op("vector", lambda E, a=ss, b=sq.rearrange("p (h e) -> p h e", e=128):
                 E.reduce_sum(a, b, axis=AXX), r=[tmpa], w=[cols])
            p.ts("vector", ss, ss, 1.0 / 128.0, None, ALU.mult, None, r=[cols], w=[cols])
            p.memset("vector", cols.h[:, 48:49], 1e-5, w=[cols])
            ss2 = cols.h[:, 16:32]
            p.act(ss2, ss, AF.Ln, bias=cols.h[:, 48:49], r=[cols], w=[cols])
            p.act(ss, ss2, AF.Exp, scale=-0.5, r=[cols], w=[cols])
            rb3 = ss.rearrange("p (h o) -> p h o", o=1).broadcast_to([128, 16, 128])
            t1 = tmpa.h[:, 1024:3072]
            p.tt("vector", t1.rearrange("p (h e) -> p h e", e=128),
                 osl.rearrange("p (h e) -> p h e", e=128), rb3, ALU.mult, r=[o_sb, cols], w=[tmpa])
            p.tt("gpsimd", t1, t1, gnw_bc.h[:], ALU.mult, r=[tmpa, gnw_bc], w=[tmpa])
            p.dma("gpsimd", enz.h[:, 0:2048],
                  S["gtD"].h.ap()[t0 + sub * 128:t0 + (sub + 1) * 128, :], r=[S["gtD"]], w=[enz])
            gsl = A_sb.h[:, 0:2048]
            p.act(gsl, enz.h[:, 0:2048], AF.Silu, r=[enz], w=[A_sb])
            og = vt.h[:, 0:2048]
            p.tt("vector", og, t1, gsl, ALU.mult, r=[tmpa, A_sb], w=[vt])
            for h in range(NH):
                p.transpose(bankT.h[:, 0:128], og[:, h * 128:(h + 1) * 128], C["ident"].h[:],
                            r=[vt, C["ident"]], w=[bankT])
                p.copy("scalar", qp.h[:, h * 512 + sub * 128:h * 512 + (sub + 1) * 128],
                       bankT.h[:, 0:128], r=[bankT], w=[qp])
        for oc in range(16):
            wb = wbuf[oc % 2]
            p.dma("gpsimd", wb.h[:], W["woT"].h.ap()[oc * 128:(oc + 1) * 128, :],
                  r=[W["woT"]], w=[wb])
            pairs = [(wb.h[:, c * 128:(c + 1) * 128], qp.h[:, c * 512:(c + 1) * 512]) for c in range(16)]
            p.mm(bankA.h[:, 0:512], pairs, r=[wb, qp], w=[bankA])
            p.copy("scalar", ysb.h[:, oc * PW:(oc + 1) * PW], bankA.h[:, 0:512], r=[bankA], w=[ysb])
        emit_post_ln_residual(p, C, ln, ysb, xsq, x_out, x_out.h.ap(), tqp,
                              a_bc, b_bc, bankS, bankQ2, bankBC, t_rot)

    p.fence()
    st.close()




HID = [1024, 2048, 3072, 4096, 1024, 2048, 3072, 4096]


def phase_l1_ffn(p, C, W, x_in, x_out):
    st = ExitStack()
    xb = p.sb("D_xb", [128, 16 * PW], BF16, st)
    xsq = p.sb("D_xsq", [128, 2 * PW], BF16, st)
    gxn = p.sb("D_gxn", [128, 16 * PW], BF16, st)
    gh = p.sb("D_gh", [128, 8 * PW], BF16, st)
    gmid = p.sb("D_gmid", [128, 32 * PW], BF16, st)
    acc = p.sb("D_acc", [128, 8 * PW], F32, st)
    accbf = gmid
    ysb = xb
    lg = p.sb("D_lg", [8, PW], F32, st)
    lgbf = p.sb("D_lgbf", [8, PW], BF16, st)
    lt = p.sb("D_lt", [128, 8], F32, st)
    cols = p.sb("D_cols", [128, 16], F32, st)
    pr8 = p.sb("D_pr8", [128, 3 * 8], F32, st)
    wexpT = p.sb("D_wexpT", [8, PW], BF16, st)
    w_bc = p.sb("D_wbc", [128, PW], BF16, st)
    crows = p.sb("D_crows", [1, 8 * 1024], BF16, st)
    b2row = p.sb("D_b2row", [1, 1024], BF16, st)
    ctot = p.sb("D_ctot", [1, 1024], F32, st)
    ctbf = p.sb("D_ctbf", [1, 1024], BF16, st)
    ctcols = p.sb("D_ctcols", [128, 8], F32, st)
    gb1 = p.sb("D_gb1", [128, 32], BF16, st)
    onespw = p.sb("D_onespw", [1, PW], BF16, st)
    w1buf = [p.sb(f"D_w1{i}", [128, 8 * 128], BF16, st) for i in range(2)]
    w2buf = [p.sb(f"D_w2{i}", [128, 32 * 128], BF16, st) for i in range(2)]
    wdbuf = [p.sb(f"D_wd{i}", [128, 16 * 128], BF16, st) for i in range(2)]
    rwsb = p.sb("D_rw", [128, 16 * 8], BF16, st)
    tmpf = p.sb("D_tmpf", [128, PW], F32, st)
    a_bc = p.sb("D_abc", [128, PW], F32, st)
    b_bc = p.sb("D_bbc", [128, PW], F32, st)
    t_rot = [(p.sb(f"D_t{i}", [128, 2 * PW], F32, st), 0) for i in range(2)]
    ln = LNStats(p, st, "D_ln")
    C["tmp_f32"] = tmpf
    load_const_tiles(p, C, W, st, ["dn_bc", "up_bc", "rt_bc", "lcost", "sh_b1c", "sh_b2c",
                                   "co_bc", "sel8"] + [f"e{e}_b1c" for e in range(8)])

    bankH, bankMID, bankF, bankR, bankS, bankQ2, bankBC = p.banks
    bankT = p.bankT

    p.dma("sync", x_out.h.ap()[:, :], x_in.h.ap()[:, :], r=[x_in], w=[x_out])
    p.memset("vector", onespw.h[:], 1.0, w=[onespw])
    p.memset("vector", ctot.h[:], 0.0, w=[ctot])
    p.dma("gpsimd", rwsb.h[:], W["rt_w"].h.ap()[:, :].rearrange("(c p) e -> p c e", p=128),
          r=[W["rt_w"]], w=[rwsb])

    for panel in range(NP):
        t0 = panel * PW
        p.dma("gpsimd", xb.h[:],
              x_in.h.ap()[:, t0:t0 + PW].rearrange("(c p) t -> p c t", p=128),
              r=[x_in], w=[xb])
        emit_ln_stats(p, C, ln, xb, xsq, 16, 1e-5, bankS, bankQ2)
        emit_xn(p, C, ln, xb, a_bc, b_bc, bankBC, 16)
        for c in range(16):
            p.act(gxn.h[:, c * PW:(c + 1) * PW], xb.h[:, c * PW:(c + 1) * PW], AF.Gelu,
                  r=[xb], w=[gxn])

        for lc in range(8):
            wd = wdbuf[lc % 2]
            p.dma("gpsimd", wd.h[:, 0:2048], W["dn_w"].h.ap()[lc * 128:(lc + 1) * 128, :],
                  r=[W["dn_w"]], w=[wd])
            pairs = [(wd.h[:, c * 128:(c + 1) * 128], xb.h[:, c * PW:(c + 1) * PW]) for c in range(16)]
            p.mm(bankH.h[:, 0:PW], pairs, r=[wd, xb], w=[bankH])
            p.act(gh.h[:, lc * PW:(lc + 1) * PW], bankH.h[:, 0:PW], AF.Gelu,
                  bias=C["dn_bc"].h[:, lc:lc + 1], r=[bankH, C["dn_bc"]], w=[gh])

        pairs = [(rwsb.h[:, c * 8:(c + 1) * 8], xb.h[:, c * PW:(c + 1) * PW]) for c in range(16)]
        p.mm(bankH.h[0:8, 0:PW], pairs, r=[rwsb, xb], w=[bankH])
        p.ts("vector", lg.h[:], bankH.h[0:8, 0:PW], C["rt_bc"].h[0:8, 0:1], C["lcost"].h[0:8, 0:1],
             ALU.add, ALU.subtract, r=[bankH, C["rt_bc"], C["lcost"]], w=[lg])
        p.copy("gpsimd", lgbf.h[:], lg.h[:], r=[lg], w=[lgbf])
        for tt_ in range(4):
            p.transpose(bankT.h[:, 0:8], lgbf.h[:, tt_ * 128:(tt_ + 1) * 128], C["ident"].h[0:8, 0:8],
                        r=[lgbf, C["ident"]], w=[bankT])
            p.copy("scalar", lt.h[:], bankT.h[:, 0:8], r=[bankT], w=[lt])
            nm = cols.h[:, 0:1]
            p.op("vector", lambda E, a=nm: E.reduce_max(a, lt.h[:], axis=AXX, negate=True),
                 r=[lt], w=[cols])
            ex = pr8.h[:, 0:8]
            p.act(ex, lt.h[:], AF.Exp, bias=nm, r=[lt, cols], w=[pr8])
            sm = cols.h[:, 1:2]
            p.op("vector", lambda E, a=sm, b=ex: E.reduce_sum(a, b, axis=AXX), r=[pr8], w=[cols])
            rc = cols.h[:, 2:3]
            p.op("vector", lambda E, a=rc, b=sm: E.reciprocal(a, b), r=[cols], w=[cols])
            probs = pr8.h[:, 0:8]
            p.ts("vector", probs, probs, rc, None, ALU.mult, None, r=[cols, pr8], w=[pr8])
            m1 = cols.h[:, 3:4]
            p.op("vector", lambda E, a=m1, b=probs: E.reduce_max(a, b, axis=AXX), r=[pr8], w=[cols])
            mask1 = pr8.h[:, 8:16]
            p.ts("vector", mask1, probs, m1, None, ALU.is_equal, None, r=[pr8, cols], w=[pr8])
            p2v = pr8.h[:, 16:24]
            p.stt("vector", p2v, mask1, -1e9, probs, ALU.mult, ALU.add, r=[pr8], w=[pr8])
            m2 = cols.h[:, 4:5]
            p.op("vector", lambda E, a=m2, b=p2v: E.reduce_max(a, b, axis=AXX), r=[pr8], w=[cols])
            mask2 = pr8.h[:, 16:24]
            p.ts("vector", mask2, p2v, m2, None, ALU.is_equal, None, r=[pr8, cols], w=[pr8])
            e1c = cols.h[:, 5:6]
            e2c = cols.h[:, 6:7]
            p.act(e1c, m1, AF.Exp, r=[cols], w=[cols])
            p.act(e2c, m2, AF.Exp, r=[cols], w=[cols])
            dn = cols.h[:, 7:8]
            p.tt("vector", dn, e1c, e2c, ALU.add, r=[cols], w=[cols])
            rd = cols.h[:, 8:9]
            p.op("vector", lambda E, a=rd, b=dn: E.reciprocal(a, b), r=[cols], w=[cols])
            g1 = cols.h[:, 9:10]
            g2 = cols.h[:, 10:11]
            p.tt("vector", g1, e1c, rd, ALU.mult, r=[cols], w=[cols])
            p.tt("vector", g2, e2c, rd, ALU.mult, r=[cols], w=[cols])
            wex = pr8.h[:, 8:16]
            p.ts("vector", wex, mask1, g1, None, ALU.mult, None, r=[pr8, cols], w=[pr8])
            p.stt("vector", wex, mask2, g2, wex, ALU.mult, ALU.add, r=[pr8, cols], w=[pr8])
            p.copy("gpsimd", gb1.h[:, 0:8], wex, r=[pr8], w=[gb1])
            p.transpose(bankT.h[0:8, 8:136], gb1.h[:, 0:8], C["ident"].h[:], r=[gb1, C["ident"]], w=[bankT])
            p.copy("scalar", wexpT.h[0:8, tt_ * 128:(tt_ + 1) * 128], bankT.h[0:8, 8:136],
                   r=[bankT], w=[wexpT])

        if panel == 0:
            p.dbg("gh", gh.h[:], gh, (128, 8 * PW))
            p.dbg("lg", lg.h[:], lg, (8, PW))
            p.dbg("wexpT", wexpT.h[0:8, :], wexpT, (8, PW))
        for e in range(8):
            ne = HID[e] // 128
            p.mm(bankR.h[:, 0:PW], [(C["sel8"].h[0:8, e * 128:(e + 1) * 128], wexpT.h[0:8, 0:PW])],
                 r=[C["sel8"], wexpT], w=[bankR])
            p.copy("scalar", w_bc.h[:], bankR.h[:, 0:PW], r=[bankR], w=[w_bc])
            if panel == 0 and e == 0:
                p.dbg("wbc0", w_bc.h[:], w_bc, (128, PW))
            if panel == 0:
                p.act(gb1.h[:, 0:ne], C[f"e{e}_b1c"].h[:, 0:ne], AF.Gelu,
                      r=[C[f"e{e}_b1c"]], w=[gb1])
                p.dma("gpsimd", b2row.h[0:1, :], W[f"e{e}_b2r"].h.ap()[0:1, :],
                      r=[W[f"e{e}_b2r"]], w=[b2row])
            for j in range(ne):
                wb1 = w1buf[j % 2]
                p.dma("gpsimd", wb1.h[:, 0:1024],
                      W[f"e{e}_w1"].h.ap()[j * 128:(j + 1) * 128, :],
                      r=[W[f"e{e}_w1"]], w=[wb1])
                pairs = [(wb1.h[:, lc * 128:(lc + 1) * 128], gh.h[:, lc * PW:(lc + 1) * PW])
                         for lc in range(8)]
                p.mm(bankMID.h[:, 0:PW], pairs, r=[wb1, gh], w=[bankMID])
                p.act(gmid.h[:, j * PW:(j + 1) * PW], bankMID.h[:, 0:PW], AF.Gelu,
                      bias=C[f"e{e}_b1c"].h[:, j:j + 1], r=[bankMID, C[f"e{e}_b1c"]], w=[gmid])
            if panel == 0 and e == 0:
                p.dbg("gmid0", gmid.h[:, 0:8 * PW], gmid, (128, 8 * PW))
            for lc in range(8):
                wb2 = w2buf[lc % 2]
                p.dma("gpsimd", wb2.h[:, 0:ne * 128],
                      W[f"e{e}_w2"].h.ap()[lc * 128:(lc + 1) * 128, :],
                      r=[W[f"e{e}_w2"]], w=[wb2])
                if panel == 0:
                    prs = [(gb1.h[:, i:i + 1], wb2.h[:, i * 128:(i + 1) * 128]) for i in range(ne)]
                    p.mm(bankR.h[0:1, 0:128], prs, r=[gb1, wb2], w=[bankR])
                    p.ts("vector", crows.h[0:1, e * 1024 + lc * 128:e * 1024 + (lc + 1) * 128],
                         bankR.h[0:1, 0:128], -1.0, None, ALU.mult, None,
                         r=[bankR], w=[crows])
                    p.tt("vector", ctot.h[0:1, lc * 128:(lc + 1) * 128],
                         ctot.h[0:1, lc * 128:(lc + 1) * 128], bankR.h[0:1, 0:128], ALU.add,
                         r=[bankR, ctot], w=[ctot])
                    p.tt("vector", ctot.h[0:1, lc * 128:(lc + 1) * 128],
                         ctot.h[0:1, lc * 128:(lc + 1) * 128], b2row.h[0:1, lc * 128:(lc + 1) * 128],
                         ALU.add, r=[b2row, ctot], w=[ctot])
                pairs = [(wb2.h[:, i * 128:(i + 1) * 128], gmid.h[:, i * PW:(i + 1) * PW])
                         for i in range(ne)]
                pairs.append((crows.h[0:1, e * 1024 + lc * 128:e * 1024 + (lc + 1) * 128], onespw.h[0:1, :]))
                p.mm(bankF.h[:, 0:PW], pairs, r=[wb2, gmid, crows, onespw], w=[bankF])
                p.tt("vector", tmpf.h[:], bankF.h[:, 0:PW], w_bc.h[:], ALU.mult,
                     r=[bankF, w_bc], w=[tmpf])
                if panel == 0 and e == 0:
                    p.dbg(f"f0_{lc}", tmpf.h[:], tmpf, (128, PW))
                asl = acc.h[:, lc * PW:(lc + 1) * PW]
                if e == 0:
                    p.copy("vector", asl, tmpf.h[:], r=[tmpf], w=[acc])
                else:
                    p.tt("vector", asl, asl, tmpf.h[:], ALU.add, r=[tmpf, acc], w=[acc])
        if panel == 0:
            p.dbg("acc0", acc.h[:], acc, (128, 8 * PW))
            p.dbg("crows", crows.h[0:1, :], crows, (1, 8 * 1024))
            p.copy("gpsimd", ctbf.h[:], ctot.h[:], r=[ctot], w=[ctbf])
            for lc in range(8):
                p.mm(bankR.h[:, lc:lc + 1],
                     [(ctbf.h[0:1, lc * 128:(lc + 1) * 128], onespw.h[0:1, 0:1])],
                     r=[ctbf, onespw], w=[bankR])
            p.copy("vector", ctcols.h[:], bankR.h[:, 0:8], r=[bankR], w=[ctcols])

        for sc in range(8):
            wb1 = w1buf[sc % 2]
            p.dma("gpsimd", wb1.h[:, 0:1024],
                  W["sh_w1"].h.ap()[sc * 128:(sc + 1) * 128, :],
                  r=[W["sh_w1"]], w=[wb1])
            pairs = [(wb1.h[:, lc * 128:(lc + 1) * 128], gh.h[:, lc * PW:(lc + 1) * PW])
                     for lc in range(8)]
            p.mm(bankMID.h[:, 0:PW], pairs, r=[wb1, gh], w=[bankMID])
            p.act(gmid.h[:, sc * PW:(sc + 1) * PW], bankMID.h[:, 0:PW], AF.Gelu,
                  bias=C["sh_b1c"].h[:, sc:sc + 1], r=[bankMID, C["sh_b1c"]], w=[gmid])
        for lc in range(8):
            wb1 = w1buf[lc % 2]
            p.dma("gpsimd", wb1.h[:, 0:1024],
                  W["sh_w2"].h.ap()[lc * 128:(lc + 1) * 128, :],
                  r=[W["sh_w2"]], w=[wb1])
            pairs = [(wb1.h[:, sc * 128:(sc + 1) * 128], gmid.h[:, sc * PW:(sc + 1) * PW])
                     for sc in range(8)]
            p.mm(bankF.h[:, 0:PW], pairs, r=[wb1, gmid], w=[bankF])
            p.ts("vector", tmpf.h[:], bankF.h[:, 0:PW], C["sh_b2c"].h[:, lc:lc + 1], 0.1,
                 ALU.add, ALU.mult, r=[bankF, C["sh_b2c"]], w=[tmpf])
            asl = acc.h[:, lc * PW:(lc + 1) * PW]
            p.tt("vector", asl, asl, tmpf.h[:], ALU.add, r=[tmpf, acc], w=[acc])

        for lc in range(8):
            p.ts("vector", accbf.h[:, (8 + lc) * PW:(9 + lc) * PW], acc.h[:, lc * PW:(lc + 1) * PW],
                 ctcols.h[:, lc:lc + 1], None, ALU.add, None, r=[acc, ctcols], w=[accbf])

        for oc in range(16):
            wup = w1buf[oc % 2]
            wco = wdbuf[oc % 2]
            p.dma("gpsimd", wup.h[:, 0:1024],
                  W["up_w"].h.ap()[oc * 128:(oc + 1) * 128, :],
                  r=[W["up_w"]], w=[wup])
            p.dma("gpsimd", wco.h[:, 0:2048],
                  W["co_w"].h.ap()[oc * 128:(oc + 1) * 128, :],
                  r=[W["co_w"]], w=[wco])
            pairs = [(wup.h[:, lc * 128:(lc + 1) * 128], accbf.h[:, (8 + lc) * PW:(9 + lc) * PW])
                     for lc in range(8)]
            pairs += [(wco.h[:, c * 128:(c + 1) * 128], gxn.h[:, c * PW:(c + 1) * PW])
                      for c in range(16)]
            p.mm(bankH.h[:, 0:PW], pairs, r=[wup, wco, accbf, gxn], w=[bankH])
            p.ts("vector", tmpf.h[:], bankH.h[:, 0:PW], C["up_bc"].h[:, oc:oc + 1],
                 C["co_bc"].h[:, oc:oc + 1], ALU.add, ALU.add, r=[bankH, C["up_bc"], C["co_bc"]],
                 w=[tmpf])
            p.copy("gpsimd", ysb.h[:, oc * PW:(oc + 1) * PW], tmpf.h[:], r=[tmpf], w=[ysb])

        emit_post_ln_residual(p, C, ln, ysb, xsq, x_out, x_out.h.ap(), panel,
                              a_bc, b_bc, bankS, bankQ2, bankBC, t_rot)

    p.fence()
    st.close()


def asc(a):
    return np.ascontiguousarray(a)


def u8_view(int32_mat):
    return asc(asc(int32_mat).view(np.uint8)[:, 0::4])


def tileKN(wT, tile_n=128):
    K, N = wT.shape
    t = wT.reshape(K // 128, 128, N // tile_n, tile_n).transpose(2, 1, 0, 3)
    return asc(t.reshape(N // tile_n * 128, (K // 128) * tile_n))


def host_prep(inp):
    w = {}
    bf = lambda a: a.astype(bfloat16)
    w["wqkvuT"] = tileKN(bf(inp["L0_attn_W_qkvu"].T))
    w["woutT"] = tileKN(bf(inp["L0_attn_W_out"].T))
    w["slope_row"] = asc(inp["L0_attn_slope_rate"].reshape(1, 16))
    w["gateT8"] = tileKN(u8_view(asc(inp["L0_ffn_gate_w"].T)))
    w["upT8"] = tileKN(u8_view(asc(inp["L0_ffn_up_w"].T)))
    w["downT8"] = tileKN(u8_view(asc(inp["L0_ffn_down_w"].T)))
    w["gsc"] = asc(inp["L0_ffn_gate_scale"].reshape(64, 128).T)
    w["usc"] = asc(inp["L0_ffn_up_scale"].reshape(64, 128).T)
    w["dsc"] = asc(inp["L0_ffn_down_scale"].reshape(16, 128).T)
    for nm in ("Wq", "Wk"):
        w[f"w{nm[1:].lower()}T"] = tileKN(bf(inp[f"L1_attn_{nm}"].T))
    for nm in ("Wv", "Wg"):
        w[f"w{nm[1:].lower()}T"] = tileKN(bf(inp[f"L1_attn_{nm}"].T), tile_n=512)
    w["woT"] = tileKN(bf(inp["L1_attn_Wo"].T))
    w["ws1T"] = asc(inp["L1_attn_Ws1"].T)
    w["ws2T"] = asc(inp["L1_attn_Ws2"].T)
    w["bs2c"] = asc(inp["L1_attn_bs2"].reshape(8, 128).T)
    w["gnw16"] = asc(np.broadcast_to(inp["L1_attn_g_norm_weight"], (16, 128)).reshape(1, 2048))
    w["dn_w"] = tileKN(bf(inp["L1_ffn_down_w"]))
    w["dn_bc"] = asc(inp["L1_ffn_down_b"].reshape(8, 128).T)
    w["up_w"] = tileKN(bf(inp["L1_ffn_up_w"]))
    w["up_bc"] = asc(inp["L1_ffn_up_b"].reshape(16, 128).T)
    w["rt_w"] = asc(inp["L1_ffn_router_w"])
    w["rt_bc"] = asc(inp["L1_ffn_router_b"].reshape(8, 1))
    w["sh_w1"] = tileKN(bf(inp["L1_ffn_shared_w1"]))
    w["sh_w2"] = tileKN(bf(inp["L1_ffn_shared_w2"]))
    w["sh_b1c"] = asc(inp["L1_ffn_shared_b1"].reshape(8, 128).T)
    w["sh_b2c"] = asc(inp["L1_ffn_shared_b2"].reshape(8, 128).T)
    w["co_w"] = tileKN(bf(inp["L1_ffn_core_w"]))
    w["co_bc"] = asc(inp["L1_ffn_core_b"].reshape(16, 128).T)
    HID = [1024, 2048, 3072, 4096, 1024, 2048, 3072, 4096]
    for e, hd in enumerate(HID):
        w[f"e{e}_w1"] = tileKN(bf(inp[f"L1_ffn_e{e}_w1"]))
        w[f"e{e}_w2"] = tileKN(bf(inp[f"L1_ffn_e{e}_w2"]))
        w[f"e{e}_b1c"] = asc(inp[f"L1_ffn_e{e}_b1"].reshape(hd // 128, 128).T)
        w[f"e{e}_b2r"] = asc(inp[f"L1_ffn_e{e}_b2"].reshape(1, 1024))
    w["ident"] = np.eye(128, dtype=bfloat16)
    w["arr_row"] = np.arange(1, 257, dtype=np.float32).reshape(1, 256)
    kc = np.zeros((128, 2), np.float32)
    for pp in range(128):
        for j in range(2):
            kc[pp, j] = 256.0 - (j * 128 + pp + 1)
    w["karr_cols"] = kc
    dm = np.zeros((128, 512), np.float32)
    for a in range(128):
        for b in range(256):
            for t in range(2):
                v = b - a - 128 * t
                dm[a, t * 256 + b] = v if v >= 0 else 1e9
    w["Dm"] = dm
    pos = np.arange(T, dtype=np.float64)
    inv_freq = 1.0 / 10000.0 ** (np.arange(0, DH, 2) / DH)
    ang = pos[None, :] * inv_freq[:, None]
    scale = DH ** -0.5
    w["cosq"] = (np.cos(ang) * scale).astype(bfloat16)
    w["sinq"] = (np.sin(ang) * scale).astype(bfloat16)
    w["cosk"] = np.cos(ang).astype(bfloat16)
    w["sink"] = np.sin(ang).astype(bfloat16)
    cm = np.zeros((128, 4 * 512), np.float32)
    for off in range(4):
        for r in range(128):
            for c in range(512):
                cm[r, off * 512 + c] = 1.0 if c >= r + off * 128 else 0.0
    w["cmask"] = cm.astype(bfloat16)
    sel = np.zeros((8, 1024), np.float32)
    for e in range(8):
        sel[e, e * 128:(e + 1) * 128] = 1.0
    w["sel8"] = sel.astype(bfloat16)
    cost = np.array([2 * 1024 * hd for hd in HID], np.float32) * 1e-07
    w["lcost"] = cost.reshape(8, 1)
    return w


W_DTYPES = {
    "gateT8": U8, "upT8": U8, "downT8": U8,
    "wqkvuT": BF16, "woutT": BF16,
    "wqT": BF16, "wkT": BF16, "wvT": BF16, "wgT": BF16, "woT": BF16,
    "dn_w": BF16, "up_w": BF16, "co_w": BF16, "sh_w1": BF16, "sh_w2": BF16,
    "ident": BF16, "cosq": BF16, "sinq": BF16, "cosk": BF16, "sink": BF16,
    "cmask": BF16, "sel8": BF16,
}
W_DTYPES.update({f"e{e}_{nm}": BF16 for e in range(8) for nm in ("w1", "w2")})


def declare_weights(p, names, host_w):
    W = {}
    for n in names:
        arr = host_w[n]
        dt = W_DTYPES.get(n, F32)
        W[n] = p.dram(n, arr.shape, dt, kind="ExternalInput")
    return W


def make_consts(p, C_stack):
    C = {}
    C["ones_col"] = p.sb("c_onescol", [128, 1], BF16, C_stack)
    C["ones_row"] = p.sb("c_onesrow", [1, 128], BF16, C_stack)
    C["ones_pw"] = p.sb("c_onespw", [1, PW], BF16, C_stack)
    C["neg128_row"] = p.sb("c_n128row", [1, 128], BF16, C_stack)
    p.memset("vector", C["ones_col"].h[:], 1.0, w=[C["ones_col"]])
    p.memset("vector", C["ones_row"].h[:], 1.0, w=[C["ones_row"]])
    p.memset("vector", C["ones_pw"].h[:], 1.0, w=[C["ones_pw"]])
    p.memset("vector", C["neg128_row"].h[:], -128.0, w=[C["neg128_row"]])
    return C


def load_const_tiles(p, C, W, C_stack, need):
    shapes = {
        "ident": (128, 128), "Dm": (128, 512), "arr_row": (1, 256), "karr_cols": (128, 2),
        "cosq": (64, 2048), "sinq": (64, 2048), "cosk": (64, 2048), "sink": (64, 2048),
        "cmask": (128, 2048), "sel8": (8, 1024), "gnw16": (1, 2048), "lcost": (8, 1),
        "gsc": (128, 64), "usc": (128, 64), "dsc": (128, 16), "bs2c": (128, 8),
        "dn_bc": (128, 8), "up_bc": (128, 16), "rt_bc": (8, 1),
        "sh_b1c": (128, 8), "sh_b2c": (128, 8), "co_bc": (128, 16),
        "e0_b1c": (128, 8), "e1_b1c": (128, 16), "e2_b1c": (128, 24), "e3_b1c": (128, 32),
        "e4_b1c": (128, 8), "e5_b1c": (128, 16), "e6_b1c": (128, 24), "e7_b1c": (128, 32),
    }
    for n in need:
        if n not in W:
            continue
        sh = shapes[n]
        dt = W_DTYPES.get(n, F32)
        C[n] = p.sb("ct_" + n, list(sh), dt, C_stack)
        p.dma("sync", C[n].h[:], W[n].h.ap()[:, :], r=[W[n]], w=[C[n]])



PHASE_WEIGHTS = {
    "A": ["wqkvuT", "woutT", "slope_row", "ident", "Dm", "arr_row", "karr_cols"],
    "B": ["gateT8", "upT8", "downT8", "gsc", "usc", "dsc"],
    "C1": ["wqT", "wkT", "wvT", "wgT", "ws1T", "ws2T", "bs2c", "ident",
           "cosq", "sinq", "cosk", "sink"],
    "C2": ["woT", "gnw16", "ident", "cmask"],
    "D": ["dn_w", "dn_bc", "up_w", "up_bc", "rt_w", "rt_bc", "lcost", "sh_w1", "sh_w2",
          "sh_b1c", "sh_b2c", "co_w", "co_bc", "sel8", "ident"]
    + [f"e{e}_{nm}" for e in range(8) for nm in ("w1", "w2", "b1c", "b2r")],
}

PHASE_CONSTS = {
    "A": ["ident", "Dm", "arr_row", "karr_cols"],
    "B": ["gsc", "usc", "dsc"],
    "C1": ["ident", "cosq", "sinq", "cosk", "sink", "bs2c"],
    "C2": ["ident", "cmask", "gnw16"],
    "D": ["ident", "dn_bc", "up_bc", "rt_bc", "lcost", "sh_b1c", "sh_b2c", "co_bc", "sel8"]
    + [f"e{e}_b1c" for e in range(8)],
}


def build_program(phases, host_w, dbg_scratch=False, dump_mids=False):
    nc = bass.Bass("TRN2", target_bir_lowering=False)
    stack = ExitStack()
    p = Prog(nc, stack)

    x0 = p.dram("x0T", (D, T), F32, kind="ExternalInput")
    out = p.dram("outT", (D, T), F32, kind="ExternalOutput")
    names = []
    for ph in phases:
        names += PHASE_WEIGHTS[ph]
    W = declare_weights(p, sorted(set(names)), host_w)

    C = make_consts(p, stack)
    load_const_tiles(p, C, W, stack, ["ident"])

    S = {}
    if "C1" in phases or "C2" in phases:
        sk = "ExternalOutput" if dbg_scratch else "Internal"
        S["qfD"] = p.dram("qfD", (2048, 2048), BF16, kind=sk)
        S["kfD"] = p.dram("kfD", (2048, 2048), BF16, kind=sk)
        S["vtD"] = p.dram("vtD", (2048, 2048), BF16, kind=sk)
        S["gtD"] = p.dram("gtD", (2048, 2048), BF16, kind=sk)
        S["esfD"] = p.dram("esfD", (1024, 2048), BF16, kind=sk)
        S["estD"] = p.dram("estD", (2048, 1024), BF16, kind=sk)
        S["enfD"] = p.dram("enfD", (1024, 2048), BF16, kind=sk)
        S["entD"] = p.dram("entD", (2048, 1024), BF16, kind=sk)

    cur = x0
    residual_phases = [ph for ph in phases if ph != "C1"]
    for i, ph in enumerate(phases):
        if ph == "C1":
            phase_l1_attn_proj(p, C, W, cur, S)
            continue
        last = (residual_phases[-1] == ph)
        mk = "ExternalOutput" if dump_mids else "Internal"
        dst = out if last else p.dram(f"xmid{i}", (D, T), F32, kind=mk)
        if ph == "A":
            phase_l0_attn(p, C, W, cur, dst)
        elif ph == "B":
            phase_l0_ffn(p, C, W, cur, dst)
        elif ph == "C2":
            phase_l1_attn_core(p, C, W, cur, dst, S)
        elif ph == "D":
            phase_l1_ffn(p, C, W, cur, dst)
        else:
            raise NotImplementedError(ph)
        cur = dst

    p.fence()
    return nc, stack


PHASES = ["A", "B", "C1", "C2", "D"]


def kernel(**inputs):
    host_w = host_prep(inputs)
    nc, stack = build_program(PHASES, host_w)
    names = sorted({n for ph in PHASES for n in PHASE_WEIGHTS[ph]})
    in_maps = []
    for core in range(NCORES):
        m = {"x0T": asc(inputs["L0_attn_x"][core].T)}
        for n in names:
            m[n] = host_w[n]
        in_maps.append(m)
    res = run_bass_kernel_spmd(nc, in_maps, core_ids=list(range(NCORES)))
    stack.close()
    outs = [r["outT"] for r in res.results]
    return np.stack([o.T for o in outs], axis=0).astype(np.float32)


# version 05 (final)
# speedup vs baseline: 1.2316x; 1.0559x over previous
import numpy as np
import ml_dtypes
from contextlib import ExitStack

import concourse.bass as bass
import concourse.mybir as mybir
from concourse.bass_utils import run_bass_kernel_spmd

F32 = mybir.dt.float32
BF16 = mybir.dt.bfloat16
U8 = mybir.dt.uint8
AF = mybir.ActivationFunctionType
ALU = mybir.AluOpType
AXX = mybir.AxisListType.X
bfloat16 = ml_dtypes.bfloat16

DBG = {}
PARANOIA = 0
DMA_WINDOW = 0
DMA_CHAIN = 1
T = 2048
D = 2048
NCORES = 8
PW = 512
NP = T // PW
NH = 16
DH = 128

ENGS = ("tensor", "vector", "scalar", "gpsimd", "sync")
DMA_Q = {"sync": "sy_d", "gpsimd": "gp_d", "scalar": "sc_d"}


class Buf:
    __slots__ = ("h", "name", "w", "r")

    def __init__(self, h, name):
        self.h = h
        self.name = name
        self.w = None
        self.r = {}


class Prog:
    def __init__(self, nc, stack):
        self.nc = nc
        self.stack = stack
        self.sems = {}
        self.cnt = {}
        for s in ENGS + tuple(DMA_Q.values()):
            if s == "sync":
                continue
            self.sems[s] = stack.enter_context(nc.semaphore("S_" + s))
            self.cnt[s] = 0
        self.waited = {e: {} for e in ENGS}
        self.dma_log = {q: [] for q in DMA_Q.values()}
        self.dma_hist = []
        self.last_dma = None
        self.banks = [Buf(stack.enter_context(nc.psum_tensor(f"pbank{i}", [128, 512], F32)), f"pbank{i}")
                      for i in range(7)]
        self.bankT = Buf(stack.enter_context(nc.psum_tensor("pbankT", [128, 1024], BF16)), "pbankT")

    def sb(self, name, shape, dtype, st=None):
        h = (st or self.stack).enter_context(self.nc.sbuf_tensor(name, shape, dtype))
        return Buf(h, name)

    def dram(self, name, shape, dtype, kind="Internal"):
        h = self.nc.dram_tensor(name, list(shape), dtype, kind=kind)
        return Buf(h, name)

    def _wait(self, eng, tok):
        if tok is None:
            return
        s, v = tok
        w = self.waited[eng]
        if w.get(s, 0) >= v:
            return
        getattr(self.nc, eng).wait_ge(self.sems[s], v)
        w[s] = v

    def op(self, eng, fn, r=(), w=(), sem=None, inc=1):
        sem = sem or eng
        is_dma = sem in ("sy_d", "gp_d", "sc_d")
        if PARANOIA == 2 or (PARANOIA == 3 and is_dma) or (PARANOIA == 4 and not is_dma):
            for s, v in self.cnt.items():
                if v:
                    self._wait(eng, (s, v))
        elif PARANOIA in (5, 6, 7) and is_dma:
            for s, v in self.cnt.items():
                if not v:
                    continue
                s_dma = s in ("sy_d", "gp_d", "sc_d")
                if ((PARANOIA == 5 and not s_dma) or (PARANOIA == 6 and s_dma)
                        or (PARANOIA == 7 and s_dma and s != sem)):
                    self._wait(eng, (s, v))
        for b in r:
            self._wait(eng, b.w)
        for b in w:
            self._wait(eng, b.w)
            for s, v in b.r.items():
                self._wait(eng, (s, v))
        inst = fn(getattr(self.nc, eng))
        self.cnt[sem] += inc
        inst.then_inc(self.sems[sem], inc)
        tok = (sem, self.cnt[sem])
        for b in r:
            b.r[tok[0]] = max(b.r.get(tok[0], 0), tok[1])
        for b in w:
            b.w = tok
            b.r = {}
        return tok

    def dbg(self, name, src_ap, src_buf, shape):
        if name not in DBG:
            return
        b = self.dram("dbg_" + name, shape, F32, kind="ExternalOutput")
        self.dma("gpsimd", b.h.ap()[:, :], src_ap, r=[src_buf], w=[b])

    def dma(self, eng, out_ap, in_ap, r=(), w=(), **kw):
        q = DMA_Q[eng]
        log = self.dma_log[q]
        if DMA_CHAIN == 2 and self.dma_hist:
            lq, lt = self.dma_hist[-1]
            if q == "gp_d" and lq == "gp_d":
                if len(self.dma_hist) >= 2:
                    self._wait(eng, self.dma_hist[-2][1])
            else:
                self._wait(eng, lt)
        elif DMA_CHAIN and self.last_dma is not None:
            self._wait(eng, self.last_dma)
        elif DMA_WINDOW and len(log) >= DMA_WINDOW:
            self._wait(eng, (q, log[-DMA_WINDOW]))
        tok = self.op(eng, lambda E: E.dma_start(out=out_ap, in_=in_ap, **kw),
                      r=r, w=w, sem=q, inc=16)
        log.append(tok[1])
        self.dma_hist.append((q, tok))
        self.last_dma = tok
        return tok

    def fence(self):
        for e in ENGS:
            for s, v in self.cnt.items():
                if v:
                    self._wait(e, (s, v))

    def tt(self, eng, out, a, b, alu, r, w):
        return self.op(eng, lambda E: E.tensor_tensor(out, a, b, op=alu), r=r, w=w)

    def ts(self, eng, out, a, s1, s2, op0, op1, r, w):
        if s2 is None:
            return self.op(eng, lambda E: E.tensor_scalar(out, a, s1, None, op0=op0), r=r, w=w)
        return self.op(eng, lambda E: E.tensor_scalar(out, a, s1, s2, op0=op0, op1=op1), r=r, w=w)

    def stt(self, eng, out, in0, scalar, in1, op0, op1, r, w):
        return self.op(eng, lambda E: E.scalar_tensor_tensor(out, in0, scalar, in1, op0, op1), r=r, w=w)

    def act(self, out, in_, func, r, w, bias=0.0, scale=1.0, accum_out=None):
        return self.op("scalar",
                       lambda E: E.activation(out=out, in_=in_, func=func, bias=bias, scale=scale,
                                              accum_out=accum_out),
                       r=r, w=w)

    def copy(self, eng, out, in_, r, w):
        if eng == "scalar":
            return self.op(eng, lambda E: E.copy(out, in_), r=r, w=w)
        return self.op(eng, lambda E: E.tensor_copy(out, in_), r=r, w=w)

    def memset(self, eng, ap, val, w):
        return self.op(eng, lambda E: E.memset(ap, val), r=(), w=w)

    def mm(self, out_ap, pairs, r, w):
        def fn(E):
            inst = None
            n = len(pairs)
            for i, (lh, rh) in enumerate(pairs):
                inst = E.matmul(out_ap, lh, rh, start=(i == 0), stop=(i == n - 1))
            return inst
        return self.op("tensor", fn, r=r, w=w)

    def mm_acc(self, out_ap, pairs, r, w, start, stop):
        def fn(E):
            inst = None
            n = len(pairs)
            for i, (lh, rh) in enumerate(pairs):
                inst = E.matmul(out_ap, lh, rh, start=(start and i == 0), stop=(stop and i == n - 1))
            return inst
        return self.op("tensor", fn, r=r, w=w)

    def transpose(self, out_ap, in_ap, ident_ap, r, w):
        return self.op("tensor", lambda E: E.transpose(out_ap, in_ap, ident_ap), r=r, w=w)



class LNStats:
    def __init__(self, p, st, tag):
        self.rows = p.sb(f"{tag}_rows", [1, PW * 4], F32, st)
        self.rows_bf = p.sb(f"{tag}_rowsbf", [1, PW * 4], BF16, st)
        self.eps = p.sb(f"{tag}_eps", [1, 1], F32, st)

    def s(self, i, n=PW):
        return self.rows.h[0:1, i * PW:i * PW + n]

    def sbf(self, i, n=PW):
        return self.rows_bf.h[0:1, i * PW:i * PW + n]

    def mu(self):
        return self.s(0)

    def a(self):
        return self.s(1)

    def b(self):
        return self.s(2)

    def sn(self):
        return self.s(3)

    def a_bf(self):
        return self.sbf(0)

    def b_bf(self):
        return self.sbf(1)

    def sn_bf(self):
        return self.sbf(2)


def emit_ln_stats(p, C, ln, xb, xsq, nchunks, eps, bank_s, bank_q, want_sn=False):
    nd = nchunks * 128
    ones = C["ones_col"]
    for c in range(nchunks):
        xb_c = xb.h[:, c * PW:(c + 1) * PW]
        sq_c = xsq.h[:, (c % 2) * PW:((c % 2) + 1) * PW]
        p.act(sq_c, xb_c, AF.Square, r=[xb], w=[xsq])
        p.mm_acc(bank_s.h[0:1, 0:PW], [(ones.h[:, 0:1], xb_c)], r=[ones, xb], w=[bank_s],
                 start=(c == 0), stop=(c == nchunks - 1))
        p.mm_acc(bank_q.h[0:1, 0:PW], [(ones.h[:, 0:1], sq_c)], r=[ones, xsq], w=[bank_q],
                 start=(c == 0), stop=(c == nchunks - 1))
    rows = ln.rows
    p.ts("vector", ln.s(0), bank_s.h[0:1, 0:PW], 1.0 / nd, None, ALU.mult, None, r=[bank_s], w=[rows])
    p.act(ln.s(3), ln.s(0), AF.Square, r=[rows], w=[rows])
    p.stt("vector", ln.s(1), bank_q.h[0:1, 0:PW], 1.0 / nd, ln.s(3), ALU.mult, ALU.subtract,
          r=[bank_q, rows], w=[rows])
    p.memset("vector", ln.eps.h[:], eps, w=[ln.eps])
    p.act(ln.s(3), ln.s(1), AF.Ln, bias=ln.eps.h[0:1, 0:1], r=[rows, ln.eps], w=[rows])
    p.act(ln.s(1), ln.s(3), AF.Exp, scale=-0.5, r=[rows], w=[rows])
    p.stt("vector", ln.s(2), ln.s(0), -1.0, ln.s(1), ALU.mult, ALU.mult, r=[rows], w=[rows])
    p.copy("gpsimd", ln.a_bf(), ln.a(), r=[rows], w=[ln.rows_bf])
    p.copy("gpsimd", ln.b_bf(), ln.b(), r=[rows], w=[ln.rows_bf])
    if want_sn:
        p.stt("vector", ln.s(3), bank_s.h[0:1, 0:PW], 1.0, ln.s(1), ALU.mult, ALU.mult,
              r=[bank_s, rows], w=[rows])
        p.stt("vector", ln.s(3), ln.s(2), float(nd), ln.s(3), ALU.mult, ALU.add, r=[rows], w=[rows])
        p.copy("gpsimd", ln.sn_bf(), ln.sn(), r=[rows], w=[ln.rows_bf])


def emit_bcast(p, C, row_bf_ap, rowsrc_buf, bank_bc, out_buf, n=PW):
    sl = bank_bc.h[:, 0:n]
    p.mm(sl, [(C["ones_row"].h[0:1, 0:128], row_bf_ap)], r=[C["ones_row"], rowsrc_buf], w=[bank_bc])
    p.copy("scalar", out_buf.h[:, 0:n], sl, r=[bank_bc], w=[out_buf])


def emit_xn(p, C, ln, xb, a_bc, b_bc, bank_bc, nchunks):
    emit_bcast(p, C, ln.a_bf(), ln.rows_bf, bank_bc, a_bc)
    emit_bcast(p, C, ln.b_bf(), ln.rows_bf, bank_bc, b_bc)
    for c in range(nchunks):
        sl = xb.h[:, c * PW:(c + 1) * PW]
        tmp = C["tmp_f32"]
        p.tt("gpsimd", tmp.h[:, 0:PW], sl, a_bc.h[:], ALU.mult, r=[xb, a_bc], w=[tmp])
        p.tt("vector", sl, tmp.h[:, 0:PW], b_bc.h[:], ALU.add, r=[tmp, b_bc], w=[xb])


def emit_post_ln_residual(p, C, ln2, y, ysq, x_out_buf, x_out_ap, panel, a2_bc, b2_bc,
                          bank_s, bank_q, bank_bc, t_rot, nchunks=16):
    emit_ln_stats(p, C, ln2, y, ysq, nchunks, 1e-5, bank_s, bank_q)
    emit_bcast(p, C, ln2.a_bf(), ln2.rows_bf, bank_bc, a2_bc)
    emit_bcast(p, C, ln2.b_bf(), ln2.rows_bf, bank_bc, b2_bc)
    for c in range(nchunks):
        y_c = y.h[:, c * PW:(c + 1) * PW]
        trb, off = t_rot[c % 2]
        p.tt("gpsimd", trb.h[:, off:off + PW], y_c, a2_bc.h[:], ALU.mult, r=[y, a2_bc], w=[trb])
        p.tt("vector", trb.h[:, off + PW:off + 2 * PW], trb.h[:, off:off + PW], b2_bc.h[:],
             ALU.add, r=[trb, b2_bc], w=[trb])
        p.dma("gpsimd", x_out_ap[c * 128:(c + 1) * 128, panel * PW:(panel + 1) * PW],
              trb.h[:, off + PW:off + 2 * PW], r=[trb, x_out_buf], w=[x_out_buf], accum_op=ALU.add)



def emit_row_pair(p, ln, row_f32_ap, hi_bf_ap, lo_bf_ap, scratch_f32_ap):
    p.copy("gpsimd", hi_bf_ap, row_f32_ap, r=[ln.rows], w=[ln.rows_bf])
    p.copy("gpsimd", scratch_f32_ap, hi_bf_ap, r=[ln.rows_bf], w=[ln.rows])
    p.tt("vector", scratch_f32_ap, row_f32_ap, scratch_f32_ap, ALU.subtract, r=[ln.rows], w=[ln.rows])
    p.copy("gpsimd", lo_bf_ap, scratch_f32_ap, r=[ln.rows], w=[ln.rows_bf])



def phase_l0_attn(p, C, W, x_in, x_out):
    nc = p.nc
    st = ExitStack()
    xb = p.sb("A_xb", [128, 16 * PW], BF16, st)
    xsq = p.sb("A_xsq", [128, 2 * PW], BF16, st)
    qsb = p.sb("A_q", [128, 16 * PW], BF16, st)
    ksb = p.sb("A_k", [128, 16 * PW], BF16, st)
    vsb = p.sb("A_v", [128, 16 * PW], BF16, st)
    usb = p.sb("A_u", [128, 16 * PW], BF16, st)
    osb = p.sb("A_o", [128, 16 * PW], BF16, st)
    wbuf = [p.sb(f"A_w{i}", [128, 2 * 2048], BF16, st) for i in range(2)]
    masks = p.sb("A_masks", [128, 16 * 512], BF16, st)
    qd_bc = p.sb("A_qdbc", [128, 16 * 256], BF16, st)
    kd_cols = p.sb("A_kdcols", [128, 16 * 2], F32, st)
    bd_cols = p.sb("A_bdcols", [128, 16], F32, st)
    kv = p.sb("A_kv", [128, 16 * 128], F32, st)
    kvb = p.sb("A_kvb", [128, 16 * 128], BF16, st)
    scol = p.sb("A_scol", [128, 16], F32, st)
    srow_bf = p.sb("A_srowbf", [1, 16], BF16, st)
    qq = p.sb("A_qq", [128, 256], BF16, st)
    qkm = p.sb("A_qkm", [128, 2 * 256], BF16, st)
    ktd = p.sb("A_ktd", [128, 2 * 128], BF16, st)
    vtd = p.sb("A_vtd", [128, 2 * 128], BF16, st)
    tmp_f32 = p.sb("A_tmpf", [128, PW], F32, st)
    a_bc = p.sb("A_abc", [128, PW], F32, st)
    b_bc = p.sb("A_bbc", [128, PW], F32, st)
    a2_bc = p.sb("A_a2bc", [128, PW], F32, st)
    b2_bc = p.sb("A_b2bc", [128, PW], F32, st)
    t_rot = [(p.sb(f"A_t{i}", [128, 2 * PW], F32, st), 0) for i in range(2)]
    rst_bc = p.sb("A_rstbc", [128, PW], F32, st)
    ln = LNStats(p, st, "A_ln")
    ln2 = ln
    C["tmp_f32"] = tmp_f32
    load_const_tiles(p, C, W, st, ["Dm", "arr_row", "karr_cols"])

    bankM, bankQK0, bankQK1, bankO, bankKV, bankS, bankQ2 = p.banks
    bankTR = p.bankT
    bank_bc = bankQK0

    p.dma("sync", x_out.h.ap()[:, :], x_in.h.ap()[:, :], r=[x_in], w=[x_out])
    p.dma("gpsimd", srow_bf.h[0:1, :], W["slope_row"].h.ap()[0:1, :], r=[W["slope_row"]], w=[srow_bf])
    p.mm(bankM.h[:, 0:16], [(C["ones_row"].h[0:1, 0:128], srow_bf.h[0:1, 0:16])],
         r=[C["ones_row"], srow_bf], w=[bankM])
    p.ts("vector", scol.h[:], bankM.h[:, 0:16], -1.0, None, ALU.mult, None, r=[bankM], w=[scol])
    for h in range(NH):
        sc = scol.h[:, h:h + 1]
        for j in range(2):
            p.ts("vector", tmp_f32.h[:, 0:256], C["Dm"].h[:, j * 256:(j + 1) * 256], sc, None,
                 ALU.mult, None, r=[C["Dm"], scol], w=[tmp_f32])
            p.act(masks.h[:, h * 512 + j * 256: h * 512 + (j + 1) * 256], tmp_f32.h[:, 0:256],
                  AF.Exp, r=[tmp_f32], w=[masks])
        p.ts("vector", ln.s(3, 256), C["arr_row"].h[0:1, 0:256], scol.h[0:1, h:h + 1], None,
             ALU.mult, None, r=[C["arr_row"], scol], w=[ln.rows])
        p.act(ln.sbf(3, 256), ln.s(3, 256), AF.Exp, r=[ln.rows], w=[ln.rows_bf])
        p.mm(bankM.h[:, 0:256], [(C["ones_row"].h[0:1, 0:128], ln.sbf(3, 256))],
             r=[C["ones_row"], ln.rows_bf], w=[bankM])
        p.copy("scalar", qd_bc.h[:, h * 256:(h + 1) * 256], bankM.h[:, 0:256], r=[bankM], w=[qd_bc])
        p.ts("vector", tmp_f32.h[:, 0:2], C["karr_cols"].h[:, 0:2], sc, None, ALU.mult, None,
             r=[C["karr_cols"], scol], w=[tmp_f32])
        p.act(kd_cols.h[:, h * 2:(h + 1) * 2], tmp_f32.h[:, 0:2], AF.Exp, r=[tmp_f32], w=[kd_cols])
        p.ts("vector", tmp_f32.h[:, 2:3], scol.h[:, h:h + 1], 256.0, None, ALU.mult, None,
             r=[scol], w=[tmp_f32])
        p.act(bd_cols.h[:, h:h + 1], tmp_f32.h[:, 2:3], AF.Exp, r=[tmp_f32], w=[bd_cols])
    p.memset("vector", kv.h[:], 0.0, w=[kv])
    p.memset("vector", kvb.h[:], 0.0, w=[kvb])

    wT = W["wqkvuT"]
    woT = W["woutT"]

    for panel in range(NP):
        t0 = panel * PW
        p.dma("gpsimd", xb.h[:],
              x_in.h.ap()[:, t0:t0 + PW].rearrange("(c p) t -> p c t", p=128),
              r=[x_in], w=[xb])
        emit_ln_stats(p, C, ln, xb, xsq, 16, 1e-5, bankS, bankQ2)
        emit_xn(p, C, ln, xb, a_bc, b_bc, bank_bc, 16)

        for fc2 in range(32):
            wb = wbuf[fc2 % 2]
            p.dma("gpsimd", wb.h[:],
                  wT.h.ap()[fc2 * 256:(fc2 + 1) * 256, :].rearrange("(t p) k -> p t k", p=128),
                  r=[wT], w=[wb])
            for sub in range(2):
                fc = fc2 * 2 + sub
                pairs = [(wb.h[:, sub * 2048 + c * 128:sub * 2048 + (c + 1) * 128],
                          xb.h[:, c * PW:(c + 1) * PW]) for c in range(16)]
                p.mm(bankM.h[:, 0:PW], pairs, r=[wb, xb], w=[bankM])
                kind, idx = divmod(fc, 16)
                dstbuf = (qsb, ksb, vsb, usb)[kind]
                dst = dstbuf.h[:, idx * PW:(idx + 1) * PW]
                if kind <= 1:
                    p.act(dst, bankM.h[:, 0:PW], AF.Silu, r=[bankM], w=[dstbuf])
                else:
                    p.copy("scalar", dst, bankM.h[:, 0:PW], r=[bankM], w=[dstbuf])

        for blk in range(2):
            boff = blk * 256
            for h in range(NH):
                qs = qsb.h[:, h * PW + boff:h * PW + boff + 256]
                ks = ksb.h[:, h * PW + boff:h * PW + boff + 256]
                vs = vsb.h[:, h * PW + boff:h * PW + boff + 256]
                kv_h = kv.h[:, h * 128:(h + 1) * 128]
                kvb_h = kvb.h[:, h * 128:(h + 1) * 128]
                p.tt("vector", qq.h[:], qs, qd_bc.h[:, h * 256:(h + 1) * 256], ALU.mult,
                     r=[qsb, qd_bc], w=[qq])
                for j in range(2):
                    p.transpose(bankTR.h[:, 0:128],
                                ks[:, j * 128:(j + 1) * 128], C["ident"].h[:],
                                r=[ksb, C["ident"]], w=[bankTR])
                    p.op("scalar", lambda E, j=j, h=h: E.activation(
                        out=ktd.h[:, j * 128:(j + 1) * 128], in_=bankTR.h[:, 0:128],
                        func=AF.Copy, scale=kd_cols.h[:, h * 2 + j:h * 2 + j + 1], bias=0.0),
                        r=[bankTR, kd_cols], w=[ktd])
                    p.transpose(bankTR.h[:, 128:256],
                                vs[:, j * 128:(j + 1) * 128], C["ident"].h[:],
                                r=[vsb, C["ident"]], w=[bankTR])
                    p.copy("scalar", vtd.h[:, j * 128:(j + 1) * 128],
                           bankTR.h[:, 128:256], r=[bankTR], w=[vtd])
                for j, bankj in ((0, bankQK0), (1, bankQK1)):
                    p.mm(bankj.h[:, 0:256], [(ks[:, j * 128:(j + 1) * 128], qs)],
                         r=[ksb, qsb], w=[bankj])
                    p.tt("vector", qkm.h[:, j * 256:(j + 1) * 256], bankj.h[:, 0:256],
                         masks.h[:, h * 512 + j * 256:h * 512 + (j + 1) * 256], ALU.mult,
                         r=[bankj, masks], w=[qkm])
                p.mm(bankO.h[:, 0:256],
                     [(kvb_h, qq.h[:]),
                      (vtd.h[:, 0:128], qkm.h[:, 0:256]),
                      (vtd.h[:, 128:256], qkm.h[:, 256:512])],
                     r=[kvb, qq, vtd, qkm], w=[bankO])
                p.copy("scalar", osb.h[:, h * PW + boff:h * PW + boff + 256], bankO.h[:, 0:256],
                       r=[bankO], w=[osb])
                p.mm(bankKV.h[:, 0:128],
                     [(ktd.h[:, 0:128], vtd.h[:, 0:128]),
                      (ktd.h[:, 128:256], vtd.h[:, 128:256])],
                     r=[ktd, vtd], w=[bankKV])
                p.stt("vector", kv_h, kv_h, bd_cols.h[:, h:h + 1], bankKV.h[:, 0:128],
                      ALU.mult, ALU.add, r=[kv, bd_cols, bankKV], w=[kv])
                p.copy("gpsimd", kvb_h, kv_h, r=[kv], w=[kvb])

        for c in range(16):
            o_c = osb.h[:, c * PW:(c + 1) * PW]
            sq_c = xsq.h[:, (c % 2) * PW:((c % 2) + 1) * PW]
            p.act(sq_c, o_c, AF.Square, r=[osb], w=[xsq])
            p.mm_acc(bankS.h[0:1, 0:PW], [(C["ones_col"].h[:, 0:1], sq_c)],
                     r=[C["ones_col"], xsq], w=[bankS], start=(c == 0), stop=(c == 15))
        p.ts("vector", ln.s(0), bankS.h[0:1, 0:PW], 1.0 / 2048.0, None, ALU.mult, None,
             r=[bankS], w=[ln.rows])
        p.memset("vector", ln.eps.h[:], 1e-6, w=[ln.eps])
        p.act(ln.s(3), ln.s(0), AF.Ln, bias=ln.eps.h[0:1, 0:1], r=[ln.rows, ln.eps], w=[ln.rows])
        p.act(ln.s(0), ln.s(3), AF.Exp, scale=-0.5, r=[ln.rows], w=[ln.rows])
        p.copy("gpsimd", ln.sbf(3), ln.s(0), r=[ln.rows], w=[ln.rows_bf])
        emit_bcast(p, C, ln.sbf(3), ln.rows_bf, bank_bc, rst_bc)
        for c in range(16):
            o_c = osb.h[:, c * PW:(c + 1) * PW]
            u_c = usb.h[:, c * PW:(c + 1) * PW]
            p.tt("gpsimd", tmp_f32.h[:, 0:PW], o_c, rst_bc.h[:], ALU.mult, r=[osb, rst_bc], w=[tmp_f32])
            p.tt("vector", o_c, tmp_f32.h[:, 0:PW], u_c, ALU.mult, r=[tmp_f32, usb], w=[osb])

        for fc2 in range(8):
            wb = wbuf[fc2 % 2]
            p.dma("gpsimd", wb.h[:],
                  woT.h.ap()[fc2 * 256:(fc2 + 1) * 256, :].rearrange("(t p) k -> p t k", p=128),
                  r=[woT], w=[wb])
            for sub in range(2):
                fc = fc2 * 2 + sub
                pairs = [(wb.h[:, sub * 2048 + c * 128:sub * 2048 + (c + 1) * 128],
                          osb.h[:, c * PW:(c + 1) * PW]) for c in range(16)]
                p.mm(bankM.h[:, 0:PW], pairs, r=[wb, osb], w=[bankM])
                p.copy("scalar", usb.h[:, fc * PW:(fc + 1) * PW], bankM.h[:, 0:PW], r=[bankM], w=[usb])

        emit_post_ln_residual(p, C, ln2, usb, xsq, x_out, x_out.h.ap(), panel,
                              a2_bc, b2_bc, bankS, bankQ2, bank_bc, t_rot)

    p.fence()
    st.close()





def phase_l0_ffn(p, C, W, x_in, x_out):
    st = ExitStack()
    xb = p.sb("B_xb", [128, 16 * PW], BF16, st)
    xsq = p.sb("B_xsq", [128, 2 * PW], BF16, st)
    hsb = p.sb("B_h", [128, 64 * PW], BF16, st)
    ysb = p.sb("B_y", [128, 16 * PW], BF16, st)
    gs = p.sb("B_gs", [128, 2 * PW], BF16, st)
    us = p.sb("B_us", [128, 2 * PW], BF16, st)
    wgbuf = [p.sb(f"B_wg{i}", [128, 2 * 2048], BF16, st) for i in range(2)]
    wubuf = [p.sb(f"B_wu{i}", [128, 2 * 2048], BF16, st) for i in range(2)]
    wdbuf = [p.sb(f"B_wd{i}", [128, 8192], BF16, st) for i in range(2)]
    tmp_f32 = p.sb("B_tmpf", [128, PW], F32, st)
    a_bc = p.sb("B_abc", [128, PW], F32, st)
    b_bc = p.sb("B_bbc", [128, PW], F32, st)
    t_rot = [(p.sb(f"B_t{i}", [128, 2 * PW], F32, st), 0) for i in range(2)]
    ln = LNStats(p, st, "B_ln")
    C["tmp_f32"] = tmp_f32
    load_const_tiles(p, C, W, st, ["gsc", "usc", "dsc"])

    bankG, bankU, bankD, bankBC, bankS, bankQ2, _sp = p.banks

    gsc, usc, dsc = C["gsc"], C["usc"], C["dsc"]
    n128 = C["neg128_row"]

    p.dma("sync", x_out.h.ap()[:, :], x_in.h.ap()[:, :], r=[x_in], w=[x_out])

    for panel in range(NP):
        t0 = panel * PW
        p.dma("gpsimd", xb.h[:],
              x_in.h.ap()[:, t0:t0 + PW].rearrange("(c p) t -> p c t", p=128),
              r=[x_in], w=[xb])
        emit_ln_stats(p, C, ln, xb, xsq, 16, 1e-5, bankS, bankQ2)
        emit_xn(p, C, ln, xb, a_bc, b_bc, bankBC, 16)
        for c in range(16):
            p.mm_acc(bankS.h[0:1, 0:PW], [(C["ones_col"].h[:, 0:1], xb.h[:, c * PW:(c + 1) * PW])],
                     r=[C["ones_col"], xb], w=[bankS], start=(c == 0), stop=(c == 15))
        p.copy("vector", ln.s(3), bankS.h[0:1, 0:PW], r=[bankS], w=[ln.rows])
        emit_row_pair(p, ln, ln.s(3), ln.sbf(2), ln.sbf(3), ln.s(0))

        for fc2 in range(32):
            wg = wgbuf[fc2 % 2]
            wu = wubuf[fc2 % 2]
            p.dma("gpsimd", wg.h[:],
                  W["gateT8"].h.ap()[fc2 * 256:(fc2 + 1) * 256, :].rearrange("(t p) k -> p t k", p=128),
                  r=[W["gateT8"]], w=[wg])
            p.dma("gpsimd", wu.h[:],
                  W["upT8"].h.ap()[fc2 * 256:(fc2 + 1) * 256, :].rearrange("(t p) k -> p t k", p=128),
                  r=[W["upT8"]], w=[wu])
            for sub in range(2):
                fc = fc2 * 2 + sub
                wgs = wg.h[:, sub * 2048:(sub + 1) * 2048]
                wus = wu.h[:, sub * 2048:(sub + 1) * 2048]
                pairs_g = [(wgs[:, c * 128:(c + 1) * 128], xb.h[:, c * PW:(c + 1) * PW]) for c in range(16)]
                pairs_g.append((n128.h[0:1, 0:128], ln.sbf(2)))
                pairs_g.append((n128.h[0:1, 0:128], ln.sbf(3)))
                p.mm(bankG.h[:, 0:PW], pairs_g, r=[wg, xb, n128, ln.rows_bf], w=[bankG])
                pairs_u = [(wus[:, c * 128:(c + 1) * 128], xb.h[:, c * PW:(c + 1) * PW]) for c in range(16)]
                pairs_u.append((n128.h[0:1, 0:128], ln.sbf(2)))
                pairs_u.append((n128.h[0:1, 0:128], ln.sbf(3)))
                p.mm(bankU.h[:, 0:PW], pairs_u, r=[wu, xb, n128, ln.rows_bf], w=[bankU])
                g_r = gs.h[:, (fc % 2) * PW:((fc % 2) + 1) * PW]
                u_r = us.h[:, (fc % 2) * PW:((fc % 2) + 1) * PW]
                p.act(g_r, bankG.h[:, 0:PW], AF.Silu, scale=gsc.h[:, fc:fc + 1], r=[bankG, gsc], w=[gs])
                p.act(u_r, bankU.h[:, 0:PW], AF.Copy, scale=usc.h[:, fc:fc + 1], bias=0.0,
                      r=[bankU, usc], w=[us])
                p.tt("gpsimd", hsb.h[:, fc * PW:(fc + 1) * PW], g_r, u_r, ALU.mult, r=[gs, us], w=[hsb])

        for c in range(64):
            p.mm_acc(bankS.h[0:1, 0:PW], [(C["ones_col"].h[:, 0:1], hsb.h[:, c * PW:(c + 1) * PW])],
                     r=[C["ones_col"], hsb], w=[bankS], start=(c == 0), stop=(c == 63))
        p.copy("vector", ln.s(3), bankS.h[0:1, 0:PW], r=[bankS], w=[ln.rows])
        emit_row_pair(p, ln, ln.s(3), ln.sbf(2), ln.sbf(3), ln.s(0))

        for oc in range(16):
            wd = wdbuf[oc % 2]
            p.dma("gpsimd", wd.h[:], W["downT8"].h.ap()[oc * 128:(oc + 1) * 128, :],
                  r=[W["downT8"]], w=[wd])
            pairs = [(wd.h[:, c * 128:(c + 1) * 128], hsb.h[:, c * PW:(c + 1) * PW]) for c in range(64)]
            pairs.append((n128.h[0:1, 0:128], ln.sbf(2)))
            pairs.append((n128.h[0:1, 0:128], ln.sbf(3)))
            p.mm(bankD.h[:, 0:PW], pairs, r=[wd, hsb, n128, ln.rows_bf], w=[bankD])
            p.act(ysb.h[:, oc * PW:(oc + 1) * PW], bankD.h[:, 0:PW], AF.Copy,
                  scale=dsc.h[:, oc:oc + 1], bias=0.0, r=[bankD, dsc], w=[ysb])

        emit_post_ln_residual(p, C, ln, ysb, xsq, x_out, x_out.h.ap(), panel,
                              a_bc, b_bc, bankS, bankQ2, bankBC, t_rot)

    p.fence()
    st.close()




def phase_l1_attn_proj(p, C, W, x_in, S):
    st = ExitStack()
    xb = p.sb("C_xb", [128, 16 * PW], BF16, st)
    xsq = p.sb("C_xsq", [128, 2 * PW], BF16, st)
    es32 = p.sb("C_es32", [128, 8 * 2048], F32, st)
    cz = p.sb("C_cz", [128, 2048], F32, st)
    wbuf = [p.sb(f"C_w{i}", [128, 2048], BF16, st) for i in range(2)]
    wpan = [p.sb(f"C_wp{i}", [128, 16 * 512], BF16, st) for i in range(2)]
    ropea = p.sb("C_ropea", [64, 4 * PW], F32, st)
    qf2 = p.sb("C_qf2", [64, 4 * PW], BF16, st)
    otile = p.sb("C_otile", [128, 2 * PW], BF16, st)
    ws1sb = p.sb("C_ws1", [128, 16 * 16], BF16, st)
    ws2sb = p.sb("C_ws2", [16, 1024], BF16, st)
    s1bf = p.sb("C_s1", [16, PW], BF16, st)
    esbf = p.sb("C_esbf", [128, 2048], BF16, st)
    estT = p.sb("C_estT", [128, 2048], BF16, st)
    tmp_f32 = p.sb("C_tmpf", [128, PW], F32, st)
    a_bc = p.sb("C_abc", [128, PW], F32, st)
    b_bc = p.sb("C_bbc", [128, PW], F32, st)
    ln = LNStats(p, st, "C_ln")
    C["tmp_f32"] = tmp_f32
    load_const_tiles(p, C, W, st, ["cosq", "sinq", "cosk", "sink", "bs2c"])

    bankM, bankV, bankS2, bankBC, bankS, bankQ2, _sp = p.banks
    bankT = p.bankT

    p.dma("gpsimd", ws1sb.h[:], W["ws1T"].h.ap()[:, :].rearrange("(c p) f -> p c f", p=128),
          r=[W["ws1T"]], w=[ws1sb])
    p.dma("gpsimd", ws2sb.h[:], W["ws2T"].h.ap()[:, :], r=[W["ws2T"]], w=[ws2sb])

    for panel in range(NP):
        t0 = panel * PW
        p.dma("gpsimd", xb.h[:],
              x_in.h.ap()[:, t0:t0 + PW].rearrange("(c p) t -> p c t", p=128),
              r=[x_in], w=[xb])
        emit_ln_stats(p, C, ln, xb, xsq, 16, 1e-5, bankS, bankQ2)
        emit_xn(p, C, ln, xb, a_bc, b_bc, bankBC, 16)

        for which, wT, dram, cosn, sinn in (
                (0, W["wqT"], S["qfD"], "cosq", "sinq"),
                (1, W["wkT"], S["kfD"], "cosk", "sink")):
            for hc in range(NH):
                wb = wbuf[hc % 2]
                p.dma("gpsimd", wb.h[:], wT.h.ap()[hc * 128:(hc + 1) * 128, :],
                      r=[wT], w=[wb])
                pairs = [(wb.h[:, c * 128:(c + 1) * 128], xb.h[:, c * PW:(c + 1) * PW]) for c in range(16)]
                p.mm(bankM.h[:, 0:PW], pairs, r=[wb, xb], w=[bankM])
                cosT = C[cosn]
                sinT = C[sinn]
                cs = cosT.h[0:64, t0:t0 + PW]
                sn = sinT.h[0:64, t0:t0 + PW]
                ps0 = bankM.h[0:64, 0:PW]
                ps1 = bankM.h[64:128, 0:PW]
                p.tt("vector", ropea.h[0:64, 0:PW], ps0, cs, ALU.mult, r=[bankM, cosT], w=[ropea])
                p.tt("vector", ropea.h[0:64, PW:2 * PW], ps1, sn, ALU.mult, r=[bankM, sinT], w=[ropea])
                p.tt("vector", ropea.h[0:64, 2 * PW:3 * PW], ps1, cs, ALU.mult, r=[bankM, cosT], w=[ropea])
                p.tt("vector", ropea.h[0:64, 3 * PW:4 * PW], ps0, sn, ALU.mult, r=[bankM, sinT], w=[ropea])
                qo = qf2.h[0:64, (hc % 2) * 2 * PW:((hc % 2) + 1) * 2 * PW]
                p.tt("gpsimd", qo[0:64, 0:PW], ropea.h[0:64, 0:PW], ropea.h[0:64, PW:2 * PW],
                     ALU.subtract, r=[ropea], w=[qf2])
                p.tt("gpsimd", qo[0:64, PW:2 * PW], ropea.h[0:64, 2 * PW:3 * PW],
                     ropea.h[0:64, 3 * PW:4 * PW], ALU.add, r=[ropea], w=[qf2])
                p.dma("sync",
                      dram.h.ap()[hc * 128:(hc + 1) * 128, t0:t0 + PW].rearrange(
                          "(half p) t -> p half t", p=64),
                      qo.rearrange("p (half t) -> p half t", t=PW),
                      r=[qf2, dram], w=[dram])

        for wT, dram in ((W["wvT"], S["vtD"]), (W["wgT"], S["gtD"])):
            for eg in range(4):
                wp = wpan[eg % 2]
                p.dma("gpsimd", wp.h[:], wT.h.ap()[eg * 128:(eg + 1) * 128, :],
                      r=[wT], w=[wp])
                for tt_ in range(4):
                    pairs = [(xb.h[:, c * PW + tt_ * 128:c * PW + (tt_ + 1) * 128],
                              wp.h[:, c * 512:(c + 1) * 512]) for c in range(16)]
                    p.mm(bankV.h[:, 0:512], pairs, r=[xb, wp], w=[bankV])
                    ot = otile.h[:, (tt_ % 2) * PW:((tt_ % 2) + 1) * PW]
                    p.copy("scalar", ot, bankV.h[:, 0:512], r=[bankV], w=[otile])
                    p.dma("sync", dram.h.ap()[t0 + tt_ * 128:t0 + (tt_ + 1) * 128, eg * 512:(eg + 1) * 512],
                          ot, r=[otile, dram], w=[dram])

        pairs = [(ws1sb.h[:, c * 16:(c + 1) * 16], xb.h[:, c * PW:(c + 1) * PW]) for c in range(16)]
        p.mm(bankS2.h[0:16, 0:PW], pairs, r=[ws1sb, xb], w=[bankS2])
        p.copy("vector", s1bf.h[:], bankS2.h[0:16, 0:PW], r=[bankS2], w=[s1bf])
        for ch in range(8):
            p.mm(bankS2.h[:, 0:PW], [(ws2sb.h[:, ch * 128:(ch + 1) * 128], s1bf.h[:])],
                 r=[ws2sb, s1bf], w=[bankS2])
            p.ts("vector", tmp_f32.h[:, 0:PW], bankS2.h[:, 0:PW], C["bs2c"].h[:, ch:ch + 1], 32.0,
                 ALU.add, ALU.min, r=[bankS2, C["bs2c"]], w=[tmp_f32])
            p.ts("vector", tmp_f32.h[:, 0:PW], tmp_f32.h[:, 0:PW], -32.0, None, ALU.max, None,
                 r=[tmp_f32], w=[tmp_f32])
            p.act(es32.h[:, ch * 2048 + t0:ch * 2048 + t0 + PW], tmp_f32.h[:, 0:PW], AF.Exp,
                  r=[tmp_f32], w=[es32])

    for ch in range(8):
        sl = es32.h[:, ch * 2048:(ch + 1) * 2048]
        p.op("vector", lambda E, sl=sl: E.tensor_tensor_scan(cz.h[:], sl, sl, 0.0,
                                                             op0=ALU.add, op1=ALU.max),
             r=[es32], w=[cz])
        p.copy("gpsimd", esbf.h[:], sl, r=[es32], w=[esbf])
        p.dma("sync", S["esfD"].h.ap()[ch * 128:(ch + 1) * 128, :], esbf.h[:],
              r=[esbf, S["esfD"]], w=[S["esfD"]])
        p.act(sl, cz.h[:], AF.Ln, r=[cz], w=[es32])
        p.act(cz.h[:], sl, AF.Exp, scale=-1.0, r=[es32], w=[cz])
        for tt_ in range(16):
            p.transpose(bankT.h[:, 0:128], esbf.h[:, tt_ * 128:(tt_ + 1) * 128], C["ident"].h[:],
                        r=[esbf, C["ident"]], w=[bankT])
            p.copy("scalar", estT.h[:, tt_ * 128:(tt_ + 1) * 128], bankT.h[:, 0:128],
                   r=[bankT], w=[estT])
        p.dma("sync", S["estD"].h.ap()[:, ch * 128:(ch + 1) * 128].rearrange("(tt p) m -> p tt m", p=128),
              estT.h[:], r=[estT, S["estD"]], w=[S["estD"]])
        p.copy("gpsimd", esbf.h[:], cz.h[:], r=[cz], w=[esbf])
        p.dma("sync", S["enfD"].h.ap()[ch * 128:(ch + 1) * 128, :], esbf.h[:],
              r=[esbf, S["enfD"]], w=[S["enfD"]])
        for tt_ in range(16):
            p.transpose(bankT.h[:, 0:128], esbf.h[:, tt_ * 128:(tt_ + 1) * 128], C["ident"].h[:],
                        r=[esbf, C["ident"]], w=[bankT])
            p.copy("scalar", estT.h[:, tt_ * 128:(tt_ + 1) * 128], bankT.h[:, 0:128],
                   r=[bankT], w=[estT])
        p.dma("sync", S["entD"].h.ap()[:, ch * 128:(ch + 1) * 128].rearrange("(tt p) m -> p tt m", p=128),
              estT.h[:], r=[estT, S["entD"]], w=[S["entD"]])

    p.fence()
    st.close()



def phase_l1_attn_core(p, C, W, x_in, x_out, S):
    st = ExitStack()
    qp = p.sb("C2_qp", [128, 16 * 512], BF16, st)
    kblk = p.sb("C2_kblk", [128, 16 * 512], BF16, st)
    estm = p.sb("C2_estm", [128, 4 * 1024], BF16, st)
    esfm = p.sb("C2_esfm", [128, 8 * 512], BF16, st)
    vt = p.sb("C2_vt", [128, 4 * 2048], BF16, st)
    enz = p.sb("C2_enz", [128, 4 * 1024], BF16, st)
    ok_sb = p.sb("C2_ok", [128, 4 * 1024], F32, st)
    tmpa = p.sb("C2_tmpa", [128, 3 * 1024], F32, st)
    qvzT = p.sb("C2_qvzT", [128, 8 * 512], BF16, st)
    o_sb = p.sb("C2_o", [128, 4 * 2048], F32, st)
    gsl = A_sb = None
    cols = p.sb("C2_cols", [128, 4 * 16], F32, st)
    gnw_bc = p.sb("C2_gnw", [128, 2048], BF16, st)
    A_sb = p.sb("C2_A", [128, 4 * 512], BF16, st)
    wbuf = [p.sb(f"C2_w{i}", [128, 2048], BF16, st) for i in range(2)]
    a_bc = p.sb("C2_abc", [128, PW], F32, st)
    b_bc = p.sb("C2_bbc", [128, PW], F32, st)
    xsq = p.sb("C2_xsq", [128, 2 * PW], BF16, st)
    ln = LNStats(p, st, "C2_ln")
    ysb = kblk
    t_rot = [(o_sb, 0), (o_sb, 2 * PW)]
    load_const_tiles(p, C, W, st, ["cmask", "gnw16"])

    bankA, bankOK, bankW2, bankO, bankS, bankQ2, bankBC = p.banks
    bankT = p.bankT

    p.dma("sync", x_out.h.ap()[:, :], x_in.h.ap()[:, :], r=[x_in], w=[x_out])
    p.copy("gpsimd", ln.sbf(0, 2048).rearrange("o (a b) -> o a b", b=PW), C["gnw16"].h[0:1, :].rearrange("o (a b) -> o a b", b=PW),
           r=[C["gnw16"]], w=[ln.rows_bf])
    for q4 in range(4):
        p.mm(bankBC.h[:, 0:PW], [(C["ones_row"].h[0:1, 0:128], ln.rows_bf.h[0:1, q4 * PW:(q4 + 1) * PW])],
             r=[C["ones_row"], ln.rows_bf], w=[bankBC])
        p.copy("scalar", gnw_bc.h[:, q4 * PW:(q4 + 1) * PW], bankBC.h[:, 0:PW], r=[bankBC], w=[gnw_bc])

    for tqp in range(NP):
        t0 = tqp * PW
        p.dma("gpsimd", qp.h[:],
              S["qfD"].h.ap()[:, t0:t0 + PW].rearrange("(h p) t -> p h t", p=128),
              r=[S["qfD"]], w=[qp])
        p.dma("gpsimd", enz.h[:],
              S["entD"].h.ap()[t0:t0 + PW, :].rearrange("(s p) m -> p s m", p=128),
              r=[S["entD"]], w=[enz])
        nblk = tqp + 1
        for kb in range(nblk):
            k0 = kb * PW
            p.dma("gpsimd", kblk.h[:],
                  S["kfD"].h.ap()[:, k0:k0 + PW].rearrange("(h p) t -> p h t", p=128),
                  r=[S["kfD"]], w=[kblk])
            p.dma("gpsimd", estm.h[:],
                  S["estD"].h.ap()[k0:k0 + PW, :].rearrange("(j p) m -> p j m", p=128),
                  r=[S["estD"]], w=[estm])
            for h in range(NH):
                kh = kblk.h[:, h * 512:(h + 1) * 512]
                qh = qp.h[:, h * 512:(h + 1) * 512]
                for tkt in range(4):
                    ps = (bankA, bankW2)[tkt % 2]
                    p.mm(ps.h[:, 0:512], [(kh[:, tkt * 128:(tkt + 1) * 128], qh)],
                         r=[kblk, qp], w=[ps])
                    asl = A_sb.h[:, tkt * 512:(tkt + 1) * 512]
                    if kb == tqp:
                        p.tt("vector", asl, ps.h[:, 0:512],
                             C["cmask"].h[:, tkt * 512:(tkt + 1) * 512], ALU.mult,
                             r=[ps, C["cmask"]], w=[A_sb])
                    else:
                        p.copy("scalar", asl, ps.h[:, 0:512], r=[ps], w=[A_sb])
                for sub in range(4):
                    prs = [(A_sb.h[:, tkt * 512 + sub * 128:tkt * 512 + (sub + 1) * 128],
                            estm.h[:, tkt * 1024 + h * 64:tkt * 1024 + (h + 1) * 64])
                           for tkt in range(4)]
                    p.mm_acc(bankOK.h[:, sub * 64:(sub + 1) * 64], prs, r=[A_sb, estm], w=[bankOK],
                             start=(True), stop=(True))
                okv = ok_sb.h[:, :].rearrange("p (s q) -> p s q", q=1024)[:, :, h * 64:(h + 1) * 64]
                psv = bankOK.h[:, 0:256].rearrange("p (s m) -> p s m", m=64)
                if kb == 0:
                    p.copy("vector", okv, psv, r=[bankOK], w=[ok_sb])
                else:
                    p.tt("vector", okv, okv, psv, ALU.add, r=[bankOK, ok_sb], w=[ok_sb])
        for sub in range(4):
            oks = ok_sb.h[:, sub * 1024:(sub + 1) * 1024]
            u0 = tmpa.h[:, 0:1024]
            p.tt("vector", u0, oks, enz.h[:, sub * 1024:(sub + 1) * 1024], ALU.mult,
                 r=[ok_sb, enz], w=[tmpa])
            u03 = u0.rearrange("p (h m) -> p h m", m=64)
            nmax = cols.h[:, 0:16]
            p.op("vector", lambda E, a=nmax, b=u03: E.reduce_max(a, b, axis=AXX, negate=True),
                 r=[tmpa], w=[cols])
            t1 = tmpa.h[:, 1024:2048]
            nb3 = cols.h[:, 0:16].rearrange("p (h o) -> p h o", o=1).broadcast_to([128, 16, 64])
            p.tt("vector", t1.rearrange("p (h m) -> p h m", m=64), u03, nb3, ALU.add,
                 r=[tmpa, cols], w=[tmpa])
            t2 = tmpa.h[:, 2048:3072]
            p.act(t2, t1, AF.Exp, r=[tmpa], w=[tmpa])
            ssum = cols.h[:, 16:32]
            p.op("vector", lambda E, a=ssum, b=t2.rearrange("p (h m) -> p h m", m=64):
                 E.reduce_sum(a, b, axis=AXX), r=[tmpa], w=[cols])
            rcp = cols.h[:, 32:48]
            p.op("vector", lambda E, a=rcp, b=ssum: E.reciprocal(a, b), r=[cols], w=[cols])
            rb3 = rcp.rearrange("p (h o) -> p h o", o=1).broadcast_to([128, 16, 64])
            t3 = tmpa.h[:, 0:1024]
            p.tt("vector", t3.rearrange("p (h m) -> p h m", m=64),
                 t2.rearrange("p (h m) -> p h m", m=64), rb3, ALU.mult, r=[tmpa, cols], w=[tmpa])
            qvz = A_sb.h[:, 1024:2048]
            p.tt("vector", qvz, t3, enz.h[:, sub * 1024:(sub + 1) * 1024], ALU.mult,
                 r=[tmpa, enz], w=[A_sb])
            for pr in range(8):
                p.transpose(bankT.h[:, 0:128], qvz[:, pr * 128:(pr + 1) * 128], C["ident"].h[:],
                            r=[A_sb, C["ident"]], w=[bankT])
                p.copy("scalar", qvzT.h[:, pr * 512 + sub * 128:pr * 512 + (sub + 1) * 128],
                       bankT.h[:, 0:128], r=[bankT], w=[qvzT])
        for kb in range(nblk):
            k0 = kb * PW
            p.dma("gpsimd", esfm.h[:],
                  S["esfD"].h.ap()[:, k0:k0 + PW].rearrange("(ch p) t -> p ch t", p=128),
                  r=[S["esfD"]], w=[esfm])
            p.dma("gpsimd", vt.h[:],
                  S["vtD"].h.ap()[k0:k0 + PW, :].rearrange("(j p) e -> p j e", p=128),
                  r=[S["vtD"]], w=[vt])
            for h in range(NH):
                pb = 64 * (h % 2)
                for tkt in range(4):
                    ps = (bankA, bankW2)[tkt % 2]
                    lh = esfm.h[pb:pb + 64, (h // 2) * 512 + tkt * 128:(h // 2) * 512 + (tkt + 1) * 128]
                    rh = qvzT.h[pb:pb + 64, (h // 2) * 512:(h // 2 + 1) * 512]
                    p.mm(ps.h[:, 0:512], [(lh, rh)], r=[esfm, qvzT], w=[ps])
                    asl = A_sb.h[:, tkt * 512:(tkt + 1) * 512]
                    if kb == tqp:
                        p.tt("vector", asl, ps.h[:, 0:512],
                             C["cmask"].h[:, tkt * 512:(tkt + 1) * 512], ALU.mult,
                             r=[ps, C["cmask"]], w=[A_sb])
                    else:
                        p.copy("scalar", asl, ps.h[:, 0:512], r=[ps], w=[A_sb])
                for sub in range(4):
                    prs = [(A_sb.h[:, tkt * 512 + sub * 128:tkt * 512 + (sub + 1) * 128],
                            vt.h[:, tkt * 2048 + h * 128:tkt * 2048 + (h + 1) * 128])
                           for tkt in range(4)]
                    p.mm_acc(bankO.h[:, sub * 128:(sub + 1) * 128], prs, r=[A_sb, vt], w=[bankO],
                             start=True, stop=True)
                ov = o_sb.h[:, :].rearrange("p (s q) -> p s q", q=2048)[:, :, h * 128:(h + 1) * 128]
                psv = bankO.h[:, 0:512].rearrange("p (s e) -> p s e", e=128)
                if kb == 0:
                    p.copy("vector", ov, psv, r=[bankO], w=[o_sb])
                else:
                    p.tt("vector", ov, ov, psv, ALU.add, r=[bankO, o_sb], w=[o_sb])
        for sub in range(4):
            osl = o_sb.h[:, sub * 2048:(sub + 1) * 2048]
            sq = tmpa.h[:, 0:2048]
            p.act(sq, osl, AF.Square, r=[o_sb], w=[tmpa])
            ss = cols.h[:, 0:16]
            p.op("vector", lambda E, a=ss, b=sq.rearrange("p (h e) -> p h e", e=128):
                 E.reduce_sum(a, b, axis=AXX), r=[tmpa], w=[cols])
            p.ts("vector", ss, ss, 1.0 / 128.0, None, ALU.mult, None, r=[cols], w=[cols])
            p.memset("vector", cols.h[:, 48:49], 1e-5, w=[cols])
            ss2 = cols.h[:, 16:32]
            p.act(ss2, ss, AF.Ln, bias=cols.h[:, 48:49], r=[cols], w=[cols])
            p.act(ss, ss2, AF.Exp, scale=-0.5, r=[cols], w=[cols])
            rb3 = ss.rearrange("p (h o) -> p h o", o=1).broadcast_to([128, 16, 128])
            t1 = tmpa.h[:, 1024:3072]
            p.tt("vector", t1.rearrange("p (h e) -> p h e", e=128),
                 osl.rearrange("p (h e) -> p h e", e=128), rb3, ALU.mult, r=[o_sb, cols], w=[tmpa])
            p.tt("gpsimd", t1, t1, gnw_bc.h[:], ALU.mult, r=[tmpa, gnw_bc], w=[tmpa])
            p.dma("gpsimd", enz.h[:, 0:2048],
                  S["gtD"].h.ap()[t0 + sub * 128:t0 + (sub + 1) * 128, :], r=[S["gtD"]], w=[enz])
            gsl = A_sb.h[:, 0:2048]
            p.act(gsl, enz.h[:, 0:2048], AF.Silu, r=[enz], w=[A_sb])
            og = vt.h[:, 0:2048]
            p.tt("vector", og, t1, gsl, ALU.mult, r=[tmpa, A_sb], w=[vt])
            for h in range(NH):
                p.transpose(bankT.h[:, 0:128], og[:, h * 128:(h + 1) * 128], C["ident"].h[:],
                            r=[vt, C["ident"]], w=[bankT])
                p.copy("scalar", qp.h[:, h * 512 + sub * 128:h * 512 + (sub + 1) * 128],
                       bankT.h[:, 0:128], r=[bankT], w=[qp])
        for oc in range(16):
            wb = wbuf[oc % 2]
            p.dma("gpsimd", wb.h[:], W["woT"].h.ap()[oc * 128:(oc + 1) * 128, :],
                  r=[W["woT"]], w=[wb])
            pairs = [(wb.h[:, c * 128:(c + 1) * 128], qp.h[:, c * 512:(c + 1) * 512]) for c in range(16)]
            p.mm(bankA.h[:, 0:512], pairs, r=[wb, qp], w=[bankA])
            p.copy("scalar", ysb.h[:, oc * PW:(oc + 1) * PW], bankA.h[:, 0:512], r=[bankA], w=[ysb])
        emit_post_ln_residual(p, C, ln, ysb, xsq, x_out, x_out.h.ap(), tqp,
                              a_bc, b_bc, bankS, bankQ2, bankBC, t_rot)

    p.fence()
    st.close()




HID = [1024, 2048, 3072, 4096, 1024, 2048, 3072, 4096]


def phase_l1_ffn(p, C, W, x_in, x_out):
    st = ExitStack()
    xb = p.sb("D_xb", [128, 16 * PW], BF16, st)
    xsq = p.sb("D_xsq", [128, 2 * PW], BF16, st)
    gxn = p.sb("D_gxn", [128, 16 * PW], BF16, st)
    gh = p.sb("D_gh", [128, 8 * PW], BF16, st)
    gmid = p.sb("D_gmid", [128, 32 * PW], BF16, st)
    acc = p.sb("D_acc", [128, 8 * PW], F32, st)
    accbf = gmid
    ysb = xb
    lg = p.sb("D_lg", [8, PW], F32, st)
    lgbf = p.sb("D_lgbf", [8, PW], BF16, st)
    lt = p.sb("D_lt", [128, 8], F32, st)
    cols = p.sb("D_cols", [128, 16], F32, st)
    pr8 = p.sb("D_pr8", [128, 3 * 8], F32, st)
    wexpT = p.sb("D_wexpT", [8, PW], BF16, st)
    w_bc = p.sb("D_wbc", [128, PW], BF16, st)
    crows = p.sb("D_crows", [1, 8 * 1024], BF16, st)
    b2row = p.sb("D_b2row", [1, 1024], BF16, st)
    ctot = p.sb("D_ctot", [1, 1024], F32, st)
    ctbf = p.sb("D_ctbf", [1, 1024], BF16, st)
    ctcols = p.sb("D_ctcols", [128, 8], F32, st)
    gb1 = p.sb("D_gb1", [128, 32], BF16, st)
    onespw = p.sb("D_onespw", [1, PW], BF16, st)
    w1buf = [p.sb(f"D_w1{i}", [128, 4 * 1024], BF16, st) for i in range(2)]
    w2buf = [p.sb(f"D_w2{i}", [128, 2 * 32 * 128], BF16, st) for i in range(2)]
    wdbuf = [p.sb(f"D_wd{i}", [128, 16 * 128], BF16, st) for i in range(2)]
    rwsb = p.sb("D_rw", [128, 16 * 8], BF16, st)
    tmpf = p.sb("D_tmpf", [128, PW], F32, st)
    a_bc = p.sb("D_abc", [128, PW], F32, st)
    b_bc = p.sb("D_bbc", [128, PW], F32, st)
    t_rot = [(p.sb(f"D_t{i}", [128, 2 * PW], F32, st), 0) for i in range(2)]
    ln = LNStats(p, st, "D_ln")
    C["tmp_f32"] = tmpf
    load_const_tiles(p, C, W, st, ["dn_bc", "up_bc", "rt_bc", "lcost", "sh_b1c", "sh_b2c",
                                   "co_bc", "sel8"] + [f"e{e}_b1c" for e in range(8)])

    bankH, bankMID, bankF, bankR, bankS, bankQ2, bankBC = p.banks
    bankT = p.bankT

    p.dma("sync", x_out.h.ap()[:, :], x_in.h.ap()[:, :], r=[x_in], w=[x_out])
    p.memset("vector", onespw.h[:], 1.0, w=[onespw])
    p.memset("vector", ctot.h[:], 0.0, w=[ctot])
    p.dma("gpsimd", rwsb.h[:], W["rt_w"].h.ap()[:, :].rearrange("(c p) e -> p c e", p=128),
          r=[W["rt_w"]], w=[rwsb])

    for panel in range(NP):
        t0 = panel * PW
        p.dma("gpsimd", xb.h[:],
              x_in.h.ap()[:, t0:t0 + PW].rearrange("(c p) t -> p c t", p=128),
              r=[x_in], w=[xb])
        emit_ln_stats(p, C, ln, xb, xsq, 16, 1e-5, bankS, bankQ2)
        emit_xn(p, C, ln, xb, a_bc, b_bc, bankBC, 16)
        for c in range(16):
            p.act(gxn.h[:, c * PW:(c + 1) * PW], xb.h[:, c * PW:(c + 1) * PW], AF.Gelu,
                  r=[xb], w=[gxn])

        for lc in range(8):
            wd = wdbuf[lc % 2]
            p.dma("gpsimd", wd.h[:, 0:2048], W["dn_w"].h.ap()[lc * 128:(lc + 1) * 128, :],
                  r=[W["dn_w"]], w=[wd])
            pairs = [(wd.h[:, c * 128:(c + 1) * 128], xb.h[:, c * PW:(c + 1) * PW]) for c in range(16)]
            p.mm(bankH.h[:, 0:PW], pairs, r=[wd, xb], w=[bankH])
            p.act(gh.h[:, lc * PW:(lc + 1) * PW], bankH.h[:, 0:PW], AF.Gelu,
                  bias=C["dn_bc"].h[:, lc:lc + 1], r=[bankH, C["dn_bc"]], w=[gh])

        pairs = [(rwsb.h[:, c * 8:(c + 1) * 8], xb.h[:, c * PW:(c + 1) * PW]) for c in range(16)]
        p.mm(bankH.h[0:8, 0:PW], pairs, r=[rwsb, xb], w=[bankH])
        p.ts("vector", lg.h[:], bankH.h[0:8, 0:PW], C["rt_bc"].h[0:8, 0:1], C["lcost"].h[0:8, 0:1],
             ALU.add, ALU.subtract, r=[bankH, C["rt_bc"], C["lcost"]], w=[lg])
        p.copy("gpsimd", lgbf.h[:], lg.h[:], r=[lg], w=[lgbf])
        for tt_ in range(4):
            p.transpose(bankT.h[:, 0:8], lgbf.h[:, tt_ * 128:(tt_ + 1) * 128], C["ident"].h[0:8, 0:8],
                        r=[lgbf, C["ident"]], w=[bankT])
            p.copy("scalar", lt.h[:], bankT.h[:, 0:8], r=[bankT], w=[lt])
            nm = cols.h[:, 0:1]
            p.op("vector", lambda E, a=nm: E.reduce_max(a, lt.h[:], axis=AXX, negate=True),
                 r=[lt], w=[cols])
            ex = pr8.h[:, 0:8]
            p.act(ex, lt.h[:], AF.Exp, bias=nm, r=[lt, cols], w=[pr8])
            sm = cols.h[:, 1:2]
            p.op("vector", lambda E, a=sm, b=ex: E.reduce_sum(a, b, axis=AXX), r=[pr8], w=[cols])
            rc = cols.h[:, 2:3]
            p.op("vector", lambda E, a=rc, b=sm: E.reciprocal(a, b), r=[cols], w=[cols])
            probs = pr8.h[:, 0:8]
            p.ts("vector", probs, probs, rc, None, ALU.mult, None, r=[cols, pr8], w=[pr8])
            m1 = cols.h[:, 3:4]
            p.op("vector", lambda E, a=m1, b=probs: E.reduce_max(a, b, axis=AXX), r=[pr8], w=[cols])
            mask1 = pr8.h[:, 8:16]
            p.ts("vector", mask1, probs, m1, None, ALU.is_equal, None, r=[pr8, cols], w=[pr8])
            p2v = pr8.h[:, 16:24]
            p.stt("vector", p2v, mask1, -1e9, probs, ALU.mult, ALU.add, r=[pr8], w=[pr8])
            m2 = cols.h[:, 4:5]
            p.op("vector", lambda E, a=m2, b=p2v: E.reduce_max(a, b, axis=AXX), r=[pr8], w=[cols])
            mask2 = pr8.h[:, 16:24]
            p.ts("vector", mask2, p2v, m2, None, ALU.is_equal, None, r=[pr8, cols], w=[pr8])
            e1c = cols.h[:, 5:6]
            e2c = cols.h[:, 6:7]
            p.act(e1c, m1, AF.Exp, r=[cols], w=[cols])
            p.act(e2c, m2, AF.Exp, r=[cols], w=[cols])
            dn = cols.h[:, 7:8]
            p.tt("vector", dn, e1c, e2c, ALU.add, r=[cols], w=[cols])
            rd = cols.h[:, 8:9]
            p.op("vector", lambda E, a=rd, b=dn: E.reciprocal(a, b), r=[cols], w=[cols])
            g1 = cols.h[:, 9:10]
            g2 = cols.h[:, 10:11]
            p.tt("vector", g1, e1c, rd, ALU.mult, r=[cols], w=[cols])
            p.tt("vector", g2, e2c, rd, ALU.mult, r=[cols], w=[cols])
            wex = pr8.h[:, 8:16]
            p.ts("vector", wex, mask1, g1, None, ALU.mult, None, r=[pr8, cols], w=[pr8])
            p.stt("vector", wex, mask2, g2, wex, ALU.mult, ALU.add, r=[pr8, cols], w=[pr8])
            p.copy("gpsimd", gb1.h[:, 0:8], wex, r=[pr8], w=[gb1])
            p.transpose(bankT.h[0:8, 8:136], gb1.h[:, 0:8], C["ident"].h[:], r=[gb1, C["ident"]], w=[bankT])
            p.copy("scalar", wexpT.h[0:8, tt_ * 128:(tt_ + 1) * 128], bankT.h[0:8, 8:136],
                   r=[bankT], w=[wexpT])

        if panel == 0:
            p.dbg("gh", gh.h[:], gh, (128, 8 * PW))
            p.dbg("lg", lg.h[:], lg, (8, PW))
            p.dbg("wexpT", wexpT.h[0:8, :], wexpT, (8, PW))
        for e in range(8):
            ne = HID[e] // 128
            p.mm(bankR.h[:, 0:PW], [(C["sel8"].h[0:8, e * 128:(e + 1) * 128], wexpT.h[0:8, 0:PW])],
                 r=[C["sel8"], wexpT], w=[bankR])
            p.copy("scalar", w_bc.h[:], bankR.h[:, 0:PW], r=[bankR], w=[w_bc])
            if panel == 0 and e == 0:
                p.dbg("wbc0", w_bc.h[:], w_bc, (128, PW))
            if panel == 0:
                p.act(gb1.h[:, 0:ne], C[f"e{e}_b1c"].h[:, 0:ne], AF.Gelu,
                      r=[C[f"e{e}_b1c"]], w=[gb1])
                p.dma("gpsimd", b2row.h[0:1, :], W[f"e{e}_b2r"].h.ap()[0:1, :],
                      r=[W[f"e{e}_b2r"]], w=[b2row])
            for jb in range(ne // 4):
                wb1 = w1buf[jb % 2]
                p.dma("gpsimd", wb1.h[:],
                      W[f"e{e}_w1"].h.ap()[jb * 512:(jb + 1) * 512, :].rearrange(
                          "(t p) k -> p t k", p=128),
                      r=[W[f"e{e}_w1"]], w=[wb1])
                for jj in range(4):
                    j = jb * 4 + jj
                    w1s = wb1.h[:, jj * 1024:(jj + 1) * 1024]
                    pairs = [(w1s[:, lc * 128:(lc + 1) * 128], gh.h[:, lc * PW:(lc + 1) * PW])
                             for lc in range(8)]
                    p.mm(bankMID.h[:, 0:PW], pairs, r=[wb1, gh], w=[bankMID])
                    p.act(gmid.h[:, j * PW:(j + 1) * PW], bankMID.h[:, 0:PW], AF.Gelu,
                          bias=C[f"e{e}_b1c"].h[:, j:j + 1], r=[bankMID, C[f"e{e}_b1c"]], w=[gmid])
            if panel == 0 and e == 0:
                p.dbg("gmid0", gmid.h[:, 0:8 * PW], gmid, (128, 8 * PW))
            for lcb in range(4):
                wb2 = w2buf[lcb % 2]
                p.dma("gpsimd", wb2.h[:, 0:2 * ne * 128],
                      W[f"e{e}_w2"].h.ap()[lcb * 256:(lcb + 1) * 256, :].rearrange(
                          "(t p) k -> p t k", p=128),
                      r=[W[f"e{e}_w2"]], w=[wb2])
                for ll in range(2):
                    lc = lcb * 2 + ll
                    w2s = wb2.h[:, ll * ne * 128:(ll + 1) * ne * 128]
                    if panel == 0:
                        prs = [(gb1.h[:, i:i + 1], w2s[:, i * 128:(i + 1) * 128]) for i in range(ne)]
                        p.mm(bankR.h[0:1, 0:128], prs, r=[gb1, wb2], w=[bankR])
                        p.ts("vector", crows.h[0:1, e * 1024 + lc * 128:e * 1024 + (lc + 1) * 128],
                             bankR.h[0:1, 0:128], -1.0, None, ALU.mult, None,
                             r=[bankR], w=[crows])
                        p.tt("vector", ctot.h[0:1, lc * 128:(lc + 1) * 128],
                             ctot.h[0:1, lc * 128:(lc + 1) * 128], bankR.h[0:1, 0:128], ALU.add,
                             r=[bankR, ctot], w=[ctot])
                        p.tt("vector", ctot.h[0:1, lc * 128:(lc + 1) * 128],
                             ctot.h[0:1, lc * 128:(lc + 1) * 128], b2row.h[0:1, lc * 128:(lc + 1) * 128],
                             ALU.add, r=[b2row, ctot], w=[ctot])
                    pairs = [(w2s[:, i * 128:(i + 1) * 128], gmid.h[:, i * PW:(i + 1) * PW])
                             for i in range(ne)]
                    pairs.append((crows.h[0:1, e * 1024 + lc * 128:e * 1024 + (lc + 1) * 128], onespw.h[0:1, :]))
                    p.mm(bankF.h[:, 0:PW], pairs, r=[wb2, gmid, crows, onespw], w=[bankF])
                    p.tt("vector", tmpf.h[:], bankF.h[:, 0:PW], w_bc.h[:], ALU.mult,
                         r=[bankF, w_bc], w=[tmpf])
                    asl = acc.h[:, lc * PW:(lc + 1) * PW]
                    if e == 0:
                        p.copy("vector", asl, tmpf.h[:], r=[tmpf], w=[acc])
                    else:
                        p.tt("vector", asl, asl, tmpf.h[:], ALU.add, r=[tmpf, acc], w=[acc])
        if panel == 0:
            p.dbg("acc0", acc.h[:], acc, (128, 8 * PW))
            p.dbg("crows", crows.h[0:1, :], crows, (1, 8 * 1024))
            p.copy("gpsimd", ctbf.h[:], ctot.h[:], r=[ctot], w=[ctbf])
            for lc in range(8):
                p.mm(bankR.h[:, lc:lc + 1],
                     [(ctbf.h[0:1, lc * 128:(lc + 1) * 128], onespw.h[0:1, 0:1])],
                     r=[ctbf, onespw], w=[bankR])
            p.copy("vector", ctcols.h[:], bankR.h[:, 0:8], r=[bankR], w=[ctcols])

        for sc in range(8):
            wb1 = w1buf[sc % 2]
            p.dma("gpsimd", wb1.h[:, 0:1024],
                  W["sh_w1"].h.ap()[sc * 128:(sc + 1) * 128, :],
                  r=[W["sh_w1"]], w=[wb1])
            pairs = [(wb1.h[:, lc * 128:(lc + 1) * 128], gh.h[:, lc * PW:(lc + 1) * PW])
                     for lc in range(8)]
            p.mm(bankMID.h[:, 0:PW], pairs, r=[wb1, gh], w=[bankMID])
            p.act(gmid.h[:, sc * PW:(sc + 1) * PW], bankMID.h[:, 0:PW], AF.Gelu,
                  bias=C["sh_b1c"].h[:, sc:sc + 1], r=[bankMID, C["sh_b1c"]], w=[gmid])
        for lc in range(8):
            wb1 = w1buf[lc % 2]
            p.dma("gpsimd", wb1.h[:, 0:1024],
                  W["sh_w2"].h.ap()[lc * 128:(lc + 1) * 128, :],
                  r=[W["sh_w2"]], w=[wb1])
            pairs = [(wb1.h[:, sc * 128:(sc + 1) * 128], gmid.h[:, sc * PW:(sc + 1) * PW])
                     for sc in range(8)]
            p.mm(bankF.h[:, 0:PW], pairs, r=[wb1, gmid], w=[bankF])
            p.ts("vector", tmpf.h[:], bankF.h[:, 0:PW], C["sh_b2c"].h[:, lc:lc + 1], 0.1,
                 ALU.add, ALU.mult, r=[bankF, C["sh_b2c"]], w=[tmpf])
            asl = acc.h[:, lc * PW:(lc + 1) * PW]
            p.tt("vector", asl, asl, tmpf.h[:], ALU.add, r=[tmpf, acc], w=[acc])

        for lc in range(8):
            p.ts("vector", accbf.h[:, (8 + lc) * PW:(9 + lc) * PW], acc.h[:, lc * PW:(lc + 1) * PW],
                 ctcols.h[:, lc:lc + 1], None, ALU.add, None, r=[acc, ctcols], w=[accbf])

        for oc in range(16):
            wup = w1buf[oc % 2]
            wco = wdbuf[oc % 2]
            p.dma("gpsimd", wup.h[:, 0:1024],
                  W["up_w"].h.ap()[oc * 128:(oc + 1) * 128, :],
                  r=[W["up_w"]], w=[wup])
            p.dma("gpsimd", wco.h[:, 0:2048],
                  W["co_w"].h.ap()[oc * 128:(oc + 1) * 128, :],
                  r=[W["co_w"]], w=[wco])
            pairs = [(wup.h[:, lc * 128:(lc + 1) * 128], accbf.h[:, (8 + lc) * PW:(9 + lc) * PW])
                     for lc in range(8)]
            pairs += [(wco.h[:, c * 128:(c + 1) * 128], gxn.h[:, c * PW:(c + 1) * PW])
                      for c in range(16)]
            p.mm(bankH.h[:, 0:PW], pairs, r=[wup, wco, accbf, gxn], w=[bankH])
            p.ts("vector", tmpf.h[:], bankH.h[:, 0:PW], C["up_bc"].h[:, oc:oc + 1],
                 C["co_bc"].h[:, oc:oc + 1], ALU.add, ALU.add, r=[bankH, C["up_bc"], C["co_bc"]],
                 w=[tmpf])
            p.copy("gpsimd", ysb.h[:, oc * PW:(oc + 1) * PW], tmpf.h[:], r=[tmpf], w=[ysb])

        emit_post_ln_residual(p, C, ln, ysb, xsq, x_out, x_out.h.ap(), panel,
                              a_bc, b_bc, bankS, bankQ2, bankBC, t_rot)

    p.fence()
    st.close()


def asc(a):
    return np.ascontiguousarray(a)


def u8_view(int32_mat):
    return asc(asc(int32_mat).view(np.uint8)[:, 0::4])


def tileKN(wT, tile_n=128):
    K, N = wT.shape
    t = wT.reshape(K // 128, 128, N // tile_n, tile_n).transpose(2, 1, 0, 3)
    return asc(t.reshape(N // tile_n * 128, (K // 128) * tile_n))


def host_prep(inp):
    w = {}
    bf = lambda a: a.astype(bfloat16)
    w["wqkvuT"] = tileKN(bf(inp["L0_attn_W_qkvu"].T))
    w["woutT"] = tileKN(bf(inp["L0_attn_W_out"].T))
    w["slope_row"] = asc(inp["L0_attn_slope_rate"].reshape(1, 16))
    w["gateT8"] = tileKN(u8_view(asc(inp["L0_ffn_gate_w"].T)))
    w["upT8"] = tileKN(u8_view(asc(inp["L0_ffn_up_w"].T)))
    w["downT8"] = tileKN(u8_view(asc(inp["L0_ffn_down_w"].T)))
    w["gsc"] = asc(inp["L0_ffn_gate_scale"].reshape(64, 128).T)
    w["usc"] = asc(inp["L0_ffn_up_scale"].reshape(64, 128).T)
    w["dsc"] = asc(inp["L0_ffn_down_scale"].reshape(16, 128).T)
    for nm in ("Wq", "Wk"):
        w[f"w{nm[1:].lower()}T"] = tileKN(bf(inp[f"L1_attn_{nm}"].T))
    for nm in ("Wv", "Wg"):
        w[f"w{nm[1:].lower()}T"] = tileKN(bf(inp[f"L1_attn_{nm}"].T), tile_n=512)
    w["woT"] = tileKN(bf(inp["L1_attn_Wo"].T))
    w["ws1T"] = asc(inp["L1_attn_Ws1"].T)
    w["ws2T"] = asc(inp["L1_attn_Ws2"].T)
    w["bs2c"] = asc(inp["L1_attn_bs2"].reshape(8, 128).T)
    w["gnw16"] = asc(np.broadcast_to(inp["L1_attn_g_norm_weight"], (16, 128)).reshape(1, 2048))
    w["dn_w"] = tileKN(bf(inp["L1_ffn_down_w"]))
    w["dn_bc"] = asc(inp["L1_ffn_down_b"].reshape(8, 128).T)
    w["up_w"] = tileKN(bf(inp["L1_ffn_up_w"]))
    w["up_bc"] = asc(inp["L1_ffn_up_b"].reshape(16, 128).T)
    w["rt_w"] = asc(inp["L1_ffn_router_w"])
    w["rt_bc"] = asc(inp["L1_ffn_router_b"].reshape(8, 1))
    w["sh_w1"] = tileKN(bf(inp["L1_ffn_shared_w1"]))
    w["sh_w2"] = tileKN(bf(inp["L1_ffn_shared_w2"]))
    w["sh_b1c"] = asc(inp["L1_ffn_shared_b1"].reshape(8, 128).T)
    w["sh_b2c"] = asc(inp["L1_ffn_shared_b2"].reshape(8, 128).T)
    w["co_w"] = tileKN(bf(inp["L1_ffn_core_w"]))
    w["co_bc"] = asc(inp["L1_ffn_core_b"].reshape(16, 128).T)
    HID = [1024, 2048, 3072, 4096, 1024, 2048, 3072, 4096]
    for e, hd in enumerate(HID):
        w[f"e{e}_w1"] = tileKN(bf(inp[f"L1_ffn_e{e}_w1"]))
        w[f"e{e}_w2"] = tileKN(bf(inp[f"L1_ffn_e{e}_w2"]))
        w[f"e{e}_b1c"] = asc(inp[f"L1_ffn_e{e}_b1"].reshape(hd // 128, 128).T)
        w[f"e{e}_b2r"] = asc(inp[f"L1_ffn_e{e}_b2"].reshape(1, 1024))
    w["ident"] = np.eye(128, dtype=bfloat16)
    w["arr_row"] = np.arange(1, 257, dtype=np.float32).reshape(1, 256)
    kc = np.zeros((128, 2), np.float32)
    for pp in range(128):
        for j in range(2):
            kc[pp, j] = 256.0 - (j * 128 + pp + 1)
    w["karr_cols"] = kc
    dm = np.zeros((128, 512), np.float32)
    for a in range(128):
        for b in range(256):
            for t in range(2):
                v = b - a - 128 * t
                dm[a, t * 256 + b] = v if v >= 0 else 1e9
    w["Dm"] = dm
    pos = np.arange(T, dtype=np.float64)
    inv_freq = 1.0 / 10000.0 ** (np.arange(0, DH, 2) / DH)
    ang = pos[None, :] * inv_freq[:, None]
    scale = DH ** -0.5
    w["cosq"] = (np.cos(ang) * scale).astype(bfloat16)
    w["sinq"] = (np.sin(ang) * scale).astype(bfloat16)
    w["cosk"] = np.cos(ang).astype(bfloat16)
    w["sink"] = np.sin(ang).astype(bfloat16)
    cm = np.zeros((128, 4 * 512), np.float32)
    for off in range(4):
        for r in range(128):
            for c in range(512):
                cm[r, off * 512 + c] = 1.0 if c >= r + off * 128 else 0.0
    w["cmask"] = cm.astype(bfloat16)
    sel = np.zeros((8, 1024), np.float32)
    for e in range(8):
        sel[e, e * 128:(e + 1) * 128] = 1.0
    w["sel8"] = sel.astype(bfloat16)
    cost = np.array([2 * 1024 * hd for hd in HID], np.float32) * 1e-07
    w["lcost"] = cost.reshape(8, 1)
    return w


W_DTYPES = {
    "gateT8": U8, "upT8": U8, "downT8": U8,
    "wqkvuT": BF16, "woutT": BF16,
    "wqT": BF16, "wkT": BF16, "wvT": BF16, "wgT": BF16, "woT": BF16,
    "dn_w": BF16, "up_w": BF16, "co_w": BF16, "sh_w1": BF16, "sh_w2": BF16,
    "ident": BF16, "cosq": BF16, "sinq": BF16, "cosk": BF16, "sink": BF16,
    "cmask": BF16, "sel8": BF16,
}
W_DTYPES.update({f"e{e}_{nm}": BF16 for e in range(8) for nm in ("w1", "w2")})


def declare_weights(p, names, host_w):
    W = {}
    for n in names:
        arr = host_w[n]
        dt = W_DTYPES.get(n, F32)
        W[n] = p.dram(n, arr.shape, dt, kind="ExternalInput")
    return W


def make_consts(p, C_stack):
    C = {}
    C["ones_col"] = p.sb("c_onescol", [128, 1], BF16, C_stack)
    C["ones_row"] = p.sb("c_onesrow", [1, 128], BF16, C_stack)
    C["ones_pw"] = p.sb("c_onespw", [1, PW], BF16, C_stack)
    C["neg128_row"] = p.sb("c_n128row", [1, 128], BF16, C_stack)
    p.memset("vector", C["ones_col"].h[:], 1.0, w=[C["ones_col"]])
    p.memset("vector", C["ones_row"].h[:], 1.0, w=[C["ones_row"]])
    p.memset("vector", C["ones_pw"].h[:], 1.0, w=[C["ones_pw"]])
    p.memset("vector", C["neg128_row"].h[:], -128.0, w=[C["neg128_row"]])
    return C


def load_const_tiles(p, C, W, C_stack, need):
    shapes = {
        "ident": (128, 128), "Dm": (128, 512), "arr_row": (1, 256), "karr_cols": (128, 2),
        "cosq": (64, 2048), "sinq": (64, 2048), "cosk": (64, 2048), "sink": (64, 2048),
        "cmask": (128, 2048), "sel8": (8, 1024), "gnw16": (1, 2048), "lcost": (8, 1),
        "gsc": (128, 64), "usc": (128, 64), "dsc": (128, 16), "bs2c": (128, 8),
        "dn_bc": (128, 8), "up_bc": (128, 16), "rt_bc": (8, 1),
        "sh_b1c": (128, 8), "sh_b2c": (128, 8), "co_bc": (128, 16),
        "e0_b1c": (128, 8), "e1_b1c": (128, 16), "e2_b1c": (128, 24), "e3_b1c": (128, 32),
        "e4_b1c": (128, 8), "e5_b1c": (128, 16), "e6_b1c": (128, 24), "e7_b1c": (128, 32),
    }
    for n in need:
        if n not in W:
            continue
        sh = shapes[n]
        dt = W_DTYPES.get(n, F32)
        C[n] = p.sb("ct_" + n, list(sh), dt, C_stack)
        p.dma("sync", C[n].h[:], W[n].h.ap()[:, :], r=[W[n]], w=[C[n]])



PHASE_WEIGHTS = {
    "A": ["wqkvuT", "woutT", "slope_row", "ident", "Dm", "arr_row", "karr_cols"],
    "B": ["gateT8", "upT8", "downT8", "gsc", "usc", "dsc"],
    "C1": ["wqT", "wkT", "wvT", "wgT", "ws1T", "ws2T", "bs2c", "ident",
           "cosq", "sinq", "cosk", "sink"],
    "C2": ["woT", "gnw16", "ident", "cmask"],
    "D": ["dn_w", "dn_bc", "up_w", "up_bc", "rt_w", "rt_bc", "lcost", "sh_w1", "sh_w2",
          "sh_b1c", "sh_b2c", "co_w", "co_bc", "sel8", "ident"]
    + [f"e{e}_{nm}" for e in range(8) for nm in ("w1", "w2", "b1c", "b2r")],
}

PHASE_CONSTS = {
    "A": ["ident", "Dm", "arr_row", "karr_cols"],
    "B": ["gsc", "usc", "dsc"],
    "C1": ["ident", "cosq", "sinq", "cosk", "sink", "bs2c"],
    "C2": ["ident", "cmask", "gnw16"],
    "D": ["ident", "dn_bc", "up_bc", "rt_bc", "lcost", "sh_b1c", "sh_b2c", "co_bc", "sel8"]
    + [f"e{e}_b1c" for e in range(8)],
}


def build_program(phases, host_w, dbg_scratch=False, dump_mids=False):
    nc = bass.Bass("TRN2", target_bir_lowering=False)
    stack = ExitStack()
    p = Prog(nc, stack)

    x0 = p.dram("x0T", (D, T), F32, kind="ExternalInput")
    out = p.dram("outT", (D, T), F32, kind="ExternalOutput")
    names = []
    for ph in phases:
        names += PHASE_WEIGHTS[ph]
    W = declare_weights(p, sorted(set(names)), host_w)

    C = make_consts(p, stack)
    load_const_tiles(p, C, W, stack, ["ident"])

    S = {}
    if "C1" in phases or "C2" in phases:
        sk = "ExternalOutput" if dbg_scratch else "Internal"
        S["qfD"] = p.dram("qfD", (2048, 2048), BF16, kind=sk)
        S["kfD"] = p.dram("kfD", (2048, 2048), BF16, kind=sk)
        S["vtD"] = p.dram("vtD", (2048, 2048), BF16, kind=sk)
        S["gtD"] = p.dram("gtD", (2048, 2048), BF16, kind=sk)
        S["esfD"] = p.dram("esfD", (1024, 2048), BF16, kind=sk)
        S["estD"] = p.dram("estD", (2048, 1024), BF16, kind=sk)
        S["enfD"] = p.dram("enfD", (1024, 2048), BF16, kind=sk)
        S["entD"] = p.dram("entD", (2048, 1024), BF16, kind=sk)

    cur = x0
    residual_phases = [ph for ph in phases if ph != "C1"]
    for i, ph in enumerate(phases):
        if ph == "C1":
            phase_l1_attn_proj(p, C, W, cur, S)
            continue
        last = (residual_phases[-1] == ph)
        mk = "ExternalOutput" if dump_mids else "Internal"
        dst = out if last else p.dram(f"xmid{i}", (D, T), F32, kind=mk)
        if ph == "A":
            phase_l0_attn(p, C, W, cur, dst)
        elif ph == "B":
            phase_l0_ffn(p, C, W, cur, dst)
        elif ph == "C2":
            phase_l1_attn_core(p, C, W, cur, dst, S)
        elif ph == "D":
            phase_l1_ffn(p, C, W, cur, dst)
        else:
            raise NotImplementedError(ph)
        cur = dst

    p.fence()
    return nc, stack


PHASES = ["A", "B", "C1", "C2", "D"]


def kernel(**inputs):
    host_w = host_prep(inputs)
    nc, stack = build_program(PHASES, host_w)
    names = sorted({n for ph in PHASES for n in PHASE_WEIGHTS[ph]})
    in_maps = []
    for core in range(NCORES):
        m = {"x0T": asc(inputs["L0_attn_x"][core].T)}
        for n in names:
            m[n] = host_w[n]
        in_maps.append(m)
    res = run_bass_kernel_spmd(nc, in_maps, core_ids=list(range(NCORES)))
    stack.close()
    outs = [r["outT"] for r in res.results]
    return np.stack([o.T for o in outs], axis=0).astype(np.float32)


# version 07
# speedup vs baseline: 1.2477x; 1.0131x over previous
import numpy as np
import ml_dtypes
from contextlib import ExitStack

import concourse.bass as bass
import concourse.mybir as mybir
from concourse.bass_utils import run_bass_kernel_spmd

F32 = mybir.dt.float32
BF16 = mybir.dt.bfloat16
U8 = mybir.dt.uint8
AF = mybir.ActivationFunctionType
ALU = mybir.AluOpType
AXX = mybir.AxisListType.X
bfloat16 = ml_dtypes.bfloat16

DBG = {}
PARANOIA = 0
DMA_WINDOW = 0
DMA_CHAIN = 1
T = 2048
D = 2048
NCORES = 8
PW = 512
NP = T // PW
NH = 16
DH = 128

ENGS = ("tensor", "vector", "scalar", "gpsimd", "sync")
DMA_Q = {"sync": "sy_d", "gpsimd": "gp_d", "scalar": "sc_d"}


class Buf:
    __slots__ = ("h", "name", "w", "r")

    def __init__(self, h, name):
        self.h = h
        self.name = name
        self.w = None
        self.r = {}


class Prog:
    def __init__(self, nc, stack):
        self.nc = nc
        self.stack = stack
        self.sems = {}
        self.cnt = {}
        for s in ENGS + tuple(DMA_Q.values()):
            if s == "sync":
                continue
            self.sems[s] = stack.enter_context(nc.semaphore("S_" + s))
            self.cnt[s] = 0
        self.waited = {e: {} for e in ENGS}
        self.dma_log = {q: [] for q in DMA_Q.values()}
        self.dma_hist = []
        self.last_dma = None
        self.banks = [Buf(stack.enter_context(nc.psum_tensor(f"pbank{i}", [128, 512], F32)), f"pbank{i}")
                      for i in range(7)]
        self.bankT = Buf(stack.enter_context(nc.psum_tensor("pbankT", [128, 1024], BF16)), "pbankT")

    def sb(self, name, shape, dtype, st=None):
        h = (st or self.stack).enter_context(self.nc.sbuf_tensor(name, shape, dtype))
        return Buf(h, name)

    def dram(self, name, shape, dtype, kind="Internal"):
        h = self.nc.dram_tensor(name, list(shape), dtype, kind=kind)
        return Buf(h, name)

    def _wait(self, eng, tok):
        if tok is None:
            return
        s, v = tok
        w = self.waited[eng]
        if w.get(s, 0) >= v:
            return
        getattr(self.nc, eng).wait_ge(self.sems[s], v)
        w[s] = v

    def op(self, eng, fn, r=(), w=(), sem=None, inc=1):
        sem = sem or eng
        is_dma = sem in ("sy_d", "gp_d", "sc_d")
        if PARANOIA == 2 or (PARANOIA == 3 and is_dma) or (PARANOIA == 4 and not is_dma):
            for s, v in self.cnt.items():
                if v:
                    self._wait(eng, (s, v))
        elif PARANOIA in (5, 6, 7) and is_dma:
            for s, v in self.cnt.items():
                if not v:
                    continue
                s_dma = s in ("sy_d", "gp_d", "sc_d")
                if ((PARANOIA == 5 and not s_dma) or (PARANOIA == 6 and s_dma)
                        or (PARANOIA == 7 and s_dma and s != sem)):
                    self._wait(eng, (s, v))
        for b in r:
            self._wait(eng, b.w)
        for b in w:
            self._wait(eng, b.w)
            for s, v in b.r.items():
                self._wait(eng, (s, v))
        inst = fn(getattr(self.nc, eng))
        self.cnt[sem] += inc
        inst.then_inc(self.sems[sem], inc)
        tok = (sem, self.cnt[sem])
        for b in r:
            b.r[tok[0]] = max(b.r.get(tok[0], 0), tok[1])
        for b in w:
            b.w = tok
            b.r = {}
        return tok

    def dbg(self, name, src_ap, src_buf, shape):
        if name not in DBG:
            return
        b = self.dram("dbg_" + name, shape, F32, kind="ExternalOutput")
        self.dma("gpsimd", b.h.ap()[:, :], src_ap, r=[src_buf], w=[b])

    def dma(self, eng, out_ap, in_ap, r=(), w=(), **kw):
        q = DMA_Q[eng]
        log = self.dma_log[q]
        if DMA_CHAIN == 2 and self.dma_hist:
            lq, lt = self.dma_hist[-1]
            if q == "gp_d" and lq == "gp_d":
                if len(self.dma_hist) >= 2:
                    self._wait(eng, self.dma_hist[-2][1])
            else:
                self._wait(eng, lt)
        elif DMA_CHAIN and self.last_dma is not None:
            self._wait(eng, self.last_dma)
        elif DMA_WINDOW and len(log) >= DMA_WINDOW:
            self._wait(eng, (q, log[-DMA_WINDOW]))
        tok = self.op(eng, lambda E: E.dma_start(out=out_ap, in_=in_ap, **kw),
                      r=r, w=w, sem=q, inc=16)
        log.append(tok[1])
        self.dma_hist.append((q, tok))
        self.last_dma = tok
        return tok

    def fence(self):
        for e in ENGS:
            for s, v in self.cnt.items():
                if v:
                    self._wait(e, (s, v))

    def tt(self, eng, out, a, b, alu, r, w):
        return self.op(eng, lambda E: E.tensor_tensor(out, a, b, op=alu), r=r, w=w)

    def ts(self, eng, out, a, s1, s2, op0, op1, r, w):
        if s2 is None:
            return self.op(eng, lambda E: E.tensor_scalar(out, a, s1, None, op0=op0), r=r, w=w)
        return self.op(eng, lambda E: E.tensor_scalar(out, a, s1, s2, op0=op0, op1=op1), r=r, w=w)

    def stt(self, eng, out, in0, scalar, in1, op0, op1, r, w):
        return self.op(eng, lambda E: E.scalar_tensor_tensor(out, in0, scalar, in1, op0, op1), r=r, w=w)

    def act(self, out, in_, func, r, w, bias=0.0, scale=1.0, accum_out=None):
        return self.op("scalar",
                       lambda E: E.activation(out=out, in_=in_, func=func, bias=bias, scale=scale,
                                              accum_out=accum_out),
                       r=r, w=w)

    def copy(self, eng, out, in_, r, w):
        if eng == "scalar":
            return self.op(eng, lambda E: E.copy(out, in_), r=r, w=w)
        return self.op(eng, lambda E: E.tensor_copy(out, in_), r=r, w=w)

    def memset(self, eng, ap, val, w):
        return self.op(eng, lambda E: E.memset(ap, val), r=(), w=w)

    def mm(self, out_ap, pairs, r, w):
        def fn(E):
            inst = None
            n = len(pairs)
            for i, (lh, rh) in enumerate(pairs):
                inst = E.matmul(out_ap, lh, rh, start=(i == 0), stop=(i == n - 1))
            return inst
        return self.op("tensor", fn, r=r, w=w)

    def mm_acc(self, out_ap, pairs, r, w, start, stop):
        def fn(E):
            inst = None
            n = len(pairs)
            for i, (lh, rh) in enumerate(pairs):
                inst = E.matmul(out_ap, lh, rh, start=(start and i == 0), stop=(stop and i == n - 1))
            return inst
        return self.op("tensor", fn, r=r, w=w)

    def transpose(self, out_ap, in_ap, ident_ap, r, w):
        return self.op("tensor", lambda E: E.transpose(out_ap, in_ap, ident_ap), r=r, w=w)



class LNStats:
    def __init__(self, p, st, tag):
        self.rows = p.sb(f"{tag}_rows", [1, PW * 4], F32, st)
        self.rows_bf = p.sb(f"{tag}_rowsbf", [1, PW * 4], BF16, st)
        self.eps = p.sb(f"{tag}_eps", [1, 1], F32, st)

    def s(self, i, n=PW):
        return self.rows.h[0:1, i * PW:i * PW + n]

    def sbf(self, i, n=PW):
        return self.rows_bf.h[0:1, i * PW:i * PW + n]

    def mu(self):
        return self.s(0)

    def a(self):
        return self.s(1)

    def b(self):
        return self.s(2)

    def sn(self):
        return self.s(3)

    def a_bf(self):
        return self.sbf(0)

    def b_bf(self):
        return self.sbf(1)

    def sn_bf(self):
        return self.sbf(2)


def emit_ln_stats(p, C, ln, xb, xsq, nchunks, eps, bank_s, bank_q, want_sn=False):
    nd = nchunks * 128
    ones = C["ones_col"]
    for c in range(nchunks):
        xb_c = xb.h[:, c * PW:(c + 1) * PW]
        sq_c = xsq.h[:, (c % 2) * PW:((c % 2) + 1) * PW]
        p.act(sq_c, xb_c, AF.Square, r=[xb], w=[xsq])
        p.mm_acc(bank_s.h[0:1, 0:PW], [(ones.h[:, 0:1], xb_c)], r=[ones, xb], w=[bank_s],
                 start=(c == 0), stop=(c == nchunks - 1))
        p.mm_acc(bank_q.h[0:1, 0:PW], [(ones.h[:, 0:1], sq_c)], r=[ones, xsq], w=[bank_q],
                 start=(c == 0), stop=(c == nchunks - 1))
    rows = ln.rows
    p.ts("vector", ln.s(0), bank_s.h[0:1, 0:PW], 1.0 / nd, None, ALU.mult, None, r=[bank_s], w=[rows])
    p.act(ln.s(3), ln.s(0), AF.Square, r=[rows], w=[rows])
    p.stt("vector", ln.s(1), bank_q.h[0:1, 0:PW], 1.0 / nd, ln.s(3), ALU.mult, ALU.subtract,
          r=[bank_q, rows], w=[rows])
    p.memset("vector", ln.eps.h[:], eps, w=[ln.eps])
    p.act(ln.s(3), ln.s(1), AF.Ln, bias=ln.eps.h[0:1, 0:1], r=[rows, ln.eps], w=[rows])
    p.act(ln.s(1), ln.s(3), AF.Exp, scale=-0.5, r=[rows], w=[rows])
    p.stt("vector", ln.s(2), ln.s(0), -1.0, ln.s(1), ALU.mult, ALU.mult, r=[rows], w=[rows])
    p.copy("gpsimd", ln.a_bf(), ln.a(), r=[rows], w=[ln.rows_bf])
    p.copy("gpsimd", ln.b_bf(), ln.b(), r=[rows], w=[ln.rows_bf])
    if want_sn:
        p.stt("vector", ln.s(3), bank_s.h[0:1, 0:PW], 1.0, ln.s(1), ALU.mult, ALU.mult,
              r=[bank_s, rows], w=[rows])
        p.stt("vector", ln.s(3), ln.s(2), float(nd), ln.s(3), ALU.mult, ALU.add, r=[rows], w=[rows])
        p.copy("gpsimd", ln.sn_bf(), ln.sn(), r=[rows], w=[ln.rows_bf])


def emit_bcast(p, C, row_bf_ap, rowsrc_buf, bank_bc, out_buf, n=PW):
    sl = bank_bc.h[:, 0:n]
    p.mm(sl, [(C["ones_row"].h[0:1, 0:128], row_bf_ap)], r=[C["ones_row"], rowsrc_buf], w=[bank_bc])
    p.copy("scalar", out_buf.h[:, 0:n], sl, r=[bank_bc], w=[out_buf])


def emit_xn(p, C, ln, xb, a_bc, b_bc, bank_bc, nchunks):
    emit_bcast(p, C, ln.a_bf(), ln.rows_bf, bank_bc, a_bc)
    emit_bcast(p, C, ln.b_bf(), ln.rows_bf, bank_bc, b_bc)
    for c in range(nchunks):
        sl = xb.h[:, c * PW:(c + 1) * PW]
        tmp = C["tmp_f32"]
        p.tt("gpsimd", tmp.h[:, 0:PW], sl, a_bc.h[:], ALU.mult, r=[xb, a_bc], w=[tmp])
        p.tt("vector", sl, tmp.h[:, 0:PW], b_bc.h[:], ALU.add, r=[tmp, b_bc], w=[xb])


def emit_post_ln_residual(p, C, ln2, y, ysq, x_out_buf, x_out_ap, panel, a2_bc, b2_bc,
                          bank_s, bank_q, bank_bc, t_rot, nchunks=16):
    emit_ln_stats(p, C, ln2, y, ysq, nchunks, 1e-5, bank_s, bank_q)
    emit_bcast(p, C, ln2.a_bf(), ln2.rows_bf, bank_bc, a2_bc)
    emit_bcast(p, C, ln2.b_bf(), ln2.rows_bf, bank_bc, b2_bc)
    for c in range(nchunks):
        y_c = y.h[:, c * PW:(c + 1) * PW]
        trb, off = t_rot[c % 2]
        p.tt("gpsimd", trb.h[:, off:off + PW], y_c, a2_bc.h[:], ALU.mult, r=[y, a2_bc], w=[trb])
        p.tt("vector", trb.h[:, off + PW:off + 2 * PW], trb.h[:, off:off + PW], b2_bc.h[:],
             ALU.add, r=[trb, b2_bc], w=[trb])
        p.dma("gpsimd", x_out_ap[c * 128:(c + 1) * 128, panel * PW:(panel + 1) * PW],
              trb.h[:, off + PW:off + 2 * PW], r=[trb, x_out_buf], w=[x_out_buf], accum_op=ALU.add)



def emit_row_pair(p, ln, row_f32_ap, hi_bf_ap, lo_bf_ap, scratch_f32_ap):
    p.copy("gpsimd", hi_bf_ap, row_f32_ap, r=[ln.rows], w=[ln.rows_bf])
    p.copy("gpsimd", scratch_f32_ap, hi_bf_ap, r=[ln.rows_bf], w=[ln.rows])
    p.tt("vector", scratch_f32_ap, row_f32_ap, scratch_f32_ap, ALU.subtract, r=[ln.rows], w=[ln.rows])
    p.copy("gpsimd", lo_bf_ap, scratch_f32_ap, r=[ln.rows], w=[ln.rows_bf])



def phase_l0_attn(p, C, W, x_in, x_out):
    nc = p.nc
    st = ExitStack()
    xb = p.sb("A_xb", [128, 16 * PW], BF16, st)
    xsq = p.sb("A_xsq", [128, 2 * PW], BF16, st)
    qsb = p.sb("A_q", [128, 16 * PW], BF16, st)
    ksb = p.sb("A_k", [128, 16 * PW], BF16, st)
    vsb = p.sb("A_v", [128, 16 * PW], BF16, st)
    usb = p.sb("A_u", [128, 16 * PW], BF16, st)
    osb = p.sb("A_o", [128, 16 * PW], BF16, st)
    wbuf = [p.sb(f"A_w{i}", [128, 2 * 2048], BF16, st) for i in range(2)]
    masks = p.sb("A_masks", [128, 16 * 512], BF16, st)
    qd_bc = p.sb("A_qdbc", [128, 16 * 256], BF16, st)
    kd_cols = p.sb("A_kdcols", [128, 16 * 2], F32, st)
    bd_cols = p.sb("A_bdcols", [128, 16], F32, st)
    kv = p.sb("A_kv", [128, 16 * 128], F32, st)
    kvb = p.sb("A_kvb", [128, 16 * 128], BF16, st)
    scol = p.sb("A_scol", [128, 16], F32, st)
    srow_bf = p.sb("A_srowbf", [1, 16], BF16, st)
    qq = p.sb("A_qq", [128, 256], BF16, st)
    qkm = p.sb("A_qkm", [128, 2 * 256], BF16, st)
    ktd = p.sb("A_ktd", [128, 2 * 128], BF16, st)
    vtd = p.sb("A_vtd", [128, 2 * 128], BF16, st)
    tmp_f32 = p.sb("A_tmpf", [128, PW], F32, st)
    a_bc = p.sb("A_abc", [128, PW], F32, st)
    b_bc = p.sb("A_bbc", [128, PW], F32, st)
    a2_bc = p.sb("A_a2bc", [128, PW], F32, st)
    b2_bc = p.sb("A_b2bc", [128, PW], F32, st)
    t_rot = [(p.sb(f"A_t{i}", [128, 2 * PW], F32, st), 0) for i in range(2)]
    rst_bc = p.sb("A_rstbc", [128, PW], F32, st)
    ln = LNStats(p, st, "A_ln")
    ln2 = ln
    C["tmp_f32"] = tmp_f32
    load_const_tiles(p, C, W, st, ["Dm", "arr_row", "karr_cols"])

    bankM, bankQK0, bankQK1, bankO, bankKV, bankS, bankQ2 = p.banks
    bankTR = p.bankT
    bank_bc = bankQK0

    p.dma("sync", x_out.h.ap()[:, :], x_in.h.ap()[:, :], r=[x_in], w=[x_out])
    p.dma("gpsimd", srow_bf.h[0:1, :], W["slope_row"].h.ap()[0:1, :], r=[W["slope_row"]], w=[srow_bf])
    p.mm(bankM.h[:, 0:16], [(C["ones_row"].h[0:1, 0:128], srow_bf.h[0:1, 0:16])],
         r=[C["ones_row"], srow_bf], w=[bankM])
    p.ts("vector", scol.h[:], bankM.h[:, 0:16], -1.0, None, ALU.mult, None, r=[bankM], w=[scol])
    for h in range(NH):
        sc = scol.h[:, h:h + 1]
        for j in range(2):
            p.ts("vector", tmp_f32.h[:, 0:256], C["Dm"].h[:, j * 256:(j + 1) * 256], sc, None,
                 ALU.mult, None, r=[C["Dm"], scol], w=[tmp_f32])
            p.act(masks.h[:, h * 512 + j * 256: h * 512 + (j + 1) * 256], tmp_f32.h[:, 0:256],
                  AF.Exp, r=[tmp_f32], w=[masks])
        p.ts("vector", ln.s(3, 256), C["arr_row"].h[0:1, 0:256], scol.h[0:1, h:h + 1], None,
             ALU.mult, None, r=[C["arr_row"], scol], w=[ln.rows])
        p.act(ln.sbf(3, 256), ln.s(3, 256), AF.Exp, r=[ln.rows], w=[ln.rows_bf])
        p.mm(bankM.h[:, 0:256], [(C["ones_row"].h[0:1, 0:128], ln.sbf(3, 256))],
             r=[C["ones_row"], ln.rows_bf], w=[bankM])
        p.copy("scalar", qd_bc.h[:, h * 256:(h + 1) * 256], bankM.h[:, 0:256], r=[bankM], w=[qd_bc])
        p.ts("vector", tmp_f32.h[:, 0:2], C["karr_cols"].h[:, 0:2], sc, None, ALU.mult, None,
             r=[C["karr_cols"], scol], w=[tmp_f32])
        p.act(kd_cols.h[:, h * 2:(h + 1) * 2], tmp_f32.h[:, 0:2], AF.Exp, r=[tmp_f32], w=[kd_cols])
        p.ts("vector", tmp_f32.h[:, 2:3], scol.h[:, h:h + 1], 256.0, None, ALU.mult, None,
             r=[scol], w=[tmp_f32])
        p.act(bd_cols.h[:, h:h + 1], tmp_f32.h[:, 2:3], AF.Exp, r=[tmp_f32], w=[bd_cols])
    p.memset("vector", kv.h[:], 0.0, w=[kv])
    p.memset("vector", kvb.h[:], 0.0, w=[kvb])

    wT = W["wqkvuT"]
    woT = W["woutT"]

    for panel in range(NP):
        t0 = panel * PW
        p.dma("gpsimd", xb.h[:],
              x_in.h.ap()[:, t0:t0 + PW].rearrange("(c p) t -> p c t", p=128),
              r=[x_in], w=[xb])
        emit_ln_stats(p, C, ln, xb, xsq, 16, 1e-5, bankS, bankQ2)
        emit_xn(p, C, ln, xb, a_bc, b_bc, bank_bc, 16)

        for fc2 in range(32):
            wb = wbuf[fc2 % 2]
            p.dma("gpsimd", wb.h[:],
                  wT.h.ap()[fc2 * 256:(fc2 + 1) * 256, :].rearrange("(t p) k -> p t k", p=128),
                  r=[wT], w=[wb])
            for sub in range(2):
                fc = fc2 * 2 + sub
                pairs = [(wb.h[:, sub * 2048 + c * 128:sub * 2048 + (c + 1) * 128],
                          xb.h[:, c * PW:(c + 1) * PW]) for c in range(16)]
                p.mm(bankM.h[:, 0:PW], pairs, r=[wb, xb], w=[bankM])
                kind, idx = divmod(fc, 16)
                dstbuf = (qsb, ksb, vsb, usb)[kind]
                dst = dstbuf.h[:, idx * PW:(idx + 1) * PW]
                if kind <= 1:
                    p.act(dst, bankM.h[:, 0:PW], AF.Silu, r=[bankM], w=[dstbuf])
                else:
                    p.copy("scalar", dst, bankM.h[:, 0:PW], r=[bankM], w=[dstbuf])

        for blk in range(2):
            boff = blk * 256
            for h in range(NH):
                qs = qsb.h[:, h * PW + boff:h * PW + boff + 256]
                ks = ksb.h[:, h * PW + boff:h * PW + boff + 256]
                vs = vsb.h[:, h * PW + boff:h * PW + boff + 256]
                kv_h = kv.h[:, h * 128:(h + 1) * 128]
                kvb_h = kvb.h[:, h * 128:(h + 1) * 128]
                p.tt("vector", qq.h[:], qs, qd_bc.h[:, h * 256:(h + 1) * 256], ALU.mult,
                     r=[qsb, qd_bc], w=[qq])
                for j in range(2):
                    p.transpose(bankTR.h[:, 0:128],
                                ks[:, j * 128:(j + 1) * 128], C["ident"].h[:],
                                r=[ksb, C["ident"]], w=[bankTR])
                    p.op("scalar", lambda E, j=j, h=h: E.activation(
                        out=ktd.h[:, j * 128:(j + 1) * 128], in_=bankTR.h[:, 0:128],
                        func=AF.Copy, scale=kd_cols.h[:, h * 2 + j:h * 2 + j + 1], bias=0.0),
                        r=[bankTR, kd_cols], w=[ktd])
                    p.transpose(bankTR.h[:, 128:256],
                                vs[:, j * 128:(j + 1) * 128], C["ident"].h[:],
                                r=[vsb, C["ident"]], w=[bankTR])
                    p.copy("scalar", vtd.h[:, j * 128:(j + 1) * 128],
                           bankTR.h[:, 128:256], r=[bankTR], w=[vtd])
                for j, bankj in ((0, bankQK0), (1, bankQK1)):
                    p.mm(bankj.h[:, 0:256], [(ks[:, j * 128:(j + 1) * 128], qs)],
                         r=[ksb, qsb], w=[bankj])
                    p.tt("vector", qkm.h[:, j * 256:(j + 1) * 256], bankj.h[:, 0:256],
                         masks.h[:, h * 512 + j * 256:h * 512 + (j + 1) * 256], ALU.mult,
                         r=[bankj, masks], w=[qkm])
                p.mm(bankO.h[:, 0:256],
                     [(kvb_h, qq.h[:]),
                      (vtd.h[:, 0:128], qkm.h[:, 0:256]),
                      (vtd.h[:, 128:256], qkm.h[:, 256:512])],
                     r=[kvb, qq, vtd, qkm], w=[bankO])
                p.copy("scalar", osb.h[:, h * PW + boff:h * PW + boff + 256], bankO.h[:, 0:256],
                       r=[bankO], w=[osb])
                p.mm(bankKV.h[:, 0:128],
                     [(ktd.h[:, 0:128], vtd.h[:, 0:128]),
                      (ktd.h[:, 128:256], vtd.h[:, 128:256])],
                     r=[ktd, vtd], w=[bankKV])
                p.stt("vector", kv_h, kv_h, bd_cols.h[:, h:h + 1], bankKV.h[:, 0:128],
                      ALU.mult, ALU.add, r=[kv, bd_cols, bankKV], w=[kv])
                p.copy("gpsimd", kvb_h, kv_h, r=[kv], w=[kvb])

        for c in range(16):
            o_c = osb.h[:, c * PW:(c + 1) * PW]
            sq_c = xsq.h[:, (c % 2) * PW:((c % 2) + 1) * PW]
            p.act(sq_c, o_c, AF.Square, r=[osb], w=[xsq])
            p.mm_acc(bankS.h[0:1, 0:PW], [(C["ones_col"].h[:, 0:1], sq_c)],
                     r=[C["ones_col"], xsq], w=[bankS], start=(c == 0), stop=(c == 15))
        p.ts("vector", ln.s(0), bankS.h[0:1, 0:PW], 1.0 / 2048.0, None, ALU.mult, None,
             r=[bankS], w=[ln.rows])
        p.memset("vector", ln.eps.h[:], 1e-6, w=[ln.eps])
        p.act(ln.s(3), ln.s(0), AF.Ln, bias=ln.eps.h[0:1, 0:1], r=[ln.rows, ln.eps], w=[ln.rows])
        p.act(ln.s(0), ln.s(3), AF.Exp, scale=-0.5, r=[ln.rows], w=[ln.rows])
        p.copy("gpsimd", ln.sbf(3), ln.s(0), r=[ln.rows], w=[ln.rows_bf])
        emit_bcast(p, C, ln.sbf(3), ln.rows_bf, bank_bc, rst_bc)
        for c in range(16):
            o_c = osb.h[:, c * PW:(c + 1) * PW]
            u_c = usb.h[:, c * PW:(c + 1) * PW]
            p.tt("gpsimd", tmp_f32.h[:, 0:PW], o_c, rst_bc.h[:], ALU.mult, r=[osb, rst_bc], w=[tmp_f32])
            p.tt("vector", o_c, tmp_f32.h[:, 0:PW], u_c, ALU.mult, r=[tmp_f32, usb], w=[osb])

        for fc2 in range(8):
            wb = wbuf[fc2 % 2]
            p.dma("gpsimd", wb.h[:],
                  woT.h.ap()[fc2 * 256:(fc2 + 1) * 256, :].rearrange("(t p) k -> p t k", p=128),
                  r=[woT], w=[wb])
            for sub in range(2):
                fc = fc2 * 2 + sub
                pairs = [(wb.h[:, sub * 2048 + c * 128:sub * 2048 + (c + 1) * 128],
                          osb.h[:, c * PW:(c + 1) * PW]) for c in range(16)]
                p.mm(bankM.h[:, 0:PW], pairs, r=[wb, osb], w=[bankM])
                p.copy("scalar", usb.h[:, fc * PW:(fc + 1) * PW], bankM.h[:, 0:PW], r=[bankM], w=[usb])

        emit_post_ln_residual(p, C, ln2, usb, xsq, x_out, x_out.h.ap(), panel,
                              a2_bc, b2_bc, bankS, bankQ2, bank_bc, t_rot)

    p.fence()
    st.close()





def phase_l0_ffn(p, C, W, x_in, x_out):
    st = ExitStack()
    xb = p.sb("B_xb", [128, 16 * PW], BF16, st)
    xsq = p.sb("B_xsq", [128, 2 * PW], BF16, st)
    hsb = p.sb("B_h", [128, 64 * PW], BF16, st)
    ysb = p.sb("B_y", [128, 16 * PW], BF16, st)
    gs = p.sb("B_gs", [128, 2 * PW], BF16, st)
    us = p.sb("B_us", [128, 2 * PW], BF16, st)
    wgbuf = [p.sb(f"B_wg{i}", [128, 2 * 2048], BF16, st) for i in range(2)]
    wubuf = [p.sb(f"B_wu{i}", [128, 2 * 2048], BF16, st) for i in range(2)]
    wdbuf = [p.sb(f"B_wd{i}", [128, 8192], BF16, st) for i in range(2)]
    tmp_f32 = p.sb("B_tmpf", [128, PW], F32, st)
    a_bc = p.sb("B_abc", [128, PW], F32, st)
    b_bc = p.sb("B_bbc", [128, PW], F32, st)
    t_rot = [(p.sb(f"B_t{i}", [128, 2 * PW], F32, st), 0) for i in range(2)]
    ln = LNStats(p, st, "B_ln")
    C["tmp_f32"] = tmp_f32
    load_const_tiles(p, C, W, st, ["gsc", "usc", "dsc"])

    bankG, bankU, bankD, bankBC, bankS, bankQ2, _sp = p.banks

    gsc, usc, dsc = C["gsc"], C["usc"], C["dsc"]
    n128 = C["neg128_row"]

    p.dma("sync", x_out.h.ap()[:, :], x_in.h.ap()[:, :], r=[x_in], w=[x_out])

    for panel in range(NP):
        t0 = panel * PW
        p.dma("gpsimd", xb.h[:],
              x_in.h.ap()[:, t0:t0 + PW].rearrange("(c p) t -> p c t", p=128),
              r=[x_in], w=[xb])
        emit_ln_stats(p, C, ln, xb, xsq, 16, 1e-5, bankS, bankQ2)
        emit_xn(p, C, ln, xb, a_bc, b_bc, bankBC, 16)
        for c in range(16):
            p.mm_acc(bankS.h[0:1, 0:PW], [(C["ones_col"].h[:, 0:1], xb.h[:, c * PW:(c + 1) * PW])],
                     r=[C["ones_col"], xb], w=[bankS], start=(c == 0), stop=(c == 15))
        p.copy("vector", ln.s(3), bankS.h[0:1, 0:PW], r=[bankS], w=[ln.rows])
        emit_row_pair(p, ln, ln.s(3), ln.sbf(2), ln.sbf(3), ln.s(0))

        for fc2 in range(32):
            wg = wgbuf[fc2 % 2]
            wu = wubuf[fc2 % 2]
            p.dma("gpsimd", wg.h[:],
                  W["gateT8"].h.ap()[fc2 * 256:(fc2 + 1) * 256, :].rearrange("(t p) k -> p t k", p=128),
                  r=[W["gateT8"]], w=[wg])
            p.dma("gpsimd", wu.h[:],
                  W["upT8"].h.ap()[fc2 * 256:(fc2 + 1) * 256, :].rearrange("(t p) k -> p t k", p=128),
                  r=[W["upT8"]], w=[wu])
            for sub in range(2):
                fc = fc2 * 2 + sub
                wgs = wg.h[:, sub * 2048:(sub + 1) * 2048]
                wus = wu.h[:, sub * 2048:(sub + 1) * 2048]
                pairs_g = [(wgs[:, c * 128:(c + 1) * 128], xb.h[:, c * PW:(c + 1) * PW]) for c in range(16)]
                pairs_g.append((n128.h[0:1, 0:128], ln.sbf(2)))
                pairs_g.append((n128.h[0:1, 0:128], ln.sbf(3)))
                p.mm(bankG.h[:, 0:PW], pairs_g, r=[wg, xb, n128, ln.rows_bf], w=[bankG])
                pairs_u = [(wus[:, c * 128:(c + 1) * 128], xb.h[:, c * PW:(c + 1) * PW]) for c in range(16)]
                pairs_u.append((n128.h[0:1, 0:128], ln.sbf(2)))
                pairs_u.append((n128.h[0:1, 0:128], ln.sbf(3)))
                p.mm(bankU.h[:, 0:PW], pairs_u, r=[wu, xb, n128, ln.rows_bf], w=[bankU])
                g_r = gs.h[:, (fc % 2) * PW:((fc % 2) + 1) * PW]
                u_r = us.h[:, (fc % 2) * PW:((fc % 2) + 1) * PW]
                p.act(g_r, bankG.h[:, 0:PW], AF.Silu, scale=gsc.h[:, fc:fc + 1], r=[bankG, gsc], w=[gs])
                p.act(u_r, bankU.h[:, 0:PW], AF.Copy, scale=usc.h[:, fc:fc + 1], bias=0.0,
                      r=[bankU, usc], w=[us])
                p.tt("gpsimd", hsb.h[:, fc * PW:(fc + 1) * PW], g_r, u_r, ALU.mult, r=[gs, us], w=[hsb])

        for c in range(64):
            p.mm_acc(bankS.h[0:1, 0:PW], [(C["ones_col"].h[:, 0:1], hsb.h[:, c * PW:(c + 1) * PW])],
                     r=[C["ones_col"], hsb], w=[bankS], start=(c == 0), stop=(c == 63))
        p.copy("vector", ln.s(3), bankS.h[0:1, 0:PW], r=[bankS], w=[ln.rows])
        emit_row_pair(p, ln, ln.s(3), ln.sbf(2), ln.sbf(3), ln.s(0))

        for oc in range(16):
            wd = wdbuf[oc % 2]
            p.dma("gpsimd", wd.h[:], W["downT8"].h.ap()[oc * 128:(oc + 1) * 128, :],
                  r=[W["downT8"]], w=[wd])
            pairs = [(wd.h[:, c * 128:(c + 1) * 128], hsb.h[:, c * PW:(c + 1) * PW]) for c in range(64)]
            pairs.append((n128.h[0:1, 0:128], ln.sbf(2)))
            pairs.append((n128.h[0:1, 0:128], ln.sbf(3)))
            p.mm(bankD.h[:, 0:PW], pairs, r=[wd, hsb, n128, ln.rows_bf], w=[bankD])
            p.act(ysb.h[:, oc * PW:(oc + 1) * PW], bankD.h[:, 0:PW], AF.Copy,
                  scale=dsc.h[:, oc:oc + 1], bias=0.0, r=[bankD, dsc], w=[ysb])

        emit_post_ln_residual(p, C, ln, ysb, xsq, x_out, x_out.h.ap(), panel,
                              a_bc, b_bc, bankS, bankQ2, bankBC, t_rot)

    p.fence()
    st.close()




def phase_l1_attn_proj(p, C, W, x_in, S):
    st = ExitStack()
    xb = p.sb("C_xb", [128, 16 * PW], BF16, st)
    xsq = p.sb("C_xsq", [128, 2 * PW], BF16, st)
    es32 = p.sb("C_es32", [128, 8 * 2048], F32, st)
    cz = p.sb("C_cz", [128, 2048], F32, st)
    wbuf = [p.sb(f"C_w{i}", [128, 2048], BF16, st) for i in range(2)]
    wpan = [p.sb(f"C_wp{i}", [128, 16 * 512], BF16, st) for i in range(2)]
    ropea = p.sb("C_ropea", [64, 4 * PW], F32, st)
    qf2 = p.sb("C_qf2", [64, 4 * PW], BF16, st)
    otile = p.sb("C_otile", [128, 2 * PW], BF16, st)
    ws1sb = p.sb("C_ws1", [128, 16 * 16], BF16, st)
    ws2sb = p.sb("C_ws2", [16, 1024], BF16, st)
    s1bf = p.sb("C_s1", [16, PW], BF16, st)
    esbf = p.sb("C_esbf", [128, 2048], BF16, st)
    estT = p.sb("C_estT", [128, 2048], BF16, st)
    tmp_f32 = p.sb("C_tmpf", [128, PW], F32, st)
    a_bc = p.sb("C_abc", [128, PW], F32, st)
    b_bc = p.sb("C_bbc", [128, PW], F32, st)
    ln = LNStats(p, st, "C_ln")
    C["tmp_f32"] = tmp_f32
    load_const_tiles(p, C, W, st, ["cosq", "sinq", "cosk", "sink", "bs2c"])

    bankM, bankV, bankS2, bankBC, bankS, bankQ2, _sp = p.banks
    bankT = p.bankT

    p.dma("gpsimd", ws1sb.h[:], W["ws1T"].h.ap()[:, :].rearrange("(c p) f -> p c f", p=128),
          r=[W["ws1T"]], w=[ws1sb])
    p.dma("gpsimd", ws2sb.h[:], W["ws2T"].h.ap()[:, :], r=[W["ws2T"]], w=[ws2sb])

    for panel in range(NP):
        t0 = panel * PW
        p.dma("gpsimd", xb.h[:],
              x_in.h.ap()[:, t0:t0 + PW].rearrange("(c p) t -> p c t", p=128),
              r=[x_in], w=[xb])
        emit_ln_stats(p, C, ln, xb, xsq, 16, 1e-5, bankS, bankQ2)
        emit_xn(p, C, ln, xb, a_bc, b_bc, bankBC, 16)

        for which, wT, dram, cosn, sinn in (
                (0, W["wqT"], S["qfD"], "cosq", "sinq"),
                (1, W["wkT"], S["kfD"], "cosk", "sink")):
            for hc in range(NH):
                wb = wbuf[hc % 2]
                p.dma("gpsimd", wb.h[:], wT.h.ap()[hc * 128:(hc + 1) * 128, :],
                      r=[wT], w=[wb])
                pairs = [(wb.h[:, c * 128:(c + 1) * 128], xb.h[:, c * PW:(c + 1) * PW]) for c in range(16)]
                p.mm(bankM.h[:, 0:PW], pairs, r=[wb, xb], w=[bankM])
                cosT = C[cosn]
                sinT = C[sinn]
                cs = cosT.h[0:64, t0:t0 + PW]
                sn = sinT.h[0:64, t0:t0 + PW]
                ps0 = bankM.h[0:64, 0:PW]
                ps1 = bankM.h[64:128, 0:PW]
                p.tt("vector", ropea.h[0:64, 0:PW], ps0, cs, ALU.mult, r=[bankM, cosT], w=[ropea])
                p.tt("vector", ropea.h[0:64, PW:2 * PW], ps1, sn, ALU.mult, r=[bankM, sinT], w=[ropea])
                p.tt("vector", ropea.h[0:64, 2 * PW:3 * PW], ps1, cs, ALU.mult, r=[bankM, cosT], w=[ropea])
                p.tt("vector", ropea.h[0:64, 3 * PW:4 * PW], ps0, sn, ALU.mult, r=[bankM, sinT], w=[ropea])
                qo = qf2.h[0:64, (hc % 2) * 2 * PW:((hc % 2) + 1) * 2 * PW]
                p.tt("gpsimd", qo[0:64, 0:PW], ropea.h[0:64, 0:PW], ropea.h[0:64, PW:2 * PW],
                     ALU.subtract, r=[ropea], w=[qf2])
                p.tt("gpsimd", qo[0:64, PW:2 * PW], ropea.h[0:64, 2 * PW:3 * PW],
                     ropea.h[0:64, 3 * PW:4 * PW], ALU.add, r=[ropea], w=[qf2])
                p.dma("sync",
                      dram.h.ap()[hc * 128:(hc + 1) * 128, t0:t0 + PW].rearrange(
                          "(half p) t -> p half t", p=64),
                      qo.rearrange("p (half t) -> p half t", t=PW),
                      r=[qf2, dram], w=[dram])

        for wT, dram in ((W["wvT"], S["vtD"]), (W["wgT"], S["gtD"])):
            for eg in range(4):
                wp = wpan[eg % 2]
                p.dma("gpsimd", wp.h[:], wT.h.ap()[eg * 128:(eg + 1) * 128, :],
                      r=[wT], w=[wp])
                for tt_ in range(4):
                    pairs = [(xb.h[:, c * PW + tt_ * 128:c * PW + (tt_ + 1) * 128],
                              wp.h[:, c * 512:(c + 1) * 512]) for c in range(16)]
                    p.mm(bankV.h[:, 0:512], pairs, r=[xb, wp], w=[bankV])
                    ot = otile.h[:, (tt_ % 2) * PW:((tt_ % 2) + 1) * PW]
                    p.copy("scalar", ot, bankV.h[:, 0:512], r=[bankV], w=[otile])
                    p.dma("sync", dram.h.ap()[t0 + tt_ * 128:t0 + (tt_ + 1) * 128, eg * 512:(eg + 1) * 512],
                          ot, r=[otile, dram], w=[dram])

        pairs = [(ws1sb.h[:, c * 16:(c + 1) * 16], xb.h[:, c * PW:(c + 1) * PW]) for c in range(16)]
        p.mm(bankS2.h[0:16, 0:PW], pairs, r=[ws1sb, xb], w=[bankS2])
        p.copy("vector", s1bf.h[:], bankS2.h[0:16, 0:PW], r=[bankS2], w=[s1bf])
        for ch in range(8):
            p.mm(bankS2.h[:, 0:PW], [(ws2sb.h[:, ch * 128:(ch + 1) * 128], s1bf.h[:])],
                 r=[ws2sb, s1bf], w=[bankS2])
            p.ts("vector", tmp_f32.h[:, 0:PW], bankS2.h[:, 0:PW], C["bs2c"].h[:, ch:ch + 1], 32.0,
                 ALU.add, ALU.min, r=[bankS2, C["bs2c"]], w=[tmp_f32])
            p.ts("vector", tmp_f32.h[:, 0:PW], tmp_f32.h[:, 0:PW], -32.0, None, ALU.max, None,
                 r=[tmp_f32], w=[tmp_f32])
            p.act(es32.h[:, ch * 2048 + t0:ch * 2048 + t0 + PW], tmp_f32.h[:, 0:PW], AF.Exp,
                  r=[tmp_f32], w=[es32])

    for ch in range(8):
        sl = es32.h[:, ch * 2048:(ch + 1) * 2048]
        p.op("vector", lambda E, sl=sl: E.tensor_tensor_scan(cz.h[:], sl, sl, 0.0,
                                                             op0=ALU.add, op1=ALU.max),
             r=[es32], w=[cz])
        p.copy("gpsimd", esbf.h[:], sl, r=[es32], w=[esbf])
        p.dma("sync", S["esfD"].h.ap()[ch * 128:(ch + 1) * 128, :], esbf.h[:],
              r=[esbf, S["esfD"]], w=[S["esfD"]])
        p.act(sl, cz.h[:], AF.Ln, r=[cz], w=[es32])
        p.act(cz.h[:], sl, AF.Exp, scale=-1.0, r=[es32], w=[cz])
        for tt_ in range(16):
            p.transpose(bankT.h[:, 0:128], esbf.h[:, tt_ * 128:(tt_ + 1) * 128], C["ident"].h[:],
                        r=[esbf, C["ident"]], w=[bankT])
            p.copy("scalar", estT.h[:, tt_ * 128:(tt_ + 1) * 128], bankT.h[:, 0:128],
                   r=[bankT], w=[estT])
        p.dma("sync", S["estD"].h.ap()[:, ch * 128:(ch + 1) * 128].rearrange("(tt p) m -> p tt m", p=128),
              estT.h[:], r=[estT, S["estD"]], w=[S["estD"]])
        p.copy("gpsimd", esbf.h[:], cz.h[:], r=[cz], w=[esbf])
        p.dma("sync", S["enfD"].h.ap()[ch * 128:(ch + 1) * 128, :], esbf.h[:],
              r=[esbf, S["enfD"]], w=[S["enfD"]])
        for tt_ in range(16):
            p.transpose(bankT.h[:, 0:128], esbf.h[:, tt_ * 128:(tt_ + 1) * 128], C["ident"].h[:],
                        r=[esbf, C["ident"]], w=[bankT])
            p.copy("scalar", estT.h[:, tt_ * 128:(tt_ + 1) * 128], bankT.h[:, 0:128],
                   r=[bankT], w=[estT])
        p.dma("sync", S["entD"].h.ap()[:, ch * 128:(ch + 1) * 128].rearrange("(tt p) m -> p tt m", p=128),
              estT.h[:], r=[estT, S["entD"]], w=[S["entD"]])

    p.fence()
    st.close()



def phase_l1_attn_core(p, C, W, x_in, x_out, S):
    st = ExitStack()
    qp = p.sb("C2_qp", [128, 16 * 512], BF16, st)
    kblk = p.sb("C2_kblk", [128, 16 * 512], BF16, st)
    estm = p.sb("C2_estm", [128, 4 * 1024], BF16, st)
    esfm = p.sb("C2_esfm", [128, 8 * 512], BF16, st)
    vt = p.sb("C2_vt", [128, 4 * 2048], BF16, st)
    enz = p.sb("C2_enz", [128, 4 * 1024], BF16, st)
    ok_sb = p.sb("C2_ok", [128, 4 * 1024], F32, st)
    tmpa = p.sb("C2_tmpa", [128, 3 * 1024], F32, st)
    qvzT = p.sb("C2_qvzT", [128, 8 * 512], BF16, st)
    o_sb = p.sb("C2_o", [128, 4 * 2048], F32, st)
    gsl = A_sb = None
    cols = p.sb("C2_cols", [128, 4 * 16], F32, st)
    gnw_bc = p.sb("C2_gnw", [128, 2048], BF16, st)
    A_sb = p.sb("C2_A", [128, 4 * 512], BF16, st)
    wbuf = [p.sb(f"C2_w{i}", [128, 2048], BF16, st) for i in range(2)]
    a_bc = p.sb("C2_abc", [128, PW], F32, st)
    b_bc = p.sb("C2_bbc", [128, PW], F32, st)
    xsq = p.sb("C2_xsq", [128, 2 * PW], BF16, st)
    ln = LNStats(p, st, "C2_ln")
    ysb = kblk
    t_rot = [(o_sb, 0), (o_sb, 2 * PW)]
    load_const_tiles(p, C, W, st, ["cmask", "gnw16"])

    bankA, bankOK, bankW2, bankO, bankS, bankQ2, bankBC = p.banks
    bankT = p.bankT

    p.dma("sync", x_out.h.ap()[:, :], x_in.h.ap()[:, :], r=[x_in], w=[x_out])
    p.copy("gpsimd", ln.sbf(0, 2048).rearrange("o (a b) -> o a b", b=PW), C["gnw16"].h[0:1, :].rearrange("o (a b) -> o a b", b=PW),
           r=[C["gnw16"]], w=[ln.rows_bf])
    for q4 in range(4):
        p.mm(bankBC.h[:, 0:PW], [(C["ones_row"].h[0:1, 0:128], ln.rows_bf.h[0:1, q4 * PW:(q4 + 1) * PW])],
             r=[C["ones_row"], ln.rows_bf], w=[bankBC])
        p.copy("scalar", gnw_bc.h[:, q4 * PW:(q4 + 1) * PW], bankBC.h[:, 0:PW], r=[bankBC], w=[gnw_bc])

    for tqp in range(NP):
        t0 = tqp * PW
        p.dma("gpsimd", qp.h[:],
              S["qfD"].h.ap()[:, t0:t0 + PW].rearrange("(h p) t -> p h t", p=128),
              r=[S["qfD"]], w=[qp])
        p.dma("gpsimd", enz.h[:],
              S["entD"].h.ap()[t0:t0 + PW, :].rearrange("(s p) m -> p s m", p=128),
              r=[S["entD"]], w=[enz])
        nblk = tqp + 1
        for kb in range(nblk):
            k0 = kb * PW
            p.dma("gpsimd", kblk.h[:],
                  S["kfD"].h.ap()[:, k0:k0 + PW].rearrange("(h p) t -> p h t", p=128),
                  r=[S["kfD"]], w=[kblk])
            p.dma("gpsimd", estm.h[:],
                  S["estD"].h.ap()[k0:k0 + PW, :].rearrange("(j p) m -> p j m", p=128),
                  r=[S["estD"]], w=[estm])
            for h in range(NH):
                kh = kblk.h[:, h * 512:(h + 1) * 512]
                qh = qp.h[:, h * 512:(h + 1) * 512]
                for tkt in range(4):
                    ps = (bankA, bankW2)[tkt % 2]
                    p.mm(ps.h[:, 0:512], [(kh[:, tkt * 128:(tkt + 1) * 128], qh)],
                         r=[kblk, qp], w=[ps])
                    asl = A_sb.h[:, tkt * 512:(tkt + 1) * 512]
                    if kb == tqp:
                        p.tt("vector", asl, ps.h[:, 0:512],
                             C["cmask"].h[:, tkt * 512:(tkt + 1) * 512], ALU.mult,
                             r=[ps, C["cmask"]], w=[A_sb])
                    else:
                        p.copy("scalar", asl, ps.h[:, 0:512], r=[ps], w=[A_sb])
                for sub in range(4):
                    prs = [(A_sb.h[:, tkt * 512 + sub * 128:tkt * 512 + (sub + 1) * 128],
                            estm.h[:, tkt * 1024 + h * 64:tkt * 1024 + (h + 1) * 64])
                           for tkt in range(4)]
                    p.mm_acc(bankOK.h[:, sub * 64:(sub + 1) * 64], prs, r=[A_sb, estm], w=[bankOK],
                             start=(True), stop=(True))
                okv = ok_sb.h[:, :].rearrange("p (s q) -> p s q", q=1024)[:, :, h * 64:(h + 1) * 64]
                psv = bankOK.h[:, 0:256].rearrange("p (s m) -> p s m", m=64)
                if kb == 0:
                    p.copy("vector", okv, psv, r=[bankOK], w=[ok_sb])
                else:
                    p.tt("vector", okv, okv, psv, ALU.add, r=[bankOK, ok_sb], w=[ok_sb])
        for sub in range(4):
            oks = ok_sb.h[:, sub * 1024:(sub + 1) * 1024]
            u0 = tmpa.h[:, 0:1024]
            p.tt("vector", u0, oks, enz.h[:, sub * 1024:(sub + 1) * 1024], ALU.mult,
                 r=[ok_sb, enz], w=[tmpa])
            u03 = u0.rearrange("p (h m) -> p h m", m=64)
            nmax = cols.h[:, 0:16]
            p.op("vector", lambda E, a=nmax, b=u03: E.reduce_max(a, b, axis=AXX, negate=True),
                 r=[tmpa], w=[cols])
            t1 = tmpa.h[:, 1024:2048]
            nb3 = cols.h[:, 0:16].rearrange("p (h o) -> p h o", o=1).broadcast_to([128, 16, 64])
            p.tt("vector", t1.rearrange("p (h m) -> p h m", m=64), u03, nb3, ALU.add,
                 r=[tmpa, cols], w=[tmpa])
            t2 = tmpa.h[:, 2048:3072]
            p.act(t2, t1, AF.Exp, r=[tmpa], w=[tmpa])
            ssum = cols.h[:, 16:32]
            p.op("vector", lambda E, a=ssum, b=t2.rearrange("p (h m) -> p h m", m=64):
                 E.reduce_sum(a, b, axis=AXX), r=[tmpa], w=[cols])
            rcp = cols.h[:, 32:48]
            p.op("vector", lambda E, a=rcp, b=ssum: E.reciprocal(a, b), r=[cols], w=[cols])
            rb3 = rcp.rearrange("p (h o) -> p h o", o=1).broadcast_to([128, 16, 64])
            t3 = tmpa.h[:, 0:1024]
            p.tt("vector", t3.rearrange("p (h m) -> p h m", m=64),
                 t2.rearrange("p (h m) -> p h m", m=64), rb3, ALU.mult, r=[tmpa, cols], w=[tmpa])
            qvz = A_sb.h[:, 1024:2048]
            p.tt("vector", qvz, t3, enz.h[:, sub * 1024:(sub + 1) * 1024], ALU.mult,
                 r=[tmpa, enz], w=[A_sb])
            for pr in range(8):
                p.transpose(bankT.h[:, 0:128], qvz[:, pr * 128:(pr + 1) * 128], C["ident"].h[:],
                            r=[A_sb, C["ident"]], w=[bankT])
                p.copy("scalar", qvzT.h[:, pr * 512 + sub * 128:pr * 512 + (sub + 1) * 128],
                       bankT.h[:, 0:128], r=[bankT], w=[qvzT])
        for kb in range(nblk):
            k0 = kb * PW
            p.dma("gpsimd", esfm.h[:],
                  S["esfD"].h.ap()[:, k0:k0 + PW].rearrange("(ch p) t -> p ch t", p=128),
                  r=[S["esfD"]], w=[esfm])
            p.dma("gpsimd", vt.h[:],
                  S["vtD"].h.ap()[k0:k0 + PW, :].rearrange("(j p) e -> p j e", p=128),
                  r=[S["vtD"]], w=[vt])
            for h in range(NH):
                pb = 64 * (h % 2)
                for tkt in range(4):
                    ps = (bankA, bankW2)[tkt % 2]
                    lh = esfm.h[pb:pb + 64, (h // 2) * 512 + tkt * 128:(h // 2) * 512 + (tkt + 1) * 128]
                    rh = qvzT.h[pb:pb + 64, (h // 2) * 512:(h // 2 + 1) * 512]
                    p.mm(ps.h[:, 0:512], [(lh, rh)], r=[esfm, qvzT], w=[ps])
                    asl = A_sb.h[:, tkt * 512:(tkt + 1) * 512]
                    if kb == tqp:
                        p.tt("vector", asl, ps.h[:, 0:512],
                             C["cmask"].h[:, tkt * 512:(tkt + 1) * 512], ALU.mult,
                             r=[ps, C["cmask"]], w=[A_sb])
                    else:
                        p.copy("scalar", asl, ps.h[:, 0:512], r=[ps], w=[A_sb])
                for sub in range(4):
                    prs = [(A_sb.h[:, tkt * 512 + sub * 128:tkt * 512 + (sub + 1) * 128],
                            vt.h[:, tkt * 2048 + h * 128:tkt * 2048 + (h + 1) * 128])
                           for tkt in range(4)]
                    p.mm_acc(bankO.h[:, sub * 128:(sub + 1) * 128], prs, r=[A_sb, vt], w=[bankO],
                             start=True, stop=True)
                ov = o_sb.h[:, :].rearrange("p (s q) -> p s q", q=2048)[:, :, h * 128:(h + 1) * 128]
                psv = bankO.h[:, 0:512].rearrange("p (s e) -> p s e", e=128)
                if kb == 0:
                    p.copy("vector", ov, psv, r=[bankO], w=[o_sb])
                else:
                    p.tt("vector", ov, ov, psv, ALU.add, r=[bankO, o_sb], w=[o_sb])
        for sub in range(4):
            osl = o_sb.h[:, sub * 2048:(sub + 1) * 2048]
            sq = tmpa.h[:, 0:2048]
            p.act(sq, osl, AF.Square, r=[o_sb], w=[tmpa])
            ss = cols.h[:, 0:16]
            p.op("vector", lambda E, a=ss, b=sq.rearrange("p (h e) -> p h e", e=128):
                 E.reduce_sum(a, b, axis=AXX), r=[tmpa], w=[cols])
            p.ts("vector", ss, ss, 1.0 / 128.0, None, ALU.mult, None, r=[cols], w=[cols])
            p.memset("vector", cols.h[:, 48:49], 1e-5, w=[cols])
            ss2 = cols.h[:, 16:32]
            p.act(ss2, ss, AF.Ln, bias=cols.h[:, 48:49], r=[cols], w=[cols])
            p.act(ss, ss2, AF.Exp, scale=-0.5, r=[cols], w=[cols])
            rb3 = ss.rearrange("p (h o) -> p h o", o=1).broadcast_to([128, 16, 128])
            t1 = tmpa.h[:, 1024:3072]
            p.tt("vector", t1.rearrange("p (h e) -> p h e", e=128),
                 osl.rearrange("p (h e) -> p h e", e=128), rb3, ALU.mult, r=[o_sb, cols], w=[tmpa])
            p.tt("gpsimd", t1, t1, gnw_bc.h[:], ALU.mult, r=[tmpa, gnw_bc], w=[tmpa])
            p.dma("gpsimd", enz.h[:, 0:2048],
                  S["gtD"].h.ap()[t0 + sub * 128:t0 + (sub + 1) * 128, :], r=[S["gtD"]], w=[enz])
            gsl = A_sb.h[:, 0:2048]
            p.act(gsl, enz.h[:, 0:2048], AF.Silu, r=[enz], w=[A_sb])
            og = vt.h[:, 0:2048]
            p.tt("vector", og, t1, gsl, ALU.mult, r=[tmpa, A_sb], w=[vt])
            for h in range(NH):
                p.transpose(bankT.h[:, 0:128], og[:, h * 128:(h + 1) * 128], C["ident"].h[:],
                            r=[vt, C["ident"]], w=[bankT])
                p.copy("scalar", qp.h[:, h * 512 + sub * 128:h * 512 + (sub + 1) * 128],
                       bankT.h[:, 0:128], r=[bankT], w=[qp])
        for oc in range(16):
            wb = wbuf[oc % 2]
            p.dma("gpsimd", wb.h[:], W["woT"].h.ap()[oc * 128:(oc + 1) * 128, :],
                  r=[W["woT"]], w=[wb])
            pairs = [(wb.h[:, c * 128:(c + 1) * 128], qp.h[:, c * 512:(c + 1) * 512]) for c in range(16)]
            p.mm(bankA.h[:, 0:512], pairs, r=[wb, qp], w=[bankA])
            p.copy("scalar", ysb.h[:, oc * PW:(oc + 1) * PW], bankA.h[:, 0:512], r=[bankA], w=[ysb])
        emit_post_ln_residual(p, C, ln, ysb, xsq, x_out, x_out.h.ap(), tqp,
                              a_bc, b_bc, bankS, bankQ2, bankBC, t_rot)

    p.fence()
    st.close()




HID = [1024, 2048, 3072, 4096, 1024, 2048, 3072, 4096]


def phase_l1_ffn(p, C, W, x_in, x_out):
    st = ExitStack()
    xb = p.sb("D_xb", [128, 16 * PW], BF16, st)
    xsq = p.sb("D_xsq", [128, 2 * PW], BF16, st)
    gxn = p.sb("D_gxn", [128, 16 * PW], BF16, st)
    gh = p.sb("D_gh", [128, 8 * PW], BF16, st)
    gmid = p.sb("D_gmid", [128, 32 * PW], BF16, st)
    acc = p.sb("D_acc", [128, 8 * PW], F32, st)
    accbf = gmid
    ysb = xb
    lg = p.sb("D_lg", [8, PW], F32, st)
    lgbf = p.sb("D_lgbf", [8, PW], BF16, st)
    lt = p.sb("D_lt", [128, 8], F32, st)
    cols = p.sb("D_cols", [128, 16], F32, st)
    pr8 = p.sb("D_pr8", [128, 3 * 8], F32, st)
    wexpT = p.sb("D_wexpT", [8, PW], BF16, st)
    w_bc = p.sb("D_wbc", [128, PW], BF16, st)
    crows = p.sb("D_crows", [1, 8 * 1024], BF16, st)
    b2row = p.sb("D_b2row", [1, 1024], BF16, st)
    ctot = p.sb("D_ctot", [1, 1024], F32, st)
    ctbf = p.sb("D_ctbf", [1, 1024], BF16, st)
    ctcols = p.sb("D_ctcols", [128, 8], F32, st)
    gb1 = p.sb("D_gb1", [128, 32], BF16, st)
    onespw = p.sb("D_onespw", [1, PW], BF16, st)
    w1buf = [p.sb(f"D_w1{i}", [128, 4 * 1024], BF16, st) for i in range(2)]
    w2buf = [p.sb(f"D_w2{i}", [128, 2 * 32 * 128], BF16, st) for i in range(2)]
    wdbuf = [p.sb(f"D_wd{i}", [128, 16 * 128], BF16, st) for i in range(2)]
    rwsb = p.sb("D_rw", [128, 16 * 8], BF16, st)
    tmpf = p.sb("D_tmpf", [128, PW], F32, st)
    a_bc = p.sb("D_abc", [128, PW], F32, st)
    b_bc = p.sb("D_bbc", [128, PW], F32, st)
    t_rot = [(p.sb(f"D_t{i}", [128, 2 * PW], F32, st), 0) for i in range(2)]
    ln = LNStats(p, st, "D_ln")
    C["tmp_f32"] = tmpf
    load_const_tiles(p, C, W, st, ["dn_bc", "up_bc", "rt_bc", "lcost", "sh_b1c", "sh_b2c",
                                   "co_bc", "sel8"] + [f"e{e}_b1c" for e in range(8)])

    bankH, bankMID, bankF, bankR, bankS, bankQ2, bankBC = p.banks
    bankT = p.bankT

    p.dma("sync", x_out.h.ap()[:, :], x_in.h.ap()[:, :], r=[x_in], w=[x_out])
    p.memset("vector", onespw.h[:], 1.0, w=[onespw])
    p.memset("vector", ctot.h[:], 0.0, w=[ctot])
    p.dma("gpsimd", rwsb.h[:], W["rt_w"].h.ap()[:, :].rearrange("(c p) e -> p c e", p=128),
          r=[W["rt_w"]], w=[rwsb])

    for panel in range(NP):
        t0 = panel * PW
        p.dma("gpsimd", xb.h[:],
              x_in.h.ap()[:, t0:t0 + PW].rearrange("(c p) t -> p c t", p=128),
              r=[x_in], w=[xb])
        emit_ln_stats(p, C, ln, xb, xsq, 16, 1e-5, bankS, bankQ2)
        emit_xn(p, C, ln, xb, a_bc, b_bc, bankBC, 16)
        for c in range(16):
            p.act(gxn.h[:, c * PW:(c + 1) * PW], xb.h[:, c * PW:(c + 1) * PW], AF.Gelu,
                  r=[xb], w=[gxn])

        for lc in range(8):
            wd = wdbuf[lc % 2]
            p.dma("gpsimd", wd.h[:, 0:2048], W["dn_w"].h.ap()[lc * 128:(lc + 1) * 128, :],
                  r=[W["dn_w"]], w=[wd])
            pairs = [(wd.h[:, c * 128:(c + 1) * 128], xb.h[:, c * PW:(c + 1) * PW]) for c in range(16)]
            p.mm(bankH.h[:, 0:PW], pairs, r=[wd, xb], w=[bankH])
            p.act(gh.h[:, lc * PW:(lc + 1) * PW], bankH.h[:, 0:PW], AF.Gelu,
                  bias=C["dn_bc"].h[:, lc:lc + 1], r=[bankH, C["dn_bc"]], w=[gh])

        pairs = [(rwsb.h[:, c * 8:(c + 1) * 8], xb.h[:, c * PW:(c + 1) * PW]) for c in range(16)]
        p.mm(bankH.h[0:8, 0:PW], pairs, r=[rwsb, xb], w=[bankH])
        p.ts("vector", lg.h[:], bankH.h[0:8, 0:PW], C["rt_bc"].h[0:8, 0:1], C["lcost"].h[0:8, 0:1],
             ALU.add, ALU.subtract, r=[bankH, C["rt_bc"], C["lcost"]], w=[lg])
        p.copy("gpsimd", lgbf.h[:], lg.h[:], r=[lg], w=[lgbf])
        for tt_ in range(4):
            p.transpose(bankT.h[:, 0:8], lgbf.h[:, tt_ * 128:(tt_ + 1) * 128], C["ident"].h[0:8, 0:8],
                        r=[lgbf, C["ident"]], w=[bankT])
            p.copy("scalar", lt.h[:], bankT.h[:, 0:8], r=[bankT], w=[lt])
            nm = cols.h[:, 0:1]
            p.op("vector", lambda E, a=nm: E.reduce_max(a, lt.h[:], axis=AXX, negate=True),
                 r=[lt], w=[cols])
            ex = pr8.h[:, 0:8]
            p.act(ex, lt.h[:], AF.Exp, bias=nm, r=[lt, cols], w=[pr8])
            sm = cols.h[:, 1:2]
            p.op("vector", lambda E, a=sm, b=ex: E.reduce_sum(a, b, axis=AXX), r=[pr8], w=[cols])
            rc = cols.h[:, 2:3]
            p.op("vector", lambda E, a=rc, b=sm: E.reciprocal(a, b), r=[cols], w=[cols])
            probs = pr8.h[:, 0:8]
            p.ts("vector", probs, probs, rc, None, ALU.mult, None, r=[cols, pr8], w=[pr8])
            m1 = cols.h[:, 3:4]
            p.op("vector", lambda E, a=m1, b=probs: E.reduce_max(a, b, axis=AXX), r=[pr8], w=[cols])
            mask1 = pr8.h[:, 8:16]
            p.ts("vector", mask1, probs, m1, None, ALU.is_equal, None, r=[pr8, cols], w=[pr8])
            p2v = pr8.h[:, 16:24]
            p.stt("vector", p2v, mask1, -1e9, probs, ALU.mult, ALU.add, r=[pr8], w=[pr8])
            m2 = cols.h[:, 4:5]
            p.op("vector", lambda E, a=m2, b=p2v: E.reduce_max(a, b, axis=AXX), r=[pr8], w=[cols])
            mask2 = pr8.h[:, 16:24]
            p.ts("vector", mask2, p2v, m2, None, ALU.is_equal, None, r=[pr8, cols], w=[pr8])
            e1c = cols.h[:, 5:6]
            e2c = cols.h[:, 6:7]
            p.act(e1c, m1, AF.Exp, r=[cols], w=[cols])
            p.act(e2c, m2, AF.Exp, r=[cols], w=[cols])
            dn = cols.h[:, 7:8]
            p.tt("vector", dn, e1c, e2c, ALU.add, r=[cols], w=[cols])
            rd = cols.h[:, 8:9]
            p.op("vector", lambda E, a=rd, b=dn: E.reciprocal(a, b), r=[cols], w=[cols])
            g1 = cols.h[:, 9:10]
            g2 = cols.h[:, 10:11]
            p.tt("vector", g1, e1c, rd, ALU.mult, r=[cols], w=[cols])
            p.tt("vector", g2, e2c, rd, ALU.mult, r=[cols], w=[cols])
            wex = pr8.h[:, 8:16]
            p.ts("vector", wex, mask1, g1, None, ALU.mult, None, r=[pr8, cols], w=[pr8])
            p.stt("vector", wex, mask2, g2, wex, ALU.mult, ALU.add, r=[pr8, cols], w=[pr8])
            p.copy("gpsimd", gb1.h[:, 0:8], wex, r=[pr8], w=[gb1])
            p.transpose(bankT.h[0:8, 8:136], gb1.h[:, 0:8], C["ident"].h[:], r=[gb1, C["ident"]], w=[bankT])
            p.copy("scalar", wexpT.h[0:8, tt_ * 128:(tt_ + 1) * 128], bankT.h[0:8, 8:136],
                   r=[bankT], w=[wexpT])

        if panel == 0:
            p.dbg("gh", gh.h[:], gh, (128, 8 * PW))
            p.dbg("lg", lg.h[:], lg, (8, PW))
            p.dbg("wexpT", wexpT.h[0:8, :], wexpT, (8, PW))
        for e in range(8):
            ne = HID[e] // 128
            p.mm(bankR.h[:, 0:PW], [(C["sel8"].h[0:8, e * 128:(e + 1) * 128], wexpT.h[0:8, 0:PW])],
                 r=[C["sel8"], wexpT], w=[bankR])
            p.copy("scalar", w_bc.h[:], bankR.h[:, 0:PW], r=[bankR], w=[w_bc])
            if panel == 0 and e == 0:
                p.dbg("wbc0", w_bc.h[:], w_bc, (128, PW))
            if panel == 0:
                p.act(gb1.h[:, 0:ne], C[f"e{e}_b1c"].h[:, 0:ne], AF.Gelu,
                      r=[C[f"e{e}_b1c"]], w=[gb1])
                p.dma("gpsimd", b2row.h[0:1, :], W[f"e{e}_b2r"].h.ap()[0:1, :],
                      r=[W[f"e{e}_b2r"]], w=[b2row])
            for jb in range(ne // 4):
                wb1 = w1buf[jb % 2]
                p.dma("gpsimd", wb1.h[:],
                      W[f"e{e}_w1"].h.ap()[jb * 512:(jb + 1) * 512, :].rearrange(
                          "(t p) k -> p t k", p=128),
                      r=[W[f"e{e}_w1"]], w=[wb1])
                for jj in range(4):
                    j = jb * 4 + jj
                    w1s = wb1.h[:, jj * 1024:(jj + 1) * 1024]
                    pairs = [(w1s[:, lc * 128:(lc + 1) * 128], gh.h[:, lc * PW:(lc + 1) * PW])
                             for lc in range(8)]
                    p.mm(bankMID.h[:, 0:PW], pairs, r=[wb1, gh], w=[bankMID])
                    p.act(gmid.h[:, j * PW:(j + 1) * PW], bankMID.h[:, 0:PW], AF.Gelu,
                          bias=C[f"e{e}_b1c"].h[:, j:j + 1], r=[bankMID, C[f"e{e}_b1c"]], w=[gmid])
            if panel == 0 and e == 0:
                p.dbg("gmid0", gmid.h[:, 0:8 * PW], gmid, (128, 8 * PW))
            for lcb in range(4):
                wb2 = w2buf[lcb % 2]
                p.dma("gpsimd", wb2.h[:, 0:2 * ne * 128],
                      W[f"e{e}_w2"].h.ap()[lcb * 256:(lcb + 1) * 256, :].rearrange(
                          "(t p) k -> p t k", p=128),
                      r=[W[f"e{e}_w2"]], w=[wb2])
                for ll in range(2):
                    lc = lcb * 2 + ll
                    w2s = wb2.h[:, ll * ne * 128:(ll + 1) * ne * 128]
                    if panel == 0:
                        prs = [(gb1.h[:, i:i + 1], w2s[:, i * 128:(i + 1) * 128]) for i in range(ne)]
                        p.mm(bankR.h[0:1, 0:128], prs, r=[gb1, wb2], w=[bankR])
                        p.ts("vector", crows.h[0:1, e * 1024 + lc * 128:e * 1024 + (lc + 1) * 128],
                             bankR.h[0:1, 0:128], -1.0, None, ALU.mult, None,
                             r=[bankR], w=[crows])
                        p.tt("vector", ctot.h[0:1, lc * 128:(lc + 1) * 128],
                             ctot.h[0:1, lc * 128:(lc + 1) * 128], bankR.h[0:1, 0:128], ALU.add,
                             r=[bankR, ctot], w=[ctot])
                        p.tt("vector", ctot.h[0:1, lc * 128:(lc + 1) * 128],
                             ctot.h[0:1, lc * 128:(lc + 1) * 128], b2row.h[0:1, lc * 128:(lc + 1) * 128],
                             ALU.add, r=[b2row, ctot], w=[ctot])
                    pairs = [(w2s[:, i * 128:(i + 1) * 128], gmid.h[:, i * PW:(i + 1) * PW])
                             for i in range(ne)]
                    pairs.append((crows.h[0:1, e * 1024 + lc * 128:e * 1024 + (lc + 1) * 128], onespw.h[0:1, :]))
                    p.mm(bankF.h[:, 0:PW], pairs, r=[wb2, gmid, crows, onespw], w=[bankF])
                    p.tt("vector", tmpf.h[:], bankF.h[:, 0:PW], w_bc.h[:], ALU.mult,
                         r=[bankF, w_bc], w=[tmpf])
                    asl = acc.h[:, lc * PW:(lc + 1) * PW]
                    if e == 0:
                        p.copy("vector", asl, tmpf.h[:], r=[tmpf], w=[acc])
                    else:
                        p.tt("vector", asl, asl, tmpf.h[:], ALU.add, r=[tmpf, acc], w=[acc])
        if panel == 0:
            p.dbg("acc0", acc.h[:], acc, (128, 8 * PW))
            p.dbg("crows", crows.h[0:1, :], crows, (1, 8 * 1024))
            p.copy("gpsimd", ctbf.h[:], ctot.h[:], r=[ctot], w=[ctbf])
            for lc in range(8):
                p.mm(bankR.h[:, lc:lc + 1],
                     [(ctbf.h[0:1, lc * 128:(lc + 1) * 128], onespw.h[0:1, 0:1])],
                     r=[ctbf, onespw], w=[bankR])
            p.copy("vector", ctcols.h[:], bankR.h[:, 0:8], r=[bankR], w=[ctcols])

        for scb in range(2):
            wb1 = w1buf[scb % 2]
            p.dma("gpsimd", wb1.h[:],
                  W["sh_w1"].h.ap()[scb * 512:(scb + 1) * 512, :].rearrange(
                      "(t p) k -> p t k", p=128),
                  r=[W["sh_w1"]], w=[wb1])
            for jj in range(4):
                sc = scb * 4 + jj
                w1s = wb1.h[:, jj * 1024:(jj + 1) * 1024]
                pairs = [(w1s[:, lc * 128:(lc + 1) * 128], gh.h[:, lc * PW:(lc + 1) * PW])
                         for lc in range(8)]
                p.mm(bankMID.h[:, 0:PW], pairs, r=[wb1, gh], w=[bankMID])
                p.act(gmid.h[:, sc * PW:(sc + 1) * PW], bankMID.h[:, 0:PW], AF.Gelu,
                      bias=C["sh_b1c"].h[:, sc:sc + 1], r=[bankMID, C["sh_b1c"]], w=[gmid])
        for lcb in range(2):
            wb1 = w1buf[lcb % 2]
            p.dma("gpsimd", wb1.h[:],
                  W["sh_w2"].h.ap()[lcb * 512:(lcb + 1) * 512, :].rearrange(
                      "(t p) k -> p t k", p=128),
                  r=[W["sh_w2"]], w=[wb1])
            for jj in range(4):
                lc = lcb * 4 + jj
                w1s = wb1.h[:, jj * 1024:(jj + 1) * 1024]
                pairs = [(w1s[:, sc * 128:(sc + 1) * 128], gmid.h[:, sc * PW:(sc + 1) * PW])
                         for sc in range(8)]
                p.mm(bankF.h[:, 0:PW], pairs, r=[wb1, gmid], w=[bankF])
                p.ts("vector", tmpf.h[:], bankF.h[:, 0:PW], C["sh_b2c"].h[:, lc:lc + 1], 0.1,
                     ALU.add, ALU.mult, r=[bankF, C["sh_b2c"]], w=[tmpf])
                asl = acc.h[:, lc * PW:(lc + 1) * PW]
                p.tt("vector", asl, asl, tmpf.h[:], ALU.add, r=[tmpf, acc], w=[acc])

        for lc in range(8):
            p.ts("vector", accbf.h[:, (8 + lc) * PW:(9 + lc) * PW], acc.h[:, lc * PW:(lc + 1) * PW],
                 ctcols.h[:, lc:lc + 1], None, ALU.add, None, r=[acc, ctcols], w=[accbf])

        for ocb in range(8):
            wup = w1buf[ocb % 2]
            p.dma("gpsimd", wup.h[:, 0:2048],
                  W["up_w"].h.ap()[ocb * 256:(ocb + 1) * 256, :].rearrange(
                      "(t p) k -> p t k", p=128),
                  r=[W["up_w"]], w=[wup])
            for sub in range(2):
                oc = ocb * 2 + sub
                wco = wdbuf[oc % 2]
                p.dma("gpsimd", wco.h[:],
                      W["co_w"].h.ap()[oc * 128:(oc + 1) * 128, :],
                      r=[W["co_w"]], w=[wco])
                wups = wup.h[:, sub * 1024:(sub + 1) * 1024]
                wcos = wco.h[:]
                pairs = [(wups[:, lc * 128:(lc + 1) * 128], accbf.h[:, (8 + lc) * PW:(9 + lc) * PW])
                         for lc in range(8)]
                pairs += [(wcos[:, c * 128:(c + 1) * 128], gxn.h[:, c * PW:(c + 1) * PW])
                          for c in range(16)]
                p.mm(bankH.h[:, 0:PW], pairs, r=[wup, wco, accbf, gxn], w=[bankH])
                p.ts("vector", tmpf.h[:], bankH.h[:, 0:PW], C["up_bc"].h[:, oc:oc + 1],
                     C["co_bc"].h[:, oc:oc + 1], ALU.add, ALU.add, r=[bankH, C["up_bc"], C["co_bc"]],
                     w=[tmpf])
                p.copy("gpsimd", ysb.h[:, oc * PW:(oc + 1) * PW], tmpf.h[:], r=[tmpf], w=[ysb])

        emit_post_ln_residual(p, C, ln, ysb, xsq, x_out, x_out.h.ap(), panel,
                              a_bc, b_bc, bankS, bankQ2, bankBC, t_rot)

    p.fence()
    st.close()


def asc(a):
    return np.ascontiguousarray(a)


def u8_view(int32_mat):
    return asc(asc(int32_mat).view(np.uint8)[:, 0::4])


def tileKN(wT, tile_n=128):
    K, N = wT.shape
    t = wT.reshape(K // 128, 128, N // tile_n, tile_n).transpose(2, 1, 0, 3)
    return asc(t.reshape(N // tile_n * 128, (K // 128) * tile_n))


def host_prep(inp):
    w = {}
    bf = lambda a: a.astype(bfloat16)
    w["wqkvuT"] = tileKN(bf(inp["L0_attn_W_qkvu"].T))
    w["woutT"] = tileKN(bf(inp["L0_attn_W_out"].T))
    w["slope_row"] = asc(inp["L0_attn_slope_rate"].reshape(1, 16))
    w["gateT8"] = tileKN(u8_view(asc(inp["L0_ffn_gate_w"].T)))
    w["upT8"] = tileKN(u8_view(asc(inp["L0_ffn_up_w"].T)))
    w["downT8"] = tileKN(u8_view(asc(inp["L0_ffn_down_w"].T)))
    w["gsc"] = asc(inp["L0_ffn_gate_scale"].reshape(64, 128).T)
    w["usc"] = asc(inp["L0_ffn_up_scale"].reshape(64, 128).T)
    w["dsc"] = asc(inp["L0_ffn_down_scale"].reshape(16, 128).T)
    for nm in ("Wq", "Wk"):
        w[f"w{nm[1:].lower()}T"] = tileKN(bf(inp[f"L1_attn_{nm}"].T))
    for nm in ("Wv", "Wg"):
        w[f"w{nm[1:].lower()}T"] = tileKN(bf(inp[f"L1_attn_{nm}"].T), tile_n=512)
    w["woT"] = tileKN(bf(inp["L1_attn_Wo"].T))
    w["ws1T"] = asc(inp["L1_attn_Ws1"].T)
    w["ws2T"] = asc(inp["L1_attn_Ws2"].T)
    w["bs2c"] = asc(inp["L1_attn_bs2"].reshape(8, 128).T)
    w["gnw16"] = asc(np.broadcast_to(inp["L1_attn_g_norm_weight"], (16, 128)).reshape(1, 2048))
    w["dn_w"] = tileKN(bf(inp["L1_ffn_down_w"]))
    w["dn_bc"] = asc(inp["L1_ffn_down_b"].reshape(8, 128).T)
    w["up_w"] = tileKN(bf(inp["L1_ffn_up_w"]))
    w["up_bc"] = asc(inp["L1_ffn_up_b"].reshape(16, 128).T)
    w["rt_w"] = asc(inp["L1_ffn_router_w"])
    w["rt_bc"] = asc(inp["L1_ffn_router_b"].reshape(8, 1))
    w["sh_w1"] = tileKN(bf(inp["L1_ffn_shared_w1"]))
    w["sh_w2"] = tileKN(bf(inp["L1_ffn_shared_w2"]))
    w["sh_b1c"] = asc(inp["L1_ffn_shared_b1"].reshape(8, 128).T)
    w["sh_b2c"] = asc(inp["L1_ffn_shared_b2"].reshape(8, 128).T)
    w["co_w"] = tileKN(bf(inp["L1_ffn_core_w"]))
    w["co_bc"] = asc(inp["L1_ffn_core_b"].reshape(16, 128).T)
    HID = [1024, 2048, 3072, 4096, 1024, 2048, 3072, 4096]
    for e, hd in enumerate(HID):
        w[f"e{e}_w1"] = tileKN(bf(inp[f"L1_ffn_e{e}_w1"]))
        w[f"e{e}_w2"] = tileKN(bf(inp[f"L1_ffn_e{e}_w2"]))
        w[f"e{e}_b1c"] = asc(inp[f"L1_ffn_e{e}_b1"].reshape(hd // 128, 128).T)
        w[f"e{e}_b2r"] = asc(inp[f"L1_ffn_e{e}_b2"].reshape(1, 1024))
    w["ident"] = np.eye(128, dtype=bfloat16)
    w["arr_row"] = np.arange(1, 257, dtype=np.float32).reshape(1, 256)
    kc = np.zeros((128, 2), np.float32)
    for pp in range(128):
        for j in range(2):
            kc[pp, j] = 256.0 - (j * 128 + pp + 1)
    w["karr_cols"] = kc
    dm = np.zeros((128, 512), np.float32)
    for a in range(128):
        for b in range(256):
            for t in range(2):
                v = b - a - 128 * t
                dm[a, t * 256 + b] = v if v >= 0 else 1e9
    w["Dm"] = dm
    pos = np.arange(T, dtype=np.float64)
    inv_freq = 1.0 / 10000.0 ** (np.arange(0, DH, 2) / DH)
    ang = pos[None, :] * inv_freq[:, None]
    scale = DH ** -0.5
    w["cosq"] = (np.cos(ang) * scale).astype(bfloat16)
    w["sinq"] = (np.sin(ang) * scale).astype(bfloat16)
    w["cosk"] = np.cos(ang).astype(bfloat16)
    w["sink"] = np.sin(ang).astype(bfloat16)
    cm = np.zeros((128, 4 * 512), np.float32)
    for off in range(4):
        for r in range(128):
            for c in range(512):
                cm[r, off * 512 + c] = 1.0 if c >= r + off * 128 else 0.0
    w["cmask"] = cm.astype(bfloat16)
    sel = np.zeros((8, 1024), np.float32)
    for e in range(8):
        sel[e, e * 128:(e + 1) * 128] = 1.0
    w["sel8"] = sel.astype(bfloat16)
    cost = np.array([2 * 1024 * hd for hd in HID], np.float32) * 1e-07
    w["lcost"] = cost.reshape(8, 1)
    return w


W_DTYPES = {
    "gateT8": U8, "upT8": U8, "downT8": U8,
    "wqkvuT": BF16, "woutT": BF16,
    "wqT": BF16, "wkT": BF16, "wvT": BF16, "wgT": BF16, "woT": BF16,
    "dn_w": BF16, "up_w": BF16, "co_w": BF16, "sh_w1": BF16, "sh_w2": BF16,
    "ident": BF16, "cosq": BF16, "sinq": BF16, "cosk": BF16, "sink": BF16,
    "cmask": BF16, "sel8": BF16,
}
W_DTYPES.update({f"e{e}_{nm}": BF16 for e in range(8) for nm in ("w1", "w2")})


def declare_weights(p, names, host_w):
    W = {}
    for n in names:
        arr = host_w[n]
        dt = W_DTYPES.get(n, F32)
        W[n] = p.dram(n, arr.shape, dt, kind="ExternalInput")
    return W


def make_consts(p, C_stack):
    C = {}
    C["ones_col"] = p.sb("c_onescol", [128, 1], BF16, C_stack)
    C["ones_row"] = p.sb("c_onesrow", [1, 128], BF16, C_stack)
    C["ones_pw"] = p.sb("c_onespw", [1, PW], BF16, C_stack)
    C["neg128_row"] = p.sb("c_n128row", [1, 128], BF16, C_stack)
    p.memset("vector", C["ones_col"].h[:], 1.0, w=[C["ones_col"]])
    p.memset("vector", C["ones_row"].h[:], 1.0, w=[C["ones_row"]])
    p.memset("vector", C["ones_pw"].h[:], 1.0, w=[C["ones_pw"]])
    p.memset("vector", C["neg128_row"].h[:], -128.0, w=[C["neg128_row"]])
    return C


def load_const_tiles(p, C, W, C_stack, need):
    shapes = {
        "ident": (128, 128), "Dm": (128, 512), "arr_row": (1, 256), "karr_cols": (128, 2),
        "cosq": (64, 2048), "sinq": (64, 2048), "cosk": (64, 2048), "sink": (64, 2048),
        "cmask": (128, 2048), "sel8": (8, 1024), "gnw16": (1, 2048), "lcost": (8, 1),
        "gsc": (128, 64), "usc": (128, 64), "dsc": (128, 16), "bs2c": (128, 8),
        "dn_bc": (128, 8), "up_bc": (128, 16), "rt_bc": (8, 1),
        "sh_b1c": (128, 8), "sh_b2c": (128, 8), "co_bc": (128, 16),
        "e0_b1c": (128, 8), "e1_b1c": (128, 16), "e2_b1c": (128, 24), "e3_b1c": (128, 32),
        "e4_b1c": (128, 8), "e5_b1c": (128, 16), "e6_b1c": (128, 24), "e7_b1c": (128, 32),
    }
    for n in need:
        if n not in W:
            continue
        sh = shapes[n]
        dt = W_DTYPES.get(n, F32)
        C[n] = p.sb("ct_" + n, list(sh), dt, C_stack)
        p.dma("sync", C[n].h[:], W[n].h.ap()[:, :], r=[W[n]], w=[C[n]])



PHASE_WEIGHTS = {
    "A": ["wqkvuT", "woutT", "slope_row", "ident", "Dm", "arr_row", "karr_cols"],
    "B": ["gateT8", "upT8", "downT8", "gsc", "usc", "dsc"],
    "C1": ["wqT", "wkT", "wvT", "wgT", "ws1T", "ws2T", "bs2c", "ident",
           "cosq", "sinq", "cosk", "sink"],
    "C2": ["woT", "gnw16", "ident", "cmask"],
    "D": ["dn_w", "dn_bc", "up_w", "up_bc", "rt_w", "rt_bc", "lcost", "sh_w1", "sh_w2",
          "sh_b1c", "sh_b2c", "co_w", "co_bc", "sel8", "ident"]
    + [f"e{e}_{nm}" for e in range(8) for nm in ("w1", "w2", "b1c", "b2r")],
}

PHASE_CONSTS = {
    "A": ["ident", "Dm", "arr_row", "karr_cols"],
    "B": ["gsc", "usc", "dsc"],
    "C1": ["ident", "cosq", "sinq", "cosk", "sink", "bs2c"],
    "C2": ["ident", "cmask", "gnw16"],
    "D": ["ident", "dn_bc", "up_bc", "rt_bc", "lcost", "sh_b1c", "sh_b2c", "co_bc", "sel8"]
    + [f"e{e}_b1c" for e in range(8)],
}


def build_program(phases, host_w, dbg_scratch=False, dump_mids=False):
    nc = bass.Bass("TRN2", target_bir_lowering=False)
    stack = ExitStack()
    p = Prog(nc, stack)

    x0 = p.dram("x0T", (D, T), F32, kind="ExternalInput")
    out = p.dram("outT", (D, T), F32, kind="ExternalOutput")
    names = []
    for ph in phases:
        names += PHASE_WEIGHTS[ph]
    W = declare_weights(p, sorted(set(names)), host_w)

    C = make_consts(p, stack)
    load_const_tiles(p, C, W, stack, ["ident"])

    S = {}
    if "C1" in phases or "C2" in phases:
        sk = "ExternalOutput" if dbg_scratch else "Internal"
        S["qfD"] = p.dram("qfD", (2048, 2048), BF16, kind=sk)
        S["kfD"] = p.dram("kfD", (2048, 2048), BF16, kind=sk)
        S["vtD"] = p.dram("vtD", (2048, 2048), BF16, kind=sk)
        S["gtD"] = p.dram("gtD", (2048, 2048), BF16, kind=sk)
        S["esfD"] = p.dram("esfD", (1024, 2048), BF16, kind=sk)
        S["estD"] = p.dram("estD", (2048, 1024), BF16, kind=sk)
        S["enfD"] = p.dram("enfD", (1024, 2048), BF16, kind=sk)
        S["entD"] = p.dram("entD", (2048, 1024), BF16, kind=sk)

    cur = x0
    residual_phases = [ph for ph in phases if ph != "C1"]
    for i, ph in enumerate(phases):
        if ph == "C1":
            phase_l1_attn_proj(p, C, W, cur, S)
            continue
        last = (residual_phases[-1] == ph)
        mk = "ExternalOutput" if dump_mids else "Internal"
        dst = out if last else p.dram(f"xmid{i}", (D, T), F32, kind=mk)
        if ph == "A":
            phase_l0_attn(p, C, W, cur, dst)
        elif ph == "B":
            phase_l0_ffn(p, C, W, cur, dst)
        elif ph == "C2":
            phase_l1_attn_core(p, C, W, cur, dst, S)
        elif ph == "D":
            phase_l1_ffn(p, C, W, cur, dst)
        else:
            raise NotImplementedError(ph)
        cur = dst

    p.fence()
    return nc, stack


PHASES = ["A", "B", "C1", "C2", "D"]


def kernel(**inputs):
    host_w = host_prep(inputs)
    nc, stack = build_program(PHASES, host_w)
    names = sorted({n for ph in PHASES for n in PHASE_WEIGHTS[ph]})
    in_maps = []
    for core in range(NCORES):
        m = {"x0T": asc(inputs["L0_attn_x"][core].T)}
        for n in names:
            m[n] = host_w[n]
        in_maps.append(m)
    res = run_bass_kernel_spmd(nc, in_maps, core_ids=list(range(NCORES)))
    stack.close()
    outs = [r["outT"] for r in res.results]
    return np.stack([o.T for o in outs], axis=0).astype(np.float32)
